# version 03
# baseline (speedup 1.0000x reference)
import contextlib
import numpy as np
import ml_dtypes
import concourse.bass as bass
import concourse.mybir as mybir
from concourse.bass_utils import run_bass_kernel_spmd

F32 = mybir.dt.float32
BF16 = mybir.dt.bfloat16
AF = mybir.ActivationFunctionType
ALU = mybir.AluOpType
AX = mybir.AxisListType

CONST_VALS = [1e-5, 1e-6, 1.0, -2.772588722239781]

T = 4096
D = 1024
NC_ = 8


ENGS = ("tensor", "vector", "scalar", "gpsimd", "sync")


class Buf:
    __slots__ = ("writers", "readers", "name", "fam", "excl")

    def __init__(self, name="", excl=False):
        self.writers = []
        self.readers = []
        self.name = name
        self.fam = None
        self.excl = excl


class Op:
    __slots__ = ("eng", "build", "deps", "dma", "needs_inc", "inc_cum", "gidx",
                 "waits", "fam")

    def __init__(self):
        self.deps = []
        self.needs_inc = False
        self.waits = None
        self.fam = None


class Pr:

    def __init__(self, nc):
        self.nc = nc
        self.ops = {e: [] for e in ENGS}
        self.gidx = 0
        self.pending_fence = {e: None for e in ENGS}
        self.nfam = 0
        self.fence_gidxs = []

    def op(self, eng, build, reads=(), writes=(), dma=False):
        o = Op()
        o.eng = eng
        o.build = build
        o.dma = dma
        if dma:
            o.needs_inc = True
            assert len(writes) >= 1, "dma op needs a destination buffer"
            b0 = writes[0]
            if b0.fam is None:
                b0.fam = self.nfam
                self.nfam += 1
            o.fam = b0.fam
        def keep(d):
            return d.dma or d.eng != eng or eng != "tensor"
        if self.pending_fence[eng]:
            for d in self.pending_fence[eng]:
                if d is not o and keep(d):
                    o.deps.append(d)
            self.pending_fence[eng] = None
        for b in reads:
            for w in b.writers:
                if keep(w):
                    o.deps.append(w)
            if b.excl:
                for r in b.readers:
                    if keep(r):
                        o.deps.append(r)
            b.readers.append(o)
        for b in writes:
            for w in b.writers:
                if keep(w):
                    o.deps.append(w)
            for r in b.readers:
                if r is not o and keep(r):
                    o.deps.append(r)
            b.readers.clear()
            b.writers.append(o)
        for d in o.deps:
            d.needs_inc = True
        o.gidx = self.gidx
        self.gidx += 1
        self.ops[eng].append(o)
        return o

    def fence(self):
        lasts = [self.ops[e][-1] for e in ENGS if self.ops[e]]
        for e in ENGS:
            self.pending_fence[e] = lasts
        self.fence_gidxs.append(self.gidx)

    @staticmethod
    def _skey(o):
        return ("f", o.fam) if o.dma else ("e", o.eng)

    def finalize(self, st):
        ecnt = {e: 0 for e in ENGS}
        fcnt = {}
        for o in sorted((o for e in ENGS for o in self.ops[e]), key=lambda o: o.gidx):
            if o.dma:
                fcnt[o.fam] = fcnt.get(o.fam, 0) + 16
                o.inc_cum = fcnt[o.fam]
            elif o.needs_inc:
                ecnt[o.eng] += 1
                o.inc_cum = ecnt[o.eng]
            else:
                o.inc_cum = ecnt[o.eng]
        allops = sorted((o for e in ENGS for o in self.ops[e]), key=lambda o: o.gidx)
        cur = {e: {} for e in ENGS}
        opclock = {}
        for o in allops:
            c = cur[o.eng]
            wd = {}
            for d in o.deps:
                skey = self._skey(d)
                v = d.inc_cum
                if c.get(skey, 0) < v:
                    wd[skey] = max(wd.get(skey, 0), v)
                    dc = opclock.get(d.gidx)
                    if dc:
                        for s2, v2 in dc.items():
                            if v2 > c.get(s2, 0):
                                c[s2] = v2
                    c[skey] = max(c.get(skey, 0), v)
            o.waits = list(wd.items())
            if o.needs_inc:
                mykey = self._skey(o)
                snap = dict(c)
                snap[mykey] = max(snap.get(mykey, 0), o.inc_cum)
                opclock[o.gidx] = snap
        import bisect
        fam_first = {}
        fam_last = {}
        for o in allops:
            if o.dma:
                fam_first.setdefault(o.fam, o.gidx)
                fam_last[o.fam] = max(fam_last.get(o.fam, 0), o.gidx)
            for d in o.deps:
                if d.dma:
                    fam_last[d.fam] = max(fam_last.get(d.fam, 0), o.gidx)
        sems = {}
        for e in ENGS:
            sems[("e", e)] = st.enter_context(self.nc.semaphore(f"s_{e}"))
        phys = []
        fam_phys = {}
        fam_base = {}
        fams_sorted = sorted(fam_first, key=lambda f: fam_first[f])
        fam_total = {}
        for o in allops:
            if o.dma:
                fam_total[o.fam] = o.inc_cum
        for f in fams_sorted:
            fg = fam_first[f]
            i = bisect.bisect_right(self.fence_gidxs, fg) - 1
            fence_g = self.fence_gidxs[i] if i >= 0 else -1
            slot = None
            for ph in phys:
                if ph[1] < fence_g:
                    slot = ph
                    break
            if slot is None:
                slot = [st.enter_context(self.nc.semaphore(f"sf{len(phys)}")), -1, 0]
                phys.append(slot)
            fam_phys[f] = slot
            fam_base[f] = slot[2]
            slot[1] = fam_last[f]
            slot[2] += fam_total[f]
        self.fam_sem = {f: fam_phys[f][0] for f in fam_phys}
        self.fam_base = fam_base
        self.nphys = len(phys)
        self.sems = sems
        nwaits = sum(len(o.waits) for o in allops)
        nincs = sum(1 for o in allops if o.needs_inc)
        self.stats = (len(allops), nwaits, nincs, self.nfam, self.nphys)

    def run_block(self):
        sems = self.sems

        def sem_of(skey):
            return sems[skey] if skey[0] == "e" else self.fam_sem[skey[1]]

        def base_of(skey):
            return 0 if skey[0] == "e" else self.fam_base[skey[1]]

        with self.nc.Block() as block:
            def mk(e):
                def runner(eng):
                    for o in self.ops[e]:
                        for (skey, v) in o.waits:
                            eng.wait_ge(sem_of(skey), base_of(skey) + v)
                        ins = o.build(eng)
                        if o.needs_inc:
                            ins.then_inc(sem_of(self._skey(o)) if o.dma else
                                         sems[self._skey(o)], 16 if o.dma else 1)
                return runner
            block.tensor(mk("tensor"))
            block.vector(mk("vector"))
            block.scalar(mk("scalar"))
            block.gpsimd(mk("gpsimd"))
            block.sync(mk("sync"))



def host_constants():
    c = {}
    c["c_id_bf"] = np.eye(128, dtype=ml_dtypes.bfloat16)
    c["c_id_f"] = np.eye(128, dtype=np.float32)
    c["c_ones_f"] = np.ones((1, 128), dtype=np.float32)
    c["c_onescol_bf"] = np.ones((128, 1), dtype=ml_dtypes.bfloat16)
    p = np.arange(128)[:, None]
    f = np.arange(512)[None, :]
    mk = np.zeros((128, 4 * 512), dtype=np.float32)
    for m in range(4):
        mk[:, 512 * m:512 * (m + 1)] = np.where(p + 128 * m > f, -1e30, 0.0)
    c["c_maskc"] = mk
    pp = np.arange(128)[:, None]
    ff = np.arange(128)[None, :]
    c["c_mA"] = np.where(ff >= pp, 1e30, 0.0).astype(np.float32)
    c["c_mW"] = np.where(pp > ff, -1e30, 0.0).astype(np.float32)
    c["c_mWs"] = np.where(pp >= ff, -1e30, 0.0).astype(np.float32)
    c["c_sel127"] = np.zeros((128, 128), dtype=np.float32)
    c["c_sel127"][127, :] = 1.0
    hs = np.zeros((48, 16), dtype=np.float32)
    for h in range(16):
        hs[3 * h:3 * h + 3, h] = 1.0
    c["c_hsel"] = hs
    c["c_fvals"] = np.tile(np.asarray([CONST_VALS], np.float32), (128, 1))
    return c



class Ctx:
    pass


_SB_N = [0]


def _sb(ctx, stack, name, shape, dt):
    _SB_N[0] += 1
    t = stack.enter_context(ctx.nc.sbuf_tensor(f"{name}_{_SB_N[0]}", shape, dt))
    return t


def declare_io(nc, input_specs, dumps=()):
    io = {}
    for name, shape, dt in input_specs:
        io[name] = nc.dram_tensor(name, list(shape), dt, kind="ExternalInput").ap()
    return io


def build_program(dumps=(), stages=("l0", "moe", "l1a", "l1f"), dbg=(), feeds=()):
    nc = bass.Bass("TRN2", target_bir_lowering=False)
    ctx = Ctx()
    ctx.nc = nc
    ctx.pr = Pr(nc)
    ctx.stack = contextlib.ExitStack()
    ctx.dbg = set(dbg)

    cns = host_constants()
    specs = [("x", (T, D), F32)]
    for k, v in cns.items():
        specs.append((k, v.shape, BF16 if v.dtype == ml_dtypes.bfloat16 else F32))
    wspec = [
        ("L0_attn_wq", (D, 48)), ("L0_attn_wk", (D, 48)), ("L0_attn_wv", (D, D)),
        ("L0_attn_wo", (D, D)), ("L0_attn_head_dirs", (16, 3)), ("L0_attn_scale", (1,)),
        ("L0_ffn_Wg", (D, 8)), ("L0_ffn_bg", (8,)),
        ("L0_ffn_W0", (8, D, 4096)), ("L0_ffn_b0", (8, 4096)),
        ("L0_ffn_W1", (8, D, 4096)), ("L0_ffn_b1", (8, 4096)),
        ("L0_ffn_W2", (8, 4096, D)), ("L0_ffn_b2", (8, D)),
        ("L1_attn_ln1_g", (D,)), ("L1_attn_ln1_b", (D,)),
        ("L1_attn_ln2_g", (D,)), ("L1_attn_ln2_b", (D,)),
        ("L1_attn_rms_in_w", (D,)),
        ("L1_attn_Wq", (D, D)), ("L1_attn_Wk", (D, D)), ("L1_attn_Wv", (D, 2048)),
        ("L1_attn_conv_q_w", (D, 4)), ("L1_attn_conv_k_w", (D, 4)),
        ("L1_attn_conv_v_w", (2048, 4)),
        ("L1_attn_Wa", (D, 4)), ("L1_attn_dt_bias", (4,)), ("L1_attn_A_log", (4,)),
        ("L1_attn_Wb", (D, 4)), ("L1_attn_Wg", (D, 2048)),
        ("L1_attn_o_norm_w", (512,)), ("L1_attn_Wo", (2048, D)),
        ("L1_attn_W2", (D, D)), ("L1_attn_b2", (D,)),
        ("L1_ffn_w1_w", (4096, D)), ("L1_ffn_w1_s", (4096,)),
        ("L1_ffn_w2_w", (4096, D)), ("L1_ffn_w2_s", (4096,)),
        ("L1_ffn_h1_w", (4096, 4096)), ("L1_ffn_h1_s", (4096,)),
        ("L1_ffn_h2_w", (4096, 4096)), ("L1_ffn_h2_s", (4096,)),
        ("L1_ffn_w3_w", (D, 4096)), ("L1_ffn_w3_s", (D,)),
    ]
    for n, s in wspec:
        specs.append((n, s, F32))
    class LazyIO(dict):
        def __init__(self, specs):
            super().__init__()
            self.specs = {n: (sh, dt) for n, sh, dt in specs}

        def __missing__(self, key):
            sh, dt = self.specs[key]
            ap = nc.dram_tensor(key, list(sh), dt, kind="ExternalInput").ap()
            self[key] = ap
            return ap

    io = LazyIO(specs)
    ctx.io = io

    def scratch(name, shape, dt):
        kind = "ExternalOutput" if name in dumps else (
            "ExternalInput" if name in feeds else "Internal")
        ap = nc.dram_tensor(name, list(shape), dt, kind=kind).ap()
        ctx.bufs[name] = Buf(name)
        return ap

    ctx.bufs = {}
    ctx.dram = {}
    for name, shape, dt in [
        ("xT32", (D, T), F32),
        ("vTs", (D, T), BF16),
        ("aoTs", (D, T), BF16),
        ("r1_32", (D, T), F32), ("r1_bf", (D, T), BF16),
        ("r2_32", (D, T), F32), ("r2_bf", (D, T), BF16),
        ("hTbf", (D, T), BF16),
        ("oTs", (2048, T), BF16),
        ("x1_32", (D, T), F32),
        ("h2bf", (D, T), BF16),
        ("r3_32", (D, T), F32), ("r3_bf", (D, T), BF16),
        ("h1T", (4096, T), BF16), ("h2T", (4096, T), BF16), ("h3T", (4096, T), BF16),
        ("wbf_w1", (4096, D), BF16), ("wbf_w2", (4096, D), BF16),
        ("wbf_h1", (4096, 4096), BF16), ("wbf_h2", (4096, 4096), BF16),
        ("wbf_w3", (D, 4096), BF16),
        ("r4_32", (D, T), F32), ("r4_bf", (D, T), BF16),
        ("lgts", (T, 8), F32),
        ("qnT_d", (256, T), BF16), ("knT_d", (256, T), BF16),
        ("kn_tok_d", (T, 256), BF16), ("v_tok_d", (T, 512), BF16),
        ("gate_d", (T, 512), BF16),
        ("dbg_q4", (16, T), BF16),
        ("dbg_pr", (128, T), BF16),
        ("dbg_sm", (48, 17), F32),
        ("dbg_moe", (128, 8192), F32),
        ("dbg_ch", (128, 12288), F32),
        ("dbg_cmb", (128, 64), F32),
    ]:
        ctx.dram[name] = scratch(name, shape, dt)
    ctx.out = nc.dram_tensor("out", [T, D], F32, kind="ExternalOutput").ap()
    ctx.bufs["out"] = Buf("out")

    stk = ctx.stack
    ctx.pb = [stk.enter_context(nc.psum_tensor(f"pb{i}", [128, 512], F32)) for i in range(6)]
    ctx.pt = [stk.enter_context(nc.psum_tensor(f"pt{i}", [128, 128], BF16)) for i in range(2)]
    ctx.pbB = [Buf(f"pb{i}", excl=True) for i in range(6)]
    ctx.ptB = [Buf(f"pt{i}", excl=True) for i in range(2)]

    ctx.c = {}
    ctx.cB = Buf("consts")
    pr = ctx.pr
    for nm, shape, dt in [("c_id_bf", [128, 128], BF16), ("c_id_f", [128, 128], F32),
                          ("c_ones_f", [1, 128], F32), ("c_onescol_bf", [128, 1], BF16),
                          ("c_maskc", [128, 2048], F32), ("c_mA", [128, 128], F32),
                          ("c_mW", [128, 128], F32), ("c_mWs", [128, 128], F32),
                          ("c_sel127", [128, 128], F32),
                          ("c_hsel", [48, 16], F32),
                          ("c_fvals", [128, len(CONST_VALS)], F32)]:
        tile = _sb(ctx, stk, nm + "_sb", shape, dt)
        ctx.c[nm] = tile
        pr.op("sync", (lambda e, t=tile, a=io[nm], s=shape: e.dma_start(
            out=t[0:s[0], 0:s[1]], in_=a)), writes=[ctx.cB], dma=True)
    for i, v in enumerate(CONST_VALS):
        nc.const_aps.register(F32, v, ctx.c["c_fvals"][0:128, i:i + 1])

    if "l0" in stages:
        st_l0(ctx)
    if "moe" in stages:
        st_moe(ctx)
    if "l1a" in stages:
        st_l1attn(ctx)
    if "l1f" in stages:
        st_l1ffn(ctx)

    ctx.pr.finalize(ctx.stack)
    ctx.pr.run_block()
    ctx.stack.close()
    return nc, ctx



L0_PARTS = ("init", "proj", "heads", "lna")
INIT_N = 32
HEADS_N = 16


def st_l0(ctx):
    nc, pr, io, c = ctx.nc, ctx.pr, ctx.io, ctx.c
    pb, pbB = ctx.pb, ctx.pbB
    cB = ctx.cB
    with contextlib.ExitStack() as stk:
        xTs = _sb(ctx, stk, "xTs", [128, 8 * T], BF16)
        xTsB = Buf("xTs")
        xin = [_sb(ctx, stk, f"xin{i}", [128, 1024], F32) for i in range(2)]
        xinB = [Buf(f"xin{i}") for i in range(2)]
        xst = [_sb(ctx, stk, f"xst{i}", [128, 1024], F32) for i in range(2)]
        xstB = [Buf(f"xst{i}") for i in range(2)]
        import os
        SUB = os.environ.get("INIT_SUB", "ltvse").lower()
        for i in range(INIT_N):
            bi = i % 2
            if "l" in SUB:
                pr.op("sync", (lambda e, i=i, t=xin[bi]: e.dma_start(
                    out=t[0:128, 0:1024], in_=io["x"][128 * i:128 * (i + 1), :])),
                    reads=[], writes=[xinB[bi]], dma=True)
            for k in range(8):
                pt = pb[k % 2]
                ptB = pbB[k % 2]
                if "t" in SUB:
                    pr.op("tensor", (lambda e, t=xin[bi], k=k, pt=pt: e.transpose(
                        pt[0:128, 0:128], t[0:128, 128 * k:128 * (k + 1)], c["c_id_f"][0:128, 0:128])),
                        reads=[xinB[bi], cB], writes=[ptB])
                if "v" in SUB:
                    pr.op("vector", (lambda e, i=i, k=k, pt=pt: e.tensor_copy(
                        xTs[0:128, T * k + 128 * i: T * k + 128 * (i + 1)], pt[0:128, 0:128])),
                        reads=[ptB], writes=[xTsB])
                if "s" in SUB:
                    pr.op("scalar", (lambda e, i=i, k=k, pt=pt, s=xst[bi]: e.copy(
                        s[0:128, 128 * k:128 * (k + 1)], pt[0:128, 0:128])),
                        reads=[ptB], writes=[xstB[bi]])
            if "e" in SUB:
                pr.op("sync", (lambda e, i=i, s=xst[bi]: e.dma_start(
                    out=ctx.dram["xT32"][0:1024, 128 * i:128 * (i + 1)].rearrange(
                        "(a p) t -> p a t", p=128), in_=s[0:128, 0:1024].rearrange(
                        "p (a t) -> p a t", a=8))),
                    reads=[xstB[bi]], writes=[ctx.bufs["xT32"]], dma=True)

        if "proj" not in L0_PARTS:
            pr.fence()
            return
        wq_sb = _sb(ctx, stk, "wq_sb", [128, 8 * 48], BF16)
        wk_sb = _sb(ctx, stk, "wk_sb", [128, 8 * 48], BF16)
        wB = Buf("l0w")
        for k in range(8):
            pr.op("gpsimd", (lambda e, k=k: e.dma_start(
                out=wq_sb[0:128, 48 * k:48 * (k + 1)], in_=io["L0_attn_wq"][128 * k:128 * (k + 1), :])),
                writes=[wB], dma=True)
            pr.op("gpsimd", (lambda e, k=k: e.dma_start(
                out=wk_sb[0:128, 48 * k:48 * (k + 1)], in_=io["L0_attn_wk"][128 * k:128 * (k + 1), :])),
                writes=[wB], dma=True)
        DTf = _sb(ctx, stk, "DTf", [48, 16], F32)
        DTb = _sb(ctx, stk, "DTb", [48, 16], BF16)
        DTB = Buf("DT")
        scale_sb = _sb(ctx, stk, "scale_sb", [1, 1], F32)
        scol = _sb(ctx, stk, "scol", [48, 1], F32)
        scolB = Buf("scol")
        dirow = _sb(ctx, stk, "dirow", [1, 48], F32)
        dicol = _sb(ctx, stk, "dicol", [48, 1], F32)
        pr.op("sync", lambda e: e.dma_start(
            out=dirow[0:1, 0:48], in_=io["L0_attn_head_dirs"].rearrange("a b -> (a b)")[None, :]),
            writes=[DTB], dma=True)
        pr.op("tensor", lambda e: e.matmul(
            pb[5][0:48, 0:1], dirow[0:1, 0:48], c["c_ones_f"][0:1, 0:1],
            start=True, stop=True),
            reads=[DTB, cB], writes=[pbB[5]])
        pr.op("vector", lambda e: e.tensor_copy(dicol[0:48, 0:1], pb[5][0:48, 0:1]),
              reads=[pbB[5]], writes=[DTB])
        pr.op("vector", lambda e: e.tensor_scalar(
            DTf[0:48, 0:16], c["c_hsel"][0:48, 0:16], dicol[0:48, 0:1], None, ALU.mult),
            reads=[DTB, cB], writes=[DTB])
        pr.op("vector", lambda e: e.tensor_copy(DTb[0:48, 0:16], DTf[0:48, 0:16]),
              reads=[DTB], writes=[DTB])
        pr.op("sync", lambda e: e.dma_start(out=scale_sb[0:1, 0:1],
                                            in_=io["L0_attn_scale"][:, None]),
              writes=[scolB], dma=True)
        pr.op("tensor", lambda e: e.matmul(pb[5][0:48, 0:1], c["c_ones_f"][0:1, 0:48],
                                           scale_sb[0:1, 0:1], start=True, stop=True),
              reads=[scolB, cB], writes=[pbB[5]])
        pr.op("vector", lambda e: e.tensor_copy(scol[0:48, 0:1], pb[5][0:48, 0:1]),
              reads=[pbB[5]], writes=[scolB])

        q3u = _sb(ctx, stk, "q3u", [48, T], BF16)
        q3s = _sb(ctx, stk, "q3s", [48, T], BF16)
        k3u = _sb(ctx, stk, "k3u", [48, T], BF16)
        qdT = _sb(ctx, stk, "qdT", [16, T], BF16)
        kdT = _sb(ctx, stk, "kdT", [16, T], BF16)
        q3uB, q3sB, k3uB, qdTB, kdTB = (Buf(n) for n in ("q3u", "q3s", "k3u", "qdT", "kdT"))
        vstg = [_sb(ctx, stk, f"vstg{i}", [128, 512], BF16) for i in range(2)]
        vstgB = [Buf(f"vstg{i}") for i in range(2)]

        for tc in range(8):
            sl = slice(512 * tc, 512 * (tc + 1))
            for (wsb, uout, uB, sout) in ((wq_sb, q3u, q3uB, q3s), (wk_sb, k3u, k3uB, None)):
                pq = pb[2]
                for k in range(8):
                    pr.op("tensor", (lambda e, wsb=wsb, k=k, tc=tc, pq=pq: e.matmul(
                        pq[0:48, 0:512], wsb[0:128, 48 * k:48 * (k + 1)],
                        xTs[0:128, T * k + 512 * tc:T * k + 512 * (tc + 1)],
                        start=(k == 0), stop=(k == 7))),
                        reads=[wB, xTsB], writes=[pbB[2]])
                pr.op("vector", (lambda e, uout=uout, sl=sl, pq=pq: e.tensor_copy(
                    uout[0:48, sl], pq[0:48, 0:512])), reads=[pbB[2]], writes=[uB])
                if sout is not None:
                    pr.op("vector", (lambda e, sout=sout, sl=sl, pq=pq: e.tensor_scalar(
                        sout[0:48, sl], pq[0:48, 0:512], scol[0:48, 0:1], None, ALU.mult)),
                        reads=[pbB[2], scolB], writes=[q3sB])
            for (uin, uB, dout, dB) in ((q3u, q3uB, qdT, qdTB), (k3u, k3uB, kdT, kdTB)):
                pr.op("tensor", (lambda e, uin=uin, sl=sl: e.matmul(
                    pb[3][0:16, 0:512], DTb[0:48, 0:16], uin[0:48, sl], start=True, stop=True)),
                    reads=[DTB, uB], writes=[pbB[3]])
                pr.op("vector", (lambda e, dout=dout, sl=sl: e.tensor_copy(
                    dout[0:16, sl], pb[3][0:16, 0:512])), reads=[pbB[3]], writes=[dB])
        wv_t = _sb(ctx, stk, "wv_t", [128, 8 * 128], BF16)
        wvB = Buf("wv")
        for m in range(8):
            for k in range(8):
                pr.op("gpsimd", (lambda e, m=m, k=k: e.dma_start(
                    out=wv_t[0:128, 128 * k:128 * (k + 1)],
                    in_=io["L0_attn_wv"][128 * k:128 * (k + 1), 128 * m:128 * (m + 1)])),
                    writes=[wvB], dma=True)
            for tc in range(8):
                bi = tc % 2
                pv = pb[4]
                for k in range(8):
                    pr.op("tensor", (lambda e, k=k, tc=tc, pv=pv: e.matmul(
                        pv[0:128, 0:512], wv_t[0:128, 128 * k:128 * (k + 1)],
                        xTs[0:128, T * k + 512 * tc:T * k + 512 * (tc + 1)],
                        start=(k == 0), stop=(k == 7))),
                        reads=[wvB, xTsB], writes=[pbB[4]])
                pr.op("vector", (lambda e, bi=bi, pv=pv: e.tensor_copy(
                    vstg[bi][0:128, 0:512], pv[0:128, 0:512])),
                    reads=[pbB[4]], writes=[vstgB[bi]])
                pr.op("sync", (lambda e, m=m, tc=tc, bi=bi: e.dma_start(
                    out=ctx.dram["vTs"][128 * m:128 * (m + 1), 512 * tc:512 * (tc + 1)],
                    in_=vstg[bi][0:128, 0:512])),
                    reads=[vstgB[bi]], writes=[ctx.bufs["vTs"]], dma=True)
        if "proj_dump" in ctx.dbg:
            pr.op("sync", lambda e: e.dma_start(out=ctx.dram["dbg_pr"][0:48, :], in_=q3u[0:48, 0:T]),
                  reads=[q3uB], writes=[ctx.bufs["dbg_pr"]], dma=True)
            pr.op("sync", lambda e: e.dma_start(out=ctx.dram["dbg_pr"][48:96, :], in_=q3s[0:48, 0:T]),
                  reads=[q3sB], writes=[ctx.bufs["dbg_pr"]], dma=True)
            pr.op("sync", lambda e: e.dma_start(out=ctx.dram["dbg_pr"][96:112, :], in_=qdT[0:16, 0:T]),
                  reads=[qdTB], writes=[ctx.bufs["dbg_pr"]], dma=True)
            pr.op("sync", lambda e: e.dma_start(out=ctx.dram["dbg_pr"][112:128, :], in_=kdT[0:16, 0:T]),
                  reads=[kdTB], writes=[ctx.bufs["dbg_pr"]], dma=True)
            pr.op("sync", lambda e: e.dma_start(out=ctx.dram["dbg_sm"][0:48, 0:16], in_=DTf[0:48, 0:16]),
                  reads=[DTB], writes=[ctx.bufs["dbg_sm"]], dma=True)
            pr.op("sync", lambda e: e.dma_start(out=ctx.dram["dbg_sm"][0:48, 16:17], in_=scol[0:48, 0:1], allow_slow_non_contiguous=True),
                  reads=[scolB], writes=[ctx.bufs["dbg_sm"]], dma=True)
        pr.fence()

        if "heads" not in L0_PARTS:
            return
        q4 = [_sb(ctx, stk, f"q4_{i}", [4, T], BF16) for i in range(2)]
        k4 = [_sb(ctx, stk, f"k4_{i}", [4, T], BF16) for i in range(2)]
        q4B = [Buf(f"q4{i}") for i in range(2)]
        k4B = [Buf(f"k4{i}") for i in range(2)]
        vhT = [_sb(ctx, stk, f"vhT{i}", [64, T], BF16) for i in range(2)]
        vhTB = [Buf(f"vhT{i}") for i in range(2)]
        vaug = [_sb(ctx, stk, f"vaug{i}", [128, 32 * 65], BF16) for i in range(2)]
        vaugB = [Buf(f"vaug{i}") for i in range(2)]
        expT = [_sb(ctx, stk, f"expT{i}", [128, 512], BF16) for i in range(4)]
        expTB = [Buf(f"expT{i}") for i in range(4)]
        mskb = [_sb(ctx, stk, "mskb0", [128, 512], F32)] * 2
        mskbB = [Buf("mskb0")] * 2
        oun = [_sb(ctx, stk, "oun0", [65, 512], F32)] * 2
        ounB = [Buf("oun0")] * 2
        zr = [_sb(ctx, stk, f"zr{i}", [1, 512], F32) for i in range(2)]
        zrB = [Buf(f"zr{i}") for i in range(2)]
        aost = [_sb(ctx, stk, f"aost{i}", [64, 512], BF16) for i in range(2)]
        aostB = [Buf(f"aost{i}") for i in range(2)]
        SCB = {0: (0, 1), 1: (4, 5)}
        AVB = {0: 2, 1: 3}

        for hp in range(HEADS_N // 2):
            pair = (2 * hp, 2 * hp + 1)
            for h in pair:
                hb = h % 2
                pr.op("sync", (lambda e, h=h, t=q4[hb]: e.dma_start(
                    out=t[0:3, 0:T], in_=q3s[3 * h:3 * h + 3, 0:T])),
                    reads=[q3sB], writes=[q4B[hb]], dma=True)
                pr.op("sync", (lambda e, h=h, t=q4[hb]: e.dma_start(
                    out=t[3:4, 0:T], in_=qdT[h:h + 1, 0:T])),
                    reads=[qdTB], writes=[q4B[hb]], dma=True)
                pr.op("sync", (lambda e, h=h, t=k4[hb]: e.dma_start(
                    out=t[0:3, 0:T], in_=k3u[3 * h:3 * h + 3, 0:T])),
                    reads=[k3uB], writes=[k4B[hb]], dma=True)
                pr.op("sync", (lambda e, h=h, t=k4[hb]: e.dma_start(
                    out=t[3:4, 0:T], in_=kdT[h:h + 1, 0:T])),
                    reads=[kdTB], writes=[k4B[hb]], dma=True)
                pr.op("sync", (lambda e, h=h, t=vhT[hb]: e.dma_start(
                    out=t[0:64, 0:T], in_=ctx.dram["vTs"][64 * h:64 * (h + 1), 0:T])),
                    reads=[ctx.bufs["vTs"]], writes=[vhTB[hb]], dma=True)
                pr.op("vector", (lambda e, t=vaug[hb]: e.memset(t[0:128, 0:32 * 65], 1.0)),
                      writes=[vaugB[hb]])
                for j in range(32):
                    ptp = ctx.pt[j % 2]
                    ptpB = ctx.ptB[j % 2]
                    pr.op("tensor", (lambda e, j=j, t=vhT[hb], ptp=ptp: e.transpose(
                        ptp[0:128, 0:64], t[0:64, 128 * j:128 * (j + 1)],
                        c["c_id_bf"][0:64, 0:64])),
                        reads=[vhTB[hb], cB], writes=[ptpB])
                    pr.op("vector", (lambda e, j=j, t=vaug[hb], ptp=ptp: e.tensor_copy(
                        t[0:128, 65 * j:65 * j + 64], ptp[0:128, 0:64])),
                        reads=[ptpB], writes=[vaugB[hb]])
            for tc in range(8):
                tb = tc % 2
                jmax = 4 * tc + 3

                def emit_sc_exp(h, j, tc=tc):
                    hb = h % 2
                    psc = pb[SCB[hb][j % 2]]
                    pscB = pbB[SCB[hb][j % 2]]
                    eb = 2 * hb + j % 2
                    pr.op("tensor", (lambda e, j=j, tc=tc, psc=psc, tk=k4[hb], tq=q4[hb]:
                                     e.matmul(
                        psc[0:128, 0:512], tk[0:4, 128 * j:128 * (j + 1)],
                        tq[0:4, 512 * tc:512 * (tc + 1)], start=True, stop=True)),
                        reads=[k4B[hb], q4B[hb]], writes=[pscB])
                    if j >= 4 * tc:
                        m = j - 4 * tc
                        pr.op("vector", (lambda e, m=m, psc=psc, mk=mskb[hb]: e.tensor_add(
                            mk[0:128, 0:512], c["c_maskc"][0:128, 512 * m:512 * (m + 1)],
                            psc[0:128, 0:512])), reads=[pscB, cB], writes=[mskbB[hb]])
                        pr.op("scalar", (lambda e, eb=eb, mk=mskb[hb]: e.activation(
                            expT[eb][0:128, 0:512], mk[0:128, 0:512], AF.Exp)),
                            reads=[mskbB[hb]], writes=[expTB[eb]])
                    else:
                        pr.op("scalar", (lambda e, eb=eb, psc=psc: e.activation(
                            expT[eb][0:128, 0:512], psc[0:128, 0:512], AF.Exp)),
                            reads=[pscB], writes=[expTB[eb]])

                def emit_av(h, j, jmax=jmax):
                    hb = h % 2
                    eb = 2 * hb + j % 2
                    bav = AVB[hb]
                    pr.op("tensor", (lambda e, j=j, jmax=jmax, tv=vaug[hb], eb=eb,
                                     bav=bav: e.matmul(
                        pb[bav][0:65, 0:512], tv[0:128, 65 * j:65 * j + 65],
                        expT[eb][0:128, 0:512], start=(j == 0), stop=(j == jmax))),
                        reads=[vaugB[hb], expTB[eb]], writes=[pbB[bav]])

                for j in range(jmax + 1):
                    for h in pair:
                        emit_sc_exp(h, j)
                        if j > 0:
                            emit_av(h, j - 1)
                for h in pair:
                    emit_av(h, jmax)
                for h in pair:
                    hb = h % 2
                    bav = AVB[hb]
                    bz = SCB[hb][0]
                    pr.op("vector", (lambda e, tb=tb, bav=bav: e.tensor_copy(
                        oun[tb][0:65, 0:512], pb[bav][0:65, 0:512])),
                        reads=[pbB[bav]], writes=[ounB[tb]])
                    pr.op("vector", (lambda e, tb=tb, z=zr[hb]: e.reciprocal(
                        z[0:1, 0:512], oun[tb][64:65, 0:512])),
                        reads=[ounB[tb]], writes=[zrB[hb]])
                    pr.op("tensor", (lambda e, z=zr[hb], bz=bz: e.matmul(
                        pb[bz][0:64, 0:512], c["c_ones_f"][0:1, 0:64], z[0:1, 0:512],
                        start=True, stop=True)), reads=[zrB[hb], cB], writes=[pbB[bz]])
                    pr.op("vector", (lambda e, tb=tb, a=aost[hb], bz=bz: e.tensor_mul(
                        a[0:64, 0:512], oun[tb][0:64, 0:512], pb[bz][0:64, 0:512])),
                        reads=[ounB[tb], pbB[bz]], writes=[aostB[hb]])
                    pr.op("sync", (lambda e, h=h, tc=tc, a=aost[hb]: e.dma_start(
                        out=ctx.dram["aoTs"][64 * h:64 * (h + 1), 512 * tc:512 * (tc + 1)],
                        in_=a[0:64, 0:512])),
                        reads=[aostB[hb]], writes=[ctx.bufs["aoTs"]], dma=True)
        pr.fence()

    if "lna" not in L0_PARTS:
        return
    st_gemm_ln(ctx, src="aoTs", wname="L0_attn_wo", K=1024, res=["xT32"],
               out32="r1_32", outbf="r1_bf", gate=True)
    pr.fence()


def _col_loads(ctx, pr, stk, name, src_ap_fn, n, tag, buf=None, dt=F32):
    t = _sb(ctx, stk, tag, [128, n], dt)
    B = buf if buf is not None else Buf(tag)
    q = "sync" if dt == F32 else "gpsimd"
    for i in range(n):
        ap = src_ap_fn(i)
        pr.op(q, (lambda e, i=i, ap=ap: e.dma_start(
            out=t[0:128, i:i + 1], in_=ap,
            allow_slow_non_contiguous=True)), writes=[B], dma=True)
    return t, B


def _ln_rows(ctx, pr, rows, rowsB, sbank, qbank, eps, nfeat):
    pb, pbB = ctx.pb, ctx.pbB
    cB = ctx.cB
    pr.op("vector", (lambda e: e.tensor_scalar(
        rows[0:1, 0:512], pb[sbank][0:1, 0:512], 1.0 / nfeat, None, ALU.mult)),
        reads=[pbB[sbank]], writes=[rowsB])
    pr.op("scalar", (lambda e: e.activation(
        rows[0:1, 512:1024], rows[0:1, 0:512], AF.Square)),
        reads=[rowsB], writes=[rowsB])
    pr.op("vector", (lambda e: e.scalar_tensor_tensor(
        rows[0:1, 1024:1536], pb[qbank][0:1, 0:512], 1.0 / nfeat, rows[0:1, 512:1024],
        ALU.mult, ALU.subtract)), reads=[pbB[qbank], rowsB], writes=[rowsB])
    pr.op("scalar", (lambda e: e.activation(
        rows[0:1, 1536:2048], rows[0:1, 1024:1536], AF.Ln, bias=eps)),
        reads=[rowsB, cB], writes=[rowsB])
    pr.op("scalar", (lambda e: e.activation(
        rows[0:1, 2048:2560], rows[0:1, 1536:2048], AF.Exp, scale=-0.5)),
        reads=[rowsB], writes=[rowsB])


def st_gemm_ln(ctx, src, wname, K, res, out32, outbf, zout=None, affine=None,
               act=None, wscale_input=None, out_scale_name=None, w_from_dram=False,
               gate=False):
    nc, pr, io, c = ctx.nc, ctx.pr, ctx.io, ctx.c
    pb, pbB = ctx.pb, ctx.pbB
    cB = ctx.cB
    with contextlib.ExitStack() as stk:
        nk = K // 128
        w_sb = _sb(ctx, stk, "bw_sb", [128, nk * 1024], BF16)
        wB = Buf("bw")
        if wscale_input is not None:
            wsname, wsper = wscale_input
            wsc, wscB = _col_loads(ctx, pr, stk, "wsc",
                                   lambda i: io[wsname][128 * i:128 * (i + 1)][:, None],
                                   wsper, "wsc")
        if w_from_dram:
            for k in range(nk):
                pr.op("sync", (lambda e, k=k: e.dma_start(
                    out=w_sb[0:128, 1024 * k:1024 * (k + 1)],
                    in_=ctx.dram[wname][0:1024, 128 * k:128 * (k + 1)],
                    transpose=True)),
                    reads=[ctx.bufs[wname]], writes=[wB], dma=True)
        else:
            pr.op("gpsimd", (lambda e: e.dma_start(
                out=w_sb[0:128, 0:nk * 1024].rearrange("p (k m) -> p k m", k=nk),
                in_=io[wname][0:K, :].rearrange("(k p) m -> p k m", p=128))),
                writes=[wB], dma=True)
        for k in range(nk):
            if wscale_input is not None:
                pr.op("vector", (lambda e, k=k: e.tensor_scalar(
                    w_sb[0:128, 1024 * k:1024 * (k + 1)],
                    w_sb[0:128, 1024 * k:1024 * (k + 1)],
                    wsc[0:128, (k % wsper):(k % wsper) + 1], None, ALU.mult)),
                    reads=[wB, wscB], writes=[wB])
        acols = None
        if act is not None:
            acols, acolsB = _col_loads(ctx, pr, stk, "acols",
                                       lambda i: io[act[1]][128 * i:128 * (i + 1)][:, None],
                                       8, "acols")
        oscols = None
        if out_scale_name is not None:
            oscols, oscolsB = _col_loads(ctx, pr, stk, "oscols",
                                         lambda i: io[out_scale_name][128 * i:128 * (i + 1)][:, None],
                                         8, "oscols")
        gcols = bcols = None
        if affine is not None:
            gcols, gcolsB = _col_loads(ctx, pr, stk, "gcols",
                                       lambda i: io[affine[0]][128 * i:128 * (i + 1)][:, None],
                                       8, "lngc")
            bcols, bcolsB = _col_loads(ctx, pr, stk, "bcols",
                                       lambda i: io[affine[1]][128 * i:128 * (i + 1)][:, None],
                                       8, "lnbc")
        src_in = [_sb(ctx, stk, f"bsrc{i}", [128, nk * 512], BF16) for i in range(2)]
        srcB = [Buf(f"bsrc{i}") for i in range(2)]
        res_in = [_sb(ctx, stk, "bres", [128, 8 * 512], F32)] * 2
        resB = [Buf("bres")] * 2
        if len(res) > 1:
            res2_in = [_sb(ctx, stk, "bres2", [128, 8 * 512], F32)] * 2
            res2B = [Buf("bres2")] * 2
        zbuf = _sb(ctx, stk, "bz", [128, 8 * 512], F32)
        zB = Buf("bz")
        zbf = [_sb(ctx, stk, f"bzb{i}", [128, 512], BF16) for i in range(2)]
        zbfB = [Buf(f"bzb{i}") for i in range(2)]
        zsq = [_sb(ctx, stk, f"bzs{i}", [128, 512], BF16) for i in range(2)]
        zsqB = [Buf(f"bzs{i}") for i in range(2)]
        rows = _sb(ctx, stk, "brows", [1, 5 * 512], F32)
        rowsB = Buf("brows")
        tg = [_sb(ctx, stk, f"btg{i}", [128, 512], F32) for i in range(2)]
        tgB = [Buf(f"btg{i}") for i in range(2)]
        yst = [_sb(ctx, stk, f"by{i}", [128, 512], F32) for i in range(4)]
        ystB = [Buf(f"by{i}") for i in range(4)]
        ybf = [_sb(ctx, stk, f"byb{i}", [128, 512], BF16) for i in range(4)]
        ybfB = [Buf(f"byb{i}") for i in range(4)]
        if gate:
            wgf = _sb(ctx, stk, "wgf", [128, 8 * 8], F32)
            wgfB = Buf("wgf")
            lgst = _sb(ctx, stk, "lgst", [128, 32], F32)
            lgstB = Buf("lgst")
            lgacc = _sb(ctx, stk, "lgacc", [128, 32], F32)
            lgaccB = Buf("lgacc")
            for k in range(8):
                pr.op("sync", (lambda e, k=k: e.dma_start(
                    out=wgf[0:128, 8 * k:8 * (k + 1)],
                    in_=io["L0_ffn_Wg"][128 * k:128 * (k + 1), :])), writes=[wgfB], dma=True)

        for tc in range(8):
            bi = tc % 2
            pr.op("sync", (lambda e, tc=tc, t=src_in[bi]: e.dma_start(
                out=t[0:128, 0:nk * 512].rearrange("p (k t) -> p k t", k=nk),
                in_=ctx.dram[src][0:K, 512 * tc:512 * (tc + 1)].rearrange(
                    "(k p) t -> p k t", p=128))),
                reads=[ctx.bufs[src]], writes=[srcB[bi]], dma=True)
            for m in range(8):
                mb = m % 2
                pm = pb[0]
                pmB = pbB[0]
                for k in range(nk):
                    pr.op("tensor", (lambda e, k=k, m=m, pm=pm, t=src_in[bi]: e.matmul(
                        pm[0:128, 0:512], w_sb[0:128, 1024 * k + 128 * m:1024 * k + 128 * (m + 1)],
                        t[0:128, 512 * k:512 * (k + 1)], start=(k == 0), stop=(k == nk - 1))),
                        reads=[wB, srcB[bi]], writes=[pmB])
                if m == 0:
                    pr.op("sync", (lambda e, tc=tc, t=res_in[bi]: e.dma_start(
                        out=t[0:128, 0:8 * 512].rearrange("p (k t) -> p k t", k=8),
                        in_=ctx.dram[res[0]][0:1024, 512 * tc:512 * (tc + 1)].rearrange(
                            "(k p) t -> p k t", p=128))),
                        reads=[ctx.bufs[res[0]]], writes=[resB[bi]], dma=True)
                    if len(res) > 1:
                        pr.op("sync", (lambda e, tc=tc, t=res2_in[bi]: e.dma_start(
                            out=t[0:128, 0:8 * 512].rearrange("p (k t) -> p k t", k=8),
                            in_=ctx.dram[res[1]][0:1024, 512 * tc:512 * (tc + 1)].rearrange(
                                "(k p) t -> p k t", p=128))),
                            reads=[ctx.bufs[res[1]]], writes=[res2B[bi]], dma=True)
                if act is not None:
                    pr.op("scalar", (lambda e, m=m, t=tg[mb], pm=pm: e.activation(
                        t[0:128, 0:512], pm[0:128, 0:512], AF.Gelu,
                        bias=acols[0:128, m:m + 1])),
                        reads=[pmB, acolsB], writes=[tgB[mb]])
                    pr.op("vector", (lambda e, m=m, t=tg[mb], r=res_in[bi]: e.tensor_add(
                        zbuf[0:128, 512 * m:512 * (m + 1)], t[0:128, 0:512],
                        r[0:128, 512 * m:512 * (m + 1)])),
                        reads=[tgB[mb], resB[bi]], writes=[zB])
                elif out_scale_name is not None:
                    pr.op("vector", (lambda e, m=m, r=res_in[bi], pm=pm: e.scalar_tensor_tensor(
                        zbuf[0:128, 512 * m:512 * (m + 1)], pm[0:128, 0:512],
                        oscols[0:128, m:m + 1], r[0:128, 512 * m:512 * (m + 1)],
                        ALU.mult, ALU.add)),
                        reads=[pmB, oscolsB, resB[bi]], writes=[zB])
                else:
                    pr.op("vector", (lambda e, m=m, r=res_in[bi], pm=pm: e.tensor_add(
                        zbuf[0:128, 512 * m:512 * (m + 1)], r[0:128, 512 * m:512 * (m + 1)],
                        pm[0:128, 0:512])), reads=[resB[bi], pmB], writes=[zB])
                if len(res) > 1:
                    pr.op("vector", (lambda e, m=m, r=res2_in[bi]: e.tensor_add(
                        zbuf[0:128, 512 * m:512 * (m + 1)],
                        zbuf[0:128, 512 * m:512 * (m + 1)],
                        r[0:128, 512 * m:512 * (m + 1)])),
                        reads=[res2B[bi], zB], writes=[zB])
                if zout is not None:
                    pr.op("sync", (lambda e, m=m, tc=tc: e.dma_start(
                        out=ctx.dram[zout][128 * m:128 * (m + 1), 512 * tc:512 * (tc + 1)],
                        in_=zbuf[0:128, 512 * m:512 * (m + 1)])),
                        reads=[zB], writes=[ctx.bufs[zout]], dma=True)
                pr.op("vector", (lambda e, m=m, zb=zbf[mb]: e.tensor_copy(
                    zb[0:128, 0:512], zbuf[0:128, 512 * m:512 * (m + 1)])),
                    reads=[zB], writes=[zbfB[mb]])
                pr.op("scalar", (lambda e, m=m, zb=zbf[mb], zs=zsq[mb]: e.activation(
                    zs[0:128, 0:512], zb[0:128, 0:512], AF.Square)),
                    reads=[zbfB[mb]], writes=[zsqB[mb]])
                pr.op("tensor", (lambda e, m=m, zb=zbf[mb]: e.matmul(
                    pb[2][0:1, 0:512], c["c_onescol_bf"][0:128, 0:1],
                    zb[0:128, 0:512], start=(m == 0), stop=(m == 7))),
                    reads=[zbfB[mb], cB], writes=[pbB[2]])
                pr.op("tensor", (lambda e, m=m, zs=zsq[mb]: e.matmul(
                    pb[3][0:1, 0:512], c["c_onescol_bf"][0:128, 0:1],
                    zs[0:128, 0:512], start=(m == 0), stop=(m == 7))),
                    reads=[zsqB[mb], cB], writes=[pbB[3]])
            _ln_rows(ctx, pr, rows, rowsB, 2, 3, 1e-5, 1024)
            pr.op("tensor", lambda e: e.matmul(
                pb[4][0:128, 0:512], c["c_ones_f"][0:1, 0:128], rows[0:1, 0:512],
                start=True, stop=True), reads=[rowsB, cB], writes=[pbB[4]])
            pr.op("tensor", lambda e: e.matmul(
                pb[5][0:128, 0:512], c["c_ones_f"][0:1, 0:128], rows[0:1, 2048:2560],
                start=True, stop=True), reads=[rowsB, cB], writes=[pbB[5]])
            for m in range(8):
                yb = m % 4
                pr.op("vector", (lambda e, m=m, y=yst[yb]: e.tensor_sub(
                    y[0:128, 0:512], zbuf[0:128, 512 * m:512 * (m + 1)], pb[4][0:128, 0:512])),
                    reads=[zB, pbB[4]], writes=[ystB[yb]])
                pr.op("vector", (lambda e, m=m, y=yst[yb]: e.tensor_mul(
                    y[0:128, 0:512], y[0:128, 0:512], pb[5][0:128, 0:512])),
                    reads=[ystB[yb], pbB[5]], writes=[ystB[yb]])
                if affine is not None:
                    pr.op("vector", (lambda e, m=m, y=yst[yb]: e.tensor_scalar(
                        y[0:128, 0:512], y[0:128, 0:512], gcols[0:128, m:m + 1],
                        bcols[0:128, m:m + 1], ALU.mult, ALU.add)),
                        reads=[ystB[yb], gcolsB, bcolsB], writes=[ystB[yb]])
                pr.op("vector", (lambda e, m=m, y=yst[yb], b=ybf[yb]: e.tensor_copy(
                    b[0:128, 0:512], y[0:128, 0:512])), reads=[ystB[yb]], writes=[ybfB[yb]])
                if gate:
                    if m == 0:
                        pr.op("vector", lambda e: e.memset(lgacc[0:128, 0:32], 0.0),
                              writes=[lgaccB])
                    for ts in range(4):
                        pr.op("tensor", (lambda e, m=m, ts=ts, y=yst[yb]: e.matmul(
                            pb[1][0:128, 8 * ts:8 * (ts + 1)],
                            y[0:128, 128 * ts:128 * (ts + 1)],
                            wgf[0:128, 8 * m:8 * (m + 1)],
                            start=True, stop=True)),
                            reads=[ystB[yb], wgfB], writes=[pbB[1]])
                    pr.op("vector", lambda e: e.tensor_add(
                        lgacc[0:128, 0:32], lgacc[0:128, 0:32], pb[1][0:128, 0:32]),
                        reads=[pbB[1], lgaccB], writes=[lgaccB])
                if out32 is not None:
                    pr.op("sync", (lambda e, m=m, tc=tc, y=yst[yb]: e.dma_start(
                        out=ctx.dram[out32][128 * m:128 * (m + 1), 512 * tc:512 * (tc + 1)],
                        in_=y[0:128, 0:512])), reads=[ystB[yb]], writes=[ctx.bufs[out32]],
                        dma=True)
                pr.op("sync", (lambda e, m=m, tc=tc, b=ybf[yb]: e.dma_start(
                    out=ctx.dram[outbf][128 * m:128 * (m + 1), 512 * tc:512 * (tc + 1)],
                    in_=b[0:128, 0:512])), reads=[ybfB[yb]], writes=[ctx.bufs[outbf]],
                    dma=True)
            if gate:
                pr.op("vector", lambda e: e.tensor_copy(lgst[0:128, 0:32], lgacc[0:128, 0:32]),
                      reads=[lgaccB], writes=[lgstB])
                pr.op("sync", (lambda e, tc=tc: e.dma_start(
                    out=ctx.dram["lgts"][512 * tc:512 * (tc + 1), 0:8].rearrange(
                        "(s p) e -> p s e", p=128), in_=lgst[0:128, 0:32].rearrange(
                        "p (s e) -> p s e", s=4))),
                    reads=[lgstB], writes=[ctx.bufs["lgts"]], dma=True)


def st_moe(ctx):
    nc, pr, io, c = ctx.nc, ctx.pr, ctx.io, ctx.c
    pb, pbB = ctx.pb, ctx.pbB
    cB = ctx.cB
    with contextlib.ExitStack() as stk:
        xqs = _sb(ctx, stk, "xqs", [128, 8 * 1024], BF16)
        xqsB = Buf("xqs")
        w2sh = _sb(ctx, stk, "w2s", [128, 8 * 1024], BF16)
        w2shB = Buf("w2s")
        wgt = [
            dict(w0=_sb(ctx, stk, f"w0s{i}", [128, 8 * 1024], BF16),
                 w1=_sb(ctx, stk, f"w1s{i}", [128, 8 * 1024], BF16),
                 w2=w2sh, w2B=w2shB,
                 B=Buf(f"moew{i}")) for i in range(2)
        ]
        ug = [_sb(ctx, stk, "ugb0", [128, 8 * 1024], BF16)] * 2
        ugB = [Buf("ugb0")] * 2
        acc = _sb(ctx, stk, "macc", [128, 8 * 1024], F32)
        accB = Buf("macc")
        usb = [_sb(ctx, stk, f"usb{i}", [128, 512], F32) for i in range(2)]
        usbB = [Buf(f"usb{i}") for i in range(2)]
        gsb = [_sb(ctx, stk, f"gsb{i}", [128, 512], F32) for i in range(2)]
        gsbB = [Buf(f"gsb{i}") for i in range(2)]
        combQ = _sb(ctx, stk, "combQ", [128, 64], F32)
        combQB = Buf("combQ")
        rt = _sb(ctx, stk, "rt", [128, 5 * 8], F32)
        rtB = Buf("rt")
        rcol = _sb(ctx, stk, "rcol", [128, 8], F32)
        rcolB = Buf("rcol")
        lgq = _sb(ctx, stk, "lgq", [128, 64], F32)
        lgqB = Buf("lgq")
        wg_sb = _sb(ctx, stk, "wg_sb", [128, 8 * 8], BF16)
        bgb = _sb(ctx, stk, "bgb", [128, 8], F32)
        bgrow = _sb(ctx, stk, "bgrow", [1, 8], F32)
        wgB = Buf("wg")
        b0c = _sb(ctx, stk, "b0c", [128, 64], F32)
        b1c = _sb(ctx, stk, "b1c", [128, 64], F32)
        bcB = Buf("bc")
        b2a = _sb(ctx, stk, "b2a", [8, 1024], BF16)
        b2aB = Buf("b2a")
        ctb = _sb(ctx, stk, "ctb", [8, 128], BF16)
        ctbB = Buf("ctb")
        rin = [_sb(ctx, stk, f"mrin{i}", [128, 512], F32) for i in range(2)]
        rinB = [Buf(f"mrin{i}") for i in range(2)]
        zbuf = [_sb(ctx, stk, "mz0", [128, 8 * 512], F32)] * 2
        zB = [Buf("mz0")] * 2
        zbf = [_sb(ctx, stk, f"mzb{i}", [128, 512], BF16) for i in range(2)]
        zbfB = [Buf(f"mzb{i}") for i in range(2)]
        zsq = [_sb(ctx, stk, f"mzs{i}", [128, 512], BF16) for i in range(2)]
        zsqB = [Buf(f"mzs{i}") for i in range(2)]
        rows = _sb(ctx, stk, "mrows", [1, 5 * 512], F32)
        rowsB = Buf("mrows")
        yst = [_sb(ctx, stk, f"my{i}", [128, 512], F32) for i in range(2)]
        ystB = [Buf(f"my{i}") for i in range(2)]
        ybf = [_sb(ctx, stk, f"myb{i}", [128, 512], BF16) for i in range(2)]
        ybfB = [Buf(f"myb{i}") for i in range(2)]

        pr.op("sync", lambda e: e.dma_start(out=bgrow[0:1, 0:8], in_=io["L0_ffn_bg"][None, :]),
              writes=[wgB], dma=True)
        pr.op("tensor", lambda e: e.matmul(pb[4][0:128, 0:8], c["c_ones_f"][0:1, 0:128],
                                           bgrow[0:1, 0:8], start=True, stop=True),
              reads=[wgB, cB], writes=[pbB[4]])
        pr.op("vector", lambda e: e.tensor_copy(bgb[0:128, 0:8], pb[4][0:128, 0:8]),
              reads=[pbB[4]], writes=[wgB])
        for ee in range(8):
            for hs in range(8):
                pr.op("sync", (lambda e, ee=ee, hs=hs: e.dma_start(
                    out=b0c[0:128, 8 * ee + hs:8 * ee + hs + 1],
                    in_=io["L0_ffn_b0"][ee, 128 * hs:128 * (hs + 1)][:, None],
                    allow_slow_non_contiguous=True)), writes=[bcB], dma=True)
                pr.op("sync", (lambda e, ee=ee, hs=hs: e.dma_start(
                    out=b1c[0:128, 8 * ee + hs:8 * ee + hs + 1],
                    in_=io["L0_ffn_b1"][ee, 128 * hs:128 * (hs + 1)][:, None],
                    allow_slow_non_contiguous=True)), writes=[bcB], dma=True)
        pr.op("gpsimd", lambda e: e.dma_start(out=b2a[0:8, 0:1024], in_=io["L0_ffn_b2"]),
              writes=[b2aB], dma=True)

        for tq in range(4):
            pr.op("sync", (lambda e, tq=tq: e.dma_start(
                out=xqs[0:128, 0:8192].rearrange("p (k t) -> p k t", k=8),
                in_=ctx.dram["r1_bf"][0:1024, 1024 * tq:1024 * (tq + 1)].rearrange(
                    "(k p) t -> p k t", p=128))),
                reads=[ctx.bufs["r1_bf"]], writes=[xqsB], dma=True)
            for tt in range(8):
                pr.op("sync", (lambda e, tt=tt, tq=tq: e.dma_start(
                    out=lgq[0:128, 8 * tt:8 * (tt + 1)],
                    in_=ctx.dram["lgts"][1024 * tq + 128 * tt:1024 * tq + 128 * (tt + 1), 0:8])),
                    reads=[ctx.bufs["lgts"]], writes=[lgqB], dma=True)
            for tt in range(8):
                pr.op("vector", (lambda e, tt=tt: e.tensor_add(
                    rt[0:128, 0:8], bgb[0:128, 0:8], lgq[0:128, 8 * tt:8 * (tt + 1)])),
                      reads=[wgB, lgqB], writes=[rtB])
                pr.op("vector", lambda e: e.reduce_max(rcol[0:128, 0:1], rt[0:128, 0:8], AX.X),
                      reads=[rtB], writes=[rcolB])
                pr.op("vector", lambda e: e.tensor_scalar(rt[0:128, 8:16], rt[0:128, 0:8],
                                                          rcol[0:128, 0:1], None, ALU.is_equal),
                      reads=[rtB, rcolB], writes=[rtB])
                pr.op("vector", lambda e: e.scalar_tensor_tensor(
                    rt[0:128, 16:24], rt[0:128, 8:16], -1e30, rt[0:128, 0:8],
                    ALU.mult, ALU.add), reads=[rtB], writes=[rtB])
                pr.op("vector", lambda e: e.reduce_max(rcol[0:128, 1:2], rt[0:128, 16:24], AX.X),
                      reads=[rtB, rcolB], writes=[rcolB])
                pr.op("vector", lambda e: e.tensor_scalar(rt[0:128, 24:32], rt[0:128, 16:24],
                                                          rcol[0:128, 1:2], None, ALU.is_equal),
                      reads=[rtB, rcolB], writes=[rtB])
                pr.op("vector", lambda e: e.tensor_scalar(rcol[0:128, 4:5], rcol[0:128, 0:1],
                                                          rcol[0:128, 1:2], None, ALU.subtract),
                      reads=[rcolB], writes=[rcolB])
                pr.op("scalar", lambda e: e.activation(rcol[0:128, 2:3], rcol[0:128, 4:5],
                                                       AF.Sigmoid),
                      reads=[rcolB], writes=[rcolB])
                pr.op("vector", lambda e: e.tensor_scalar(rcol[0:128, 3:4], rcol[0:128, 2:3],
                                                          -1.0, 1.0, ALU.mult, ALU.add),
                      reads=[rcolB], writes=[rcolB])
                pr.op("vector", lambda e: e.tensor_scalar(rt[0:128, 32:40], rt[0:128, 8:16],
                                                          rcol[0:128, 2:3], None, ALU.mult),
                      reads=[rtB, rcolB], writes=[rtB])
                pr.op("vector", (lambda e, tt=tt: e.scalar_tensor_tensor(
                    combQ[0:128, 8 * tt:8 * (tt + 1)], rt[0:128, 24:32], rcol[0:128, 3:4],
                    rt[0:128, 32:40], ALU.mult, ALU.add)),
                    reads=[rtB, rcolB], writes=[combQB])
            pr.op("vector", lambda e: e.memset(acc[0:128, 0:8192], 0.0), writes=[accB])

            sets = [(ee, g) for ee in range(8) for g in range(4)]
            for si, (ee, g) in enumerate(sets):
                wb = wgt[si % 2]
                pr.op("gpsimd", (lambda e, ee=ee, g=g, wb=wb: e.dma_start(
                    out=wb["w0"][0:128, 0:8192].rearrange("p (k h) -> p k h", k=8),
                    in_=io["L0_ffn_W0"][ee, :, 1024 * g:1024 * (g + 1)].rearrange(
                        "(k p) h -> p k h", p=128))), writes=[wb["B"]], dma=True)
                pr.op("gpsimd", (lambda e, ee=ee, g=g, wb=wb: e.dma_start(
                    out=wb["w1"][0:128, 0:8192].rearrange("p (k h) -> p k h", k=8),
                    in_=io["L0_ffn_W1"][ee, :, 1024 * g:1024 * (g + 1)].rearrange(
                        "(k p) h -> p k h", p=128))), writes=[wb["B"]], dma=True)
                pr.op("gpsimd", (lambda e, ee=ee, g=g, wb=wb: e.dma_start(
                    out=wb["w2"][0:128, 0:8192].rearrange("p (k h) -> p k h", k=8),
                    in_=io["L0_ffn_W2"][ee, 1024 * g:1024 * (g + 1), :].rearrange(
                        "(k p) h -> p k h", p=128))), writes=[wb["w2B"]], dma=True)
                for idx in range(16):
                    hs, th = idx // 2, idx % 2
                    pu = pb[(idx % 2) * 2]
                    pg = pb[(idx % 2) * 2 + 1]
                    puB = pbB[(idx % 2) * 2]
                    pgB = pbB[(idx % 2) * 2 + 1]
                    for k in range(8):
                        pr.op("tensor", (lambda e, k=k, hs=hs, th=th, wb=wb, pu=pu: e.matmul(
                            pu[0:128, 0:512],
                            wb["w0"][0:128, 1024 * k + 128 * hs:1024 * k + 128 * (hs + 1)],
                            xqs[0:128, 1024 * k + 512 * th:1024 * k + 512 * (th + 1)],
                            start=(k == 0), stop=(k == 7))),
                            reads=[wb["B"], xqsB], writes=[puB])
                    for k in range(8):
                        pr.op("tensor", (lambda e, k=k, hs=hs, th=th, wb=wb, pg=pg: e.matmul(
                            pg[0:128, 0:512],
                            wb["w1"][0:128, 1024 * k + 128 * hs:1024 * k + 128 * (hs + 1)],
                            xqs[0:128, 1024 * k + 512 * th:1024 * k + 512 * (th + 1)],
                            start=(k == 0), stop=(k == 7))),
                            reads=[wb["B"], xqsB], writes=[pgB])
                    ub = usb[idx % 2]; gb = gsb[idx % 2]
                    pr.op("vector", (lambda e, ee=ee, hs=hs, ub=ub, pu=pu: e.tensor_scalar(
                        ub[0:128, 0:512], pu[0:128, 0:512],
                        b0c[0:128, 8 * ee + hs:8 * ee + hs + 1], None, ALU.add)),
                        reads=[puB, bcB], writes=[usbB[idx % 2]])
                    pr.op("scalar", (lambda e, ee=ee, hs=hs, gb=gb, pg=pg: e.activation(
                        gb[0:128, 0:512], pg[0:128, 0:512], AF.Silu,
                        bias=b1c[0:128, 8 * ee + hs:8 * ee + hs + 1])),
                        reads=[pgB, bcB], writes=[gsbB[idx % 2]])
                    pr.op("vector", (lambda e, hs=hs, th=th, ub=ub, gb=gb, u=ug[si % 2]:
                                     e.tensor_mul(
                        u[0:128, 1024 * hs + 512 * th:1024 * hs + 512 * (th + 1)],
                        ub[0:128, 0:512], gb[0:128, 0:512])),
                        reads=[usbB[idx % 2], gsbB[idx % 2]], writes=[ugB[si % 2]])
                for tt in range(8):
                    for ch in range(2):
                        py = pb[4 + ((tt * 2 + ch) % 2)]
                        pyB = pbB[4 + ((tt * 2 + ch) % 2)]
                        for hs in range(8):
                            pr.op("tensor", (lambda e, hs=hs, tt=tt, ch=ch, wb=wb,
                                             u=ug[si % 2], py=py: e.matmul(
                                py[0:128, 0:512],
                                u[0:128, 1024 * hs + 128 * tt:1024 * hs + 128 * (tt + 1)],
                                wb["w2"][0:128, 1024 * hs + 512 * ch:1024 * hs + 512 * (ch + 1)],
                                start=(hs == 0), stop=(hs == 7))),
                                reads=[ugB[si % 2], wb["w2B"]], writes=[pyB])
                        pr.op("vector", (lambda e, tt=tt, ch=ch, ee=ee, py=py:
                                         e.scalar_tensor_tensor(
                            acc[0:128, 1024 * tt + 512 * ch:1024 * tt + 512 * (ch + 1)],
                            py[0:128, 0:512], combQ[0:128, 8 * tt + ee:8 * tt + ee + 1],
                            acc[0:128, 1024 * tt + 512 * ch:1024 * tt + 512 * (ch + 1)],
                            ALU.mult, ALU.add)),
                            reads=[pyB, combQB, accB], writes=[accB])
            if tq == 0 and "moe" in ctx.dbg:
                pr.op("sync", lambda e: e.dma_start(out=ctx.dram["dbg_moe"][0:128, :],
                                                    in_=acc[0:128, 0:8192]),
                      reads=[accB], writes=[ctx.bufs["dbg_moe"]], dma=True)
                pr.op("sync", lambda e: e.dma_start(out=ctx.dram["dbg_cmb"][0:128, :],
                                                    in_=combQ[0:128, 0:64]),
                      reads=[combQB], writes=[ctx.bufs["dbg_cmb"]], dma=True)
            for tt in range(8):
                pr.op("tensor", (lambda e, tt=tt: e.transpose(
                    pb[0][0:8, 0:128], combQ[0:128, 8 * tt:8 * (tt + 1)],
                    c["c_id_f"][0:128, 0:128])), reads=[combQB, cB], writes=[pbB[0]])
                pr.op("vector", lambda e: e.tensor_copy(ctb[0:8, 0:128], pb[0][0:8, 0:128]),
                      reads=[pbB[0]], writes=[ctbB])
                for ch in range(2):
                    pr.op("tensor", (lambda e, ch=ch: e.matmul(
                        pb[1][0:128, 0:512], ctb[0:8, 0:128],
                        b2a[0:8, 512 * ch:512 * (ch + 1)], start=True, stop=True)),
                        reads=[ctbB, b2aB], writes=[pbB[1]])
                    pr.op("vector", (lambda e, tt=tt, ch=ch: e.tensor_add(
                        acc[0:128, 1024 * tt + 512 * ch:1024 * tt + 512 * (ch + 1)],
                        acc[0:128, 1024 * tt + 512 * ch:1024 * tt + 512 * (ch + 1)],
                        pb[1][0:128, 0:512])), reads=[pbB[1], accB], writes=[accB])
            for tch in range(2):
                tc = 2 * tq + tch
                bi = tch % 2
                for m in range(8):
                    mb = m % 2
                    pz = pb[m % 2]
                    pzB = pbB[m % 2]
                    pr.op("sync", (lambda e, m=m, tc=tc, t=rin[mb]: e.dma_start(
                        out=t[0:128, 0:512],
                        in_=ctx.dram["r1_32"][128 * m:128 * (m + 1), 512 * tc:512 * (tc + 1)])),
                        reads=[ctx.bufs["r1_32"]], writes=[rinB[mb]], dma=True)
                    for ti in range(4):
                        pr.op("tensor", (lambda e, m=m, ti=ti, tch=tch, pz=pz: e.transpose(
                            pz[0:128, 128 * ti:128 * (ti + 1)],
                            acc[0:128, 1024 * (4 * tch + ti) + 128 * m:
                                1024 * (4 * tch + ti) + 128 * (m + 1)],
                            c["c_id_f"][0:128, 0:128])),
                            reads=[accB, cB], writes=[pzB])
                    pr.op("vector", (lambda e, m=m, t=rin[mb], z=zbuf[bi], pz=pz: e.tensor_add(
                        z[0:128, 512 * m:512 * (m + 1)], t[0:128, 0:512],
                        pz[0:128, 0:512])), reads=[rinB[mb], pzB], writes=[zB[bi]])
                    pr.op("vector", (lambda e, m=m, z=zbuf[bi], zb=zbf[mb]: e.tensor_copy(
                        zb[0:128, 0:512], z[0:128, 512 * m:512 * (m + 1)])),
                        reads=[zB[bi]], writes=[zbfB[mb]])
                    pr.op("scalar", (lambda e, m=m, zb=zbf[mb], zs=zsq[mb]: e.activation(
                        zs[0:128, 0:512], zb[0:128, 0:512],
                        AF.Square)), reads=[zbfB[mb]], writes=[zsqB[mb]])
                    pr.op("tensor", (lambda e, m=m, zb=zbf[mb]: e.matmul(
                        pb[2][0:1, 0:512], c["c_onescol_bf"][0:128, 0:1],
                        zb[0:128, 0:512], start=(m == 0), stop=(m == 7))),
                        reads=[zbfB[mb], cB], writes=[pbB[2]])
                    pr.op("tensor", (lambda e, m=m, zs=zsq[mb]: e.matmul(
                        pb[3][0:1, 0:512], c["c_onescol_bf"][0:128, 0:1],
                        zs[0:128, 0:512], start=(m == 0), stop=(m == 7))),
                        reads=[zsqB[mb], cB], writes=[pbB[3]])
                pr.op("vector", lambda e: e.tensor_scalar(
                    rows[0:1, 0:512], pb[2][0:1, 0:512], 1.0 / 1024, None, ALU.mult),
                    reads=[pbB[2]], writes=[rowsB])
                pr.op("scalar", lambda e: e.activation(
                    rows[0:1, 512:1024], rows[0:1, 0:512], AF.Square),
                    reads=[rowsB], writes=[rowsB])
                pr.op("vector", lambda e: e.scalar_tensor_tensor(
                    rows[0:1, 1024:1536], pb[3][0:1, 0:512], 1.0 / 1024, rows[0:1, 512:1024],
                    ALU.mult, ALU.subtract), reads=[pbB[3], rowsB], writes=[rowsB])
                pr.op("scalar", lambda e: e.activation(
                    rows[0:1, 1536:2048], rows[0:1, 1024:1536], AF.Ln, bias=1e-5),
                    reads=[rowsB, cB], writes=[rowsB])
                pr.op("scalar", lambda e: e.activation(
                    rows[0:1, 2048:2560], rows[0:1, 1536:2048], AF.Exp, scale=-0.5),
                    reads=[rowsB], writes=[rowsB])
                pr.op("tensor", lambda e: e.matmul(
                    pb[4][0:128, 0:512], c["c_ones_f"][0:1, 0:128], rows[0:1, 0:512],
                    start=True, stop=True), reads=[rowsB, cB], writes=[pbB[4]])
                pr.op("tensor", lambda e: e.matmul(
                    pb[5][0:128, 0:512], c["c_ones_f"][0:1, 0:128], rows[0:1, 2048:2560],
                    start=True, stop=True), reads=[rowsB, cB], writes=[pbB[5]])
                for m in range(8):
                    yb = m % 2
                    pr.op("vector", (lambda e, m=m, z=zbuf[bi], y=yst[yb]: e.tensor_sub(
                        y[0:128, 0:512], z[0:128, 512 * m:512 * (m + 1)], pb[4][0:128, 0:512])),
                        reads=[zB[bi], pbB[4]], writes=[ystB[yb]])
                    pr.op("vector", (lambda e, m=m, y=yst[yb]: e.tensor_mul(
                        y[0:128, 0:512], y[0:128, 0:512], pb[5][0:128, 0:512])),
                        reads=[ystB[yb], pbB[5]], writes=[ystB[yb]])
                    pr.op("vector", (lambda e, m=m, y=yst[yb], b=ybf[yb]: e.tensor_copy(
                        b[0:128, 0:512], y[0:128, 0:512])), reads=[ystB[yb]], writes=[ybfB[yb]])
                    pr.op("sync", (lambda e, m=m, tc=tc, y=yst[yb]: e.dma_start(
                        out=ctx.dram["r2_32"][128 * m:128 * (m + 1), 512 * tc:512 * (tc + 1)],
                        in_=y[0:128, 0:512])), reads=[ystB[yb]], writes=[ctx.bufs["r2_32"]],
                        dma=True)
                    pr.op("sync", (lambda e, m=m, tc=tc, b=ybf[yb]: e.dma_start(
                        out=ctx.dram["r2_bf"][128 * m:128 * (m + 1), 512 * tc:512 * (tc + 1)],
                        in_=b[0:128, 0:512])), reads=[ybfB[yb]], writes=[ctx.bufs["r2_bf"]],
                        dma=True)
        pr.fence()


def st_l1attn(ctx):
    nc, pr, io, c = ctx.nc, ctx.pr, ctx.io, ctx.c
    pb, pbB = ctx.pb, ctx.pbB
    pt, ptB = ctx.pt, ctx.ptB
    cB = ctx.cB
    for wn, dst in (("L1_ffn_w1_w", "wbf_w1"), ("L1_ffn_w2_w", "wbf_w2"),
                    ("L1_ffn_h1_w", "wbf_h1"), ("L1_ffn_h2_w", "wbf_h2"),
                    ("L1_ffn_w3_w", "wbf_w3")):
        pr.op("gpsimd", (lambda e, wn=wn, dst=dst: e.dma_start(
            out=ctx.dram[dst][:, :], in_=io[wn][:, :])),
            writes=[ctx.bufs[dst]], dma=True)

    with contextlib.ExitStack() as stk:
        g1c, g1B = _col_loads(ctx, pr, stk, "g1c",
                              lambda i: io["L1_attn_ln1_g"][128 * i:128 * (i + 1)][:, None],
                              8, "g1c")
        b1c, b1B = _col_loads(ctx, pr, stk, "b1c",
                              lambda i: io["L1_attn_ln1_b"][128 * i:128 * (i + 1)][:, None],
                              8, "b1c2")
        rwc, rwB = _col_loads(ctx, pr, stk, "rwc",
                              lambda i: io["L1_attn_rms_in_w"][128 * i:128 * (i + 1)][:, None],
                              8, "rwc")
        rbf_in = [_sb(ctx, stk, f"arin{i}", [128, 8 * 512], BF16) for i in range(2)]
        rbfB = [Buf(f"arin{i}") for i in range(2)]
        r32_in = [_sb(ctx, stk, f"ar32{i}", [128, 8 * 512], F32) for i in range(2)]
        r32B = [Buf(f"ar32{i}") for i in range(2)]
        zsq = [_sb(ctx, stk, f"azs{i}", [128, 512], BF16) for i in range(2)]
        zsqB = [Buf(f"azs{i}") for i in range(2)]
        tbuf = _sb(ctx, stk, "atb", [128, 8 * 512], F32)
        tB = Buf("atb")
        tbf = [_sb(ctx, stk, f"atbf{i}", [128, 512], BF16) for i in range(2)]
        tbfB = [Buf(f"atbf{i}") for i in range(2)]
        rows = _sb(ctx, stk, "arows", [1, 5 * 512], F32)
        rowsB = Buf("arows")
        rows2 = _sb(ctx, stk, "arows2", [1, 2 * 512], F32)
        rows2B = Buf("arows2")
        hst = [_sb(ctx, stk, f"ahs{i}", [128, 512], BF16) for i in range(2)]
        hstB = [Buf(f"ahs{i}") for i in range(2)]
        for tc in range(8):
            bi2 = tc % 2
            pr.op("sync", (lambda e, tc=tc, t=rbf_in[bi2]: e.dma_start(
                out=t[0:128, 0:8 * 512].rearrange("p (k t) -> p k t", k=8),
                in_=ctx.dram["r2_bf"][0:1024, 512 * tc:512 * (tc + 1)].rearrange(
                    "(k p) t -> p k t", p=128))),
                reads=[ctx.bufs["r2_bf"]], writes=[rbfB[bi2]], dma=True)
            pr.op("sync", (lambda e, tc=tc, t=r32_in[bi2]: e.dma_start(
                out=t[0:128, 0:8 * 512].rearrange("p (k t) -> p k t", k=8),
                in_=ctx.dram["r2_32"][0:1024, 512 * tc:512 * (tc + 1)].rearrange(
                    "(k p) t -> p k t", p=128))),
                reads=[ctx.bufs["r2_32"]], writes=[r32B[bi2]], dma=True)
            for m in range(8):
                mb = m % 2
                pr.op("scalar", (lambda e, m=m, t=rbf_in[bi2], zs=zsq[mb]: e.activation(
                    zs[0:128, 0:512], t[0:128, 512 * m:512 * (m + 1)], AF.Square)),
                    reads=[rbfB[bi2]], writes=[zsqB[mb]])
                pr.op("tensor", (lambda e, m=m, t=rbf_in[bi2]: e.matmul(
                    pb[2][0:1, 0:512], c["c_onescol_bf"][0:128, 0:1],
                    t[0:128, 512 * m:512 * (m + 1)], start=(m == 0), stop=(m == 7))),
                    reads=[rbfB[bi2], cB], writes=[pbB[2]])
                pr.op("tensor", (lambda e, m=m, zs=zsq[mb]: e.matmul(
                    pb[3][0:1, 0:512], c["c_onescol_bf"][0:128, 0:1],
                    zs[0:128, 0:512], start=(m == 0), stop=(m == 7))),
                    reads=[zsqB[mb], cB], writes=[pbB[3]])
            _ln_rows(ctx, pr, rows, rowsB, 2, 3, 1e-5, 1024)
            pr.op("tensor", lambda e: e.matmul(
                pb[4][0:128, 0:512], c["c_ones_f"][0:1, 0:128], rows[0:1, 0:512],
                start=True, stop=True), reads=[rowsB, cB], writes=[pbB[4]])
            pr.op("tensor", lambda e: e.matmul(
                pb[5][0:128, 0:512], c["c_ones_f"][0:1, 0:128], rows[0:1, 2048:2560],
                start=True, stop=True), reads=[rowsB, cB], writes=[pbB[5]])
            for m in range(8):
                mb = m % 2
                pr.op("vector", (lambda e, m=m, t=r32_in[bi2]: e.tensor_sub(
                    tbuf[0:128, 512 * m:512 * (m + 1)], t[0:128, 512 * m:512 * (m + 1)],
                    pb[4][0:128, 0:512])), reads=[r32B[bi2], pbB[4]], writes=[tB])
                pr.op("vector", (lambda e, m=m: e.tensor_mul(
                    tbuf[0:128, 512 * m:512 * (m + 1)],
                    tbuf[0:128, 512 * m:512 * (m + 1)], pb[5][0:128, 0:512])),
                    reads=[tB, pbB[5]], writes=[tB])
                pr.op("vector", (lambda e, m=m: e.tensor_scalar(
                    tbuf[0:128, 512 * m:512 * (m + 1)],
                    tbuf[0:128, 512 * m:512 * (m + 1)],
                    g1c[0:128, m:m + 1], b1c[0:128, m:m + 1], ALU.mult, ALU.add)),
                    reads=[tB, g1B, b1B], writes=[tB])
                pr.op("vector", (lambda e, m=m, zb=tbf[mb]: e.tensor_copy(
                    zb[0:128, 0:512], tbuf[0:128, 512 * m:512 * (m + 1)])),
                    reads=[tB], writes=[tbfB[mb]])
                pr.op("scalar", (lambda e, m=m, zb=tbf[mb], zs=zsq[mb]: e.activation(
                    zs[0:128, 0:512], zb[0:128, 0:512], AF.Square)),
                    reads=[tbfB[mb]], writes=[zsqB[mb]])
                pr.op("tensor", (lambda e, m=m, zs=zsq[mb]: e.matmul(
                    pb[3][0:1, 0:512], c["c_onescol_bf"][0:128, 0:1],
                    zs[0:128, 0:512], start=(m == 0), stop=(m == 7))),
                    reads=[zsqB[mb], cB], writes=[pbB[3]])
            pr.op("scalar", lambda e: e.activation(
                rows2[0:1, 0:512], pb[3][0:1, 0:512], AF.Ln, scale=1.0 / 1024, bias=1e-6),
                reads=[pbB[3], cB], writes=[rows2B])
            pr.op("scalar", lambda e: e.activation(
                rows2[0:1, 512:1024], rows2[0:1, 0:512], AF.Exp, scale=-0.5),
                reads=[rows2B], writes=[rows2B])
            pr.op("tensor", lambda e: e.matmul(
                pb[5][0:128, 0:512], c["c_ones_f"][0:1, 0:128], rows2[0:1, 512:1024],
                start=True, stop=True), reads=[rows2B, cB], writes=[pbB[5]])
            for m in range(8):
                mb = m % 2
                pr.op("vector", (lambda e, m=m, hs=hst[mb]: e.tensor_mul(
                    hs[0:128, 0:512], tbuf[0:128, 512 * m:512 * (m + 1)],
                    pb[5][0:128, 0:512])), reads=[tB, pbB[5]], writes=[hstB[mb]])
                pr.op("vector", (lambda e, m=m, hs=hst[mb]: e.tensor_scalar(
                    hs[0:128, 0:512], hs[0:128, 0:512], rwc[0:128, m:m + 1], None,
                    ALU.mult)), reads=[hstB[mb], rwB], writes=[hstB[mb]])
                pr.op("sync", (lambda e, m=m, tc=tc, hs=hst[mb]: e.dma_start(
                    out=ctx.dram["hTbf"][128 * m:128 * (m + 1), 512 * tc:512 * (tc + 1)],
                    in_=hs[0:128, 0:512])), reads=[hstB[mb]], writes=[ctx.bufs["hTbf"]],
                    dma=True)
        pr.fence()

    CHUNK = 128
    NCH = T // CHUNK
    with contextlib.ExitStack() as stko:
        dtA = _sb(ctx, stko, "dtA", [1, 8], F32)
        dtAB = Buf("dtA")
        pr.op("sync", lambda e: e.dma_start(out=dtA[0:1, 0:4], in_=io["L1_attn_dt_bias"][None, :]),
              writes=[dtAB], dma=True)
        pr.op("sync", lambda e: e.dma_start(out=dtA[0:1, 4:8], in_=io["L1_attn_A_log"][None, :]),
              writes=[dtAB], dma=True)
        pr.op("scalar", lambda e: e.activation(dtA[0:1, 4:8], dtA[0:1, 4:8], AF.Exp),
              reads=[dtAB], writes=[dtAB])
        pr.op("vector", lambda e: e.tensor_scalar(dtA[0:1, 4:8], dtA[0:1, 4:8], -1.0,
                                                  None, ALU.mult),
              reads=[dtAB], writes=[dtAB])
        arow = _sb(ctx, stko, "arow", [1, T], F32)
        arowB = Buf("arow")
        brow = _sb(ctx, stko, "brow", [1, T], F32)
        browB = Buf("brow")
        nbrow = _sb(ctx, stko, "nbrow", [1, T], F32)
        nbrowB = Buf("nbrow")
        Grow = _sb(ctx, stko, "Grow", [1, T], F32)
        GrowB = Buf("Grow")
        zrow = _sb(ctx, stko, "zrow", [1, 128], F32)
        zrowB = Buf("zr0")
        pr.op("vector", lambda e: e.memset(zrow[0:1, 0:128], 0.0), writes=[zrowB])
        S32 = _sb(ctx, stko, "S32", [128, 2 * 512], F32)
        S32B = Buf("S32")
        Sbf = _sb(ctx, stko, "Sbf", [128, 2 * 512], BF16)
        SbfB = Buf("Sbf")

        HB = {k: Buf(k) for k in (
            "wh", "wacol", "wbcol", "cwq", "cwk", "cwv", "qp", "kp", "vp",
            "htc0", "htc1", "gst0", "gst1", "cacc", "tka", "tsq", "scc", "tst",
            "tmpr", "qnti0", "qnti1", "knti0", "knti1", "knki0", "knki1",
            "vti0", "vti1", "gti0", "gti1", "gcol2", "gcb", "sc", "eaf", "ewf",
            "tchf", "nab", "natb", "rtn0", "rtn1", "snn0", "snn1", "stn0", "stn1",
            "ut", "ubf", "dbf", "o1t", "wtb", "ot", "osq", "ofin", "kdb", "otst")}
        for h in range(4):
            with contextlib.ExitStack() as stk:
                wq_h = _sb(ctx, stk, "wq_h", [128, 8 * 256], BF16)
                wk_h = _sb(ctx, stk, "wk_h", [128, 8 * 256], BF16)
                wv_h = _sb(ctx, stk, "wv_h", [128, 8 * 512], BF16)
                wg_h = _sb(ctx, stk, "wg_h", [128, 8 * 512], BF16)
                whB = HB["wh"]
                for k in range(8):
                    pr.op("gpsimd", (lambda e, k=k, h=h: e.dma_start(
                        out=wq_h[0:128, 256 * k:256 * (k + 1)],
                        in_=io["L1_attn_Wq"][128 * k:128 * (k + 1), 256 * h:256 * (h + 1)])),
                        writes=[whB], dma=True)
                    pr.op("gpsimd", (lambda e, k=k, h=h: e.dma_start(
                        out=wk_h[0:128, 256 * k:256 * (k + 1)],
                        in_=io["L1_attn_Wk"][128 * k:128 * (k + 1), 256 * h:256 * (h + 1)])),
                        writes=[whB], dma=True)
                    pr.op("gpsimd", (lambda e, k=k, h=h: e.dma_start(
                        out=wv_h[0:128, 512 * k:512 * (k + 1)],
                        in_=io["L1_attn_Wv"][128 * k:128 * (k + 1), 512 * h:512 * (h + 1)])),
                        writes=[whB], dma=True)
                    pr.op("gpsimd", (lambda e, k=k, h=h: e.dma_start(
                        out=wg_h[0:128, 512 * k:512 * (k + 1)],
                        in_=io["L1_attn_Wg"][128 * k:128 * (k + 1), 512 * h:512 * (h + 1)])),
                        writes=[whB], dma=True)
                wacol, waB = _col_loads(ctx, pr, stk, "wacol",
                                        lambda i: io["L1_attn_Wa"][128 * i:128 * (i + 1), h:h + 1],
                                        8, "wacol", buf=HB["wacol"], dt=BF16)
                wbcol, wbB = _col_loads(ctx, pr, stk, "wbcol",
                                        lambda i: io["L1_attn_Wb"][128 * i:128 * (i + 1), h:h + 1],
                                        8, "wbcol", buf=HB["wbcol"], dt=BF16)
                cwq, cwqB = _col_loads(ctx, pr, stk, "cwq",
                                       lambda i: io["L1_attn_conv_q_w"]
                                       [256 * h + 128 * (i // 4):256 * h + 128 * (i // 4 + 1),
                                        i % 4:i % 4 + 1], 8, "cwq", buf=HB["cwq"])
                cwk, cwkB = _col_loads(ctx, pr, stk, "cwk",
                                       lambda i: io["L1_attn_conv_k_w"]
                                       [256 * h + 128 * (i // 4):256 * h + 128 * (i // 4 + 1),
                                        i % 4:i % 4 + 1], 8, "cwk", buf=HB["cwk"])
                cwv, cwvB = _col_loads(ctx, pr, stk, "cwv",
                                       lambda i: io["L1_attn_conv_v_w"]
                                       [512 * h + 128 * (i // 4):512 * h + 128 * (i // 4 + 1),
                                        i % 4:i % 4 + 1], 16, "cwv", buf=HB["cwv"])
                PW = 4099
                qp = _sb(ctx, stk, "qp", [128, 2 * PW], BF16)
                qpB = HB["qp"]
                kp = _sb(ctx, stk, "kp", [128, 2 * PW], BF16)
                kpB = HB["kp"]
                vp = _sb(ctx, stk, "vp", [128, 4 * PW], BF16)
                vpB = HB["vp"]
                pr.op("vector", lambda e: e.memset(qp[0:128, 0:2 * PW], 0.0), writes=[qpB])
                pr.op("vector", lambda e: e.memset(kp[0:128, 0:2 * PW], 0.0), writes=[kpB])
                pr.op("vector", lambda e: e.memset(vp[0:128, 0:4 * PW], 0.0), writes=[vpB])
                htc = [_sb(ctx, stk, f"htc{i}", [128, 8 * 512], BF16) for i in range(2)]
                htcB = [HB["htc0"], HB["htc1"]]
                gst = [_sb(ctx, stk, f"gst{i}", [128, 512], BF16) for i in range(2)]
                gstB = [HB["gst0"], HB["gst1"]]
                for tc in range(8):
                    bi = tc % 2
                    pr.op("sync", (lambda e, tc=tc, t=htc[bi]: e.dma_start(
                        out=t[0:128, 0:8 * 512].rearrange("p (k t) -> p k t", k=8),
                        in_=ctx.dram["hTbf"][0:1024, 512 * tc:512 * (tc + 1)].rearrange(
                            "(k p) t -> p k t", p=128))),
                        reads=[ctx.bufs["hTbf"]], writes=[htcB[bi]], dma=True)
                    for (wt, nsub, dst, dstB) in ((wq_h, 2, qp, qpB), (wk_h, 2, kp, kpB),
                                                  (wv_h, 4, vp, vpB)):
                        for sub in range(nsub):
                            pp = pb[sub % 2]
                            ppB = pbB[sub % 2]
                            for k in range(8):
                                pr.op("tensor", (lambda e, k=k, sub=sub, wt=wt, nsub=nsub,
                                                 t=htc[bi], pp=pp: e.matmul(
                                    pp[0:128, 0:512],
                                    wt[0:128, 128 * nsub * k + 128 * sub:
                                       128 * nsub * k + 128 * (sub + 1)],
                                    t[0:128, 512 * k:512 * (k + 1)],
                                    start=(k == 0), stop=(k == 7))),
                                    reads=[whB, htcB[bi]], writes=[ppB])
                            pr.op("vector", (lambda e, sub=sub, tc=tc, dst=dst, pp=pp:
                                             e.tensor_copy(
                                dst[0:128, PW * sub + 3 + 512 * tc:PW * sub + 3 + 512 * (tc + 1)],
                                pp[0:128, 0:512])), reads=[ppB], writes=[dstB])
                    for ts in range(4):
                        pg = pb[2 + ts % 2]
                        pgB = pbB[2 + ts % 2]
                        for k in range(8):
                            pr.op("tensor", (lambda e, k=k, ts=ts, t=htc[bi], pg=pg: e.matmul(
                                pg[0:128, 0:512],
                                t[0:128, 512 * k + 128 * ts:512 * k + 128 * (ts + 1)],
                                wg_h[0:128, 512 * k:512 * (k + 1)],
                                start=(k == 0), stop=(k == 7))),
                                reads=[whB, htcB[bi]], writes=[pgB])
                        pr.op("scalar", (lambda e, ts=ts, g=gst[ts % 2], pg=pg: e.activation(
                            g[0:128, 0:512], pg[0:128, 0:512], AF.Silu)),
                            reads=[pgB], writes=[gstB[ts % 2]])
                        pr.op("sync", (lambda e, ts=ts, tc=tc, h=h, g=gst[ts % 2]: e.dma_start(
                            out=ctx.dram["gate_d"][512 * tc + 128 * ts:512 * tc + 128 * (ts + 1),
                                                   0:512],
                            in_=g[0:128, 0:512])), reads=[gstB[ts % 2]],
                            writes=[ctx.bufs["gate_d"]], dma=True)
                    for (wc, wcB, dst, dstB, bias) in ((wacol, waB, arow, arowB, True),
                                                       (wbcol, wbB, brow, browB, False)):
                        for k in range(8):
                            pr.op("tensor", (lambda e, k=k, wc=wc, t=htc[bi]: e.matmul(
                                pb[4][0:1, 0:512], wc[0:128, k:k + 1],
                                t[0:128, 512 * k:512 * (k + 1)],
                                start=(k == 0), stop=(k == 7))),
                                reads=[wcB, htcB[bi]], writes=[pbB[4]])
                        if bias:
                            pr.op("vector", (lambda e, tc=tc, dst=dst: e.tensor_scalar(
                                dst[0:1, 512 * tc:512 * (tc + 1)], pb[4][0:1, 0:512],
                                dtA[0:1, h:h + 1], None, ALU.add)),
                                reads=[pbB[4], dtAB], writes=[dstB])
                        else:
                            pr.op("vector", (lambda e, tc=tc, dst=dst: e.tensor_copy(
                                dst[0:1, 512 * tc:512 * (tc + 1)], pb[4][0:1, 0:512])),
                                reads=[pbB[4]], writes=[dstB])
                if h == 0 and "ch" in ctx.dbg:
                    pr.op("gpsimd", lambda e: e.dma_start(
                        out=ctx.dram["dbg_ch"][0:128, 8848:8856], in_=wacol[0:128, 0:8]),
                        reads=[waB], writes=[ctx.bufs["dbg_ch"]], dma=True)
                    pr.op("gpsimd", (lambda e, t=htc[1]: e.dma_start(
                        out=ctx.dram["dbg_ch"][0:128, 8856:9368], in_=t[0:128, 0:512])),
                        reads=[htcB[1]], writes=[ctx.bufs["dbg_ch"]], dma=True)
                    pr.op("sync", lambda e: e.dma_start(
                        out=ctx.dram["dbg_ch"][0:1, 9368:9880], in_=arow[0:1, 0:512]),
                        reads=[arowB], writes=[ctx.bufs["dbg_ch"]], dma=True)
                cacc = _sb(ctx, stk, "cacc", [128, T], F32)
                caccB = HB["cacc"]
                tka = _sb(ctx, stk, "tka", [128, 512], BF16)
                tkaB = HB["tka"]
                tsq = _sb(ctx, stk, "tsq", [128, 512], BF16)
                tsqB = HB["tsq"]
                scc = _sb(ctx, stk, "scc", [128, 4], F32)
                sccB = HB["scc"]
                tst = _sb(ctx, stk, "tst", [128, 512], BF16)
                tstB = HB["tst"]
                for (src, srcB, nsub, wcol, wcolB, kind) in (
                        (qp, qpB, 2, cwq, cwqB, "q"), (kp, kpB, 2, cwk, cwkB, "k"),
                        (vp, vpB, 4, cwv, cwvB, "v")):
                    for sub in range(nsub):
                        base = PW * sub
                        pr.op("vector", (lambda e, base=base, src=src, wcol=wcol, sub=sub:
                                         e.tensor_scalar(
                            cacc[0:128, 0:T], src[0:128, base:base + T],
                            wcol[0:128, 4 * sub:4 * sub + 1], None, ALU.mult)),
                            reads=[srcB, wcolB], writes=[caccB])
                        for i in range(1, 4):
                            pr.op("vector", (lambda e, base=base, i=i, src=src, wcol=wcol,
                                             sub=sub: e.scalar_tensor_tensor(
                                cacc[0:128, 0:T], src[0:128, base + i:base + i + T],
                                wcol[0:128, 4 * sub + i:4 * sub + i + 1],
                                cacc[0:128, 0:T], ALU.mult, ALU.add)),
                                reads=[srcB, wcolB, caccB], writes=[caccB])
                        pr.op("scalar", (lambda e, base=base, src=src: e.activation(
                            src[0:128, base + 3:base + 3 + T], cacc[0:128, 0:T], AF.Silu)),
                            reads=[caccB], writes=[srcB])
                    for j in range(NCH):
                        for sub in range(nsub):
                            ptp = pt[sub % 2]
                            ptpB = ptB[sub % 2]
                            pr.op("tensor", (lambda e, j=j, sub=sub, src=src, ptp=ptp:
                                             e.transpose(
                                ptp[0:128, 0:128],
                                src[0:128, PW * sub + 3 + 128 * j:PW * sub + 3 + 128 * (j + 1)],
                                c["c_id_bf"][0:128, 0:128])),
                                reads=[srcB, cB], writes=[ptpB])
                            pr.op("vector", (lambda e, sub=sub, ptp=ptp: e.tensor_copy(
                                tka[0:128, 128 * sub:128 * (sub + 1)], ptp[0:128, 0:128])),
                                reads=[ptpB], writes=[tkaB])
                        if kind == "v":
                            pr.op("sync", (lambda e, j=j: e.dma_start(
                                out=ctx.dram["v_tok_d"][128 * j:128 * (j + 1), 0:512],
                                in_=tka[0:128, 0:512])), reads=[tkaB],
                                writes=[ctx.bufs["v_tok_d"]], dma=True)
                            continue
                        pr.op("scalar", (lambda e: e.activation(
                            tsq[0:128, 0:256], tka[0:128, 0:256], AF.Square)),
                            reads=[tkaB], writes=[tsqB])
                        pr.op("vector", (lambda e: e.reduce_sum(
                            scc[0:128, 0:1], tsq[0:128, 0:256], AX.X)),
                            reads=[tsqB], writes=[sccB])
                        pr.op("scalar", (lambda e: e.activation(
                            scc[0:128, 1:2], scc[0:128, 0:1], AF.Ln, bias=1e-6)),
                            reads=[sccB, cB], writes=[sccB])
                        if kind == "q":
                            pr.op("scalar", (lambda e: e.activation(
                                scc[0:128, 2:3], scc[0:128, 1:2], AF.Exp, scale=-0.5,
                                bias=-2.772588722239781)),
                                reads=[sccB, cB], writes=[sccB])
                        else:
                            pr.op("scalar", (lambda e: e.activation(
                                scc[0:128, 2:3], scc[0:128, 1:2], AF.Exp, scale=-0.5)),
                                reads=[sccB], writes=[sccB])
                        pr.op("vector", (lambda e: e.tensor_scalar(
                            tka[0:128, 0:256], tka[0:128, 0:256], scc[0:128, 2:3], None,
                            ALU.mult)), reads=[tkaB, sccB], writes=[tkaB])
                        if kind == "k":
                            pr.op("sync", (lambda e, j=j: e.dma_start(
                                out=ctx.dram["kn_tok_d"][128 * j:128 * (j + 1), 0:256],
                                in_=tka[0:128, 0:256])), reads=[tkaB],
                                writes=[ctx.bufs["kn_tok_d"]], dma=True)
                        for sub in range(2):
                            ptp = pt[sub % 2]
                            ptpB = ptB[sub % 2]
                            pr.op("tensor", (lambda e, sub=sub, ptp=ptp: e.transpose(
                                ptp[0:128, 0:128], tka[0:128, 128 * sub:128 * (sub + 1)],
                                c["c_id_bf"][0:128, 0:128])),
                                reads=[tkaB, cB], writes=[ptpB])
                            pr.op("vector", (lambda e, sub=sub, ptp=ptp: e.tensor_copy(
                                tst[0:128, 128 * sub:128 * (sub + 1)], ptp[0:128, 0:128])),
                                reads=[ptpB], writes=[tstB])
                        dname = "qnT_d" if kind == "q" else "knT_d"
                        pr.op("sync", (lambda e, j=j, dname=dname: e.dma_start(
                            out=ctx.dram[dname][0:256, 128 * j:128 * (j + 1)].rearrange(
                                "(s p) t -> p s t", p=128),
                            in_=tst[0:128, 0:256].rearrange("p (s t) -> p s t", s=2))),
                            reads=[tstB], writes=[ctx.bufs[dname]], dma=True)
            pr.fence()
            with contextlib.ExitStack() as stk:
                if h == 0 and "ch" in ctx.dbg:
                    pr.op("sync", lambda e: e.dma_start(
                        out=ctx.dram["dbg_ch"][0:1, 7824:7824 + 512], in_=arow[0:1, 0:512]),
                        reads=[arowB], writes=[ctx.bufs["dbg_ch"]], dma=True)
                    pr.op("sync", lambda e: e.dma_start(
                        out=ctx.dram["dbg_ch"][0:1, 8336:8336 + 512], in_=brow[0:1, 0:512]),
                        reads=[browB], writes=[ctx.bufs["dbg_ch"]], dma=True)
                tmpr = _sb(ctx, stk, "tmpr", [1, T], F32)
                tmprB = HB["tmpr"]
                pr.op("scalar", lambda e: e.activation(tmpr[0:1, 0:T], arow[0:1, 0:T], AF.Abs),
                      reads=[arowB], writes=[tmprB])
                pr.op("scalar", lambda e: e.activation(tmpr[0:1, 0:T], tmpr[0:1, 0:T],
                                                       AF.Exp, scale=-1.0),
                      reads=[tmprB], writes=[tmprB])
                pr.op("scalar", lambda e: e.activation(tmpr[0:1, 0:T], tmpr[0:1, 0:T],
                                                       AF.Ln, bias=1.0),
                      reads=[tmprB, cB], writes=[tmprB])
                pr.op("vector", lambda e: e.tensor_relu(arow[0:1, 0:T], arow[0:1, 0:T]),
                      reads=[arowB], writes=[arowB])
                pr.op("vector", lambda e: e.tensor_add(arow[0:1, 0:T], arow[0:1, 0:T],
                                                       tmpr[0:1, 0:T]),
                      reads=[arowB, tmprB], writes=[arowB])
                pr.op("vector", (lambda e, h=h: e.tensor_scalar(
                    arow[0:1, 0:T], arow[0:1, 0:T], dtA[0:1, 4 + h:5 + h], None, ALU.mult)),
                    reads=[arowB, dtAB], writes=[arowB])
                pr.op("scalar", lambda e: e.activation(brow[0:1, 0:T], brow[0:1, 0:T],
                                                       AF.Sigmoid),
                      reads=[browB], writes=[browB])
                pr.op("vector", lambda e: e.tensor_scalar(nbrow[0:1, 0:T], brow[0:1, 0:T],
                                                          -1.0, None, ALU.mult),
                      reads=[browB], writes=[nbrowB])
                for j in range(NCH):
                    pr.op("vector", (lambda e, j=j: e.tensor_tensor_scan(
                        Grow[0:1, 128 * j:128 * (j + 1)], arow[0:1, 128 * j:128 * (j + 1)],
                        zrow[0:1, 0:128], 0.0, ALU.add, ALU.bypass)),
                        reads=[arowB, zrowB], writes=[GrowB])
            pr.fence()
            with contextlib.ExitStack() as stk:
                def par2(name, shape, dt):
                    ts = [_sb(ctx, stk, f"{name}{i}", shape, dt) for i in range(2)]
                    bs = [Buf(f"{name}{i}_{h}") for i in range(2)]
                    return ts, bs
                qnT_i, qnTB = par2("qnti", [128, 256], BF16)
                knT_i, knTB = par2("knti", [128, 256], BF16)
                knk_i, knkB = par2("knki", [128, 256], BF16)
                vt_i, vtB = par2("vti", [128, 512], BF16)
                gt_i, gtB = par2("gti", [128, 512], BF16)
                gcol2, gcol2B = par2("gcol2", [128, 2], F32)
                gcb, gcbB = par2("gcb", [128, 1], F32)
                sc, scB = par2("sc", [128, 8], F32)
                eaf, eafB = par2("eaf", [128, 128], F32)
                ewf, ewfB = par2("ewf", [128, 128], F32)
                ews, ewsB = par2("ews", [128, 128], F32)
                tchf, tchfB = par2("tchf", [128, 128], F32)
                nab, nabB = par2("nab", [128, 128], BF16)
                natb, natbB = par2("natb", [128, 128], BF16)
                rtA, rtAB = par2("rtA", [128, 128], BF16)
                rtBt, rtBB = par2("rtB", [128, 128], BF16)
                snA, snAB = par2("snA", [128, 128], BF16)
                snBt, snBB = par2("snB", [128, 128], BF16)
                stA, stAB = par2("stA", [128, 128], BF16)
                stBt, stBB = par2("stB", [128, 128], BF16)
                ut, utB = par2("ut", [128, 512], F32)
                ubf, ubfB = par2("ubf", [128, 512], BF16)
                dbf, dbfB = par2("dbf", [128, 512], BF16)
                o1t, o1tB = par2("o1t", [128, 512], F32)
                wtb, wtbB = par2("wtb", [128, 128], BF16)
                ot, otB = par2("ot", [128, 512], F32)
                osq, osqB = par2("osq", [128, 512], BF16)
                ofin, ofinB = par2("ofin", [128, 512], BF16)
                kdb, kdbB = par2("kdb", [128, 256], BF16)
                otst, otstB = par2("otst", [128, 512], BF16)
                pr.op("vector", lambda e: e.memset(S32[0:128, 0:1024], 0.0), writes=[S32B])
                pr.op("vector", lambda e: e.memset(Sbf[0:128, 0:1024], 0.0), writes=[SbfB])
                rt_fin = [None, None]
                neu_state = [None, None]

                def emit_pre(j):
                    jb = j % 2
                    bA, bB_, bC = (0, 1, 2) if jb == 0 else (3, 4, 5)
                    sl = slice(128 * j, 128 * (j + 1))
                    pr.op("sync", (lambda e, j=j, t=qnT_i[jb]: e.dma_start(
                        out=t[0:128, 0:256].rearrange("p (s t) -> p s t", s=2),
                        in_=ctx.dram["qnT_d"][0:256, 128 * j:128 * (j + 1)].rearrange(
                            "(s p) t -> p s t", p=128))),
                        reads=[ctx.bufs["qnT_d"]], writes=[qnTB[jb]], dma=True)
                    pr.op("sync", (lambda e, j=j, t=knT_i[jb]: e.dma_start(
                        out=t[0:128, 0:256].rearrange("p (s t) -> p s t", s=2),
                        in_=ctx.dram["knT_d"][0:256, 128 * j:128 * (j + 1)].rearrange(
                            "(s p) t -> p s t", p=128))),
                        reads=[ctx.bufs["knT_d"]], writes=[knTB[jb]], dma=True)
                    pr.op("sync", (lambda e, j=j, t=knk_i[jb]: e.dma_start(
                        out=t[0:128, 0:256],
                        in_=ctx.dram["kn_tok_d"][128 * j:128 * (j + 1), 0:256])),
                        reads=[ctx.bufs["kn_tok_d"]], writes=[knkB[jb]], dma=True)
                    pr.op("sync", (lambda e, j=j, t=vt_i[jb]: e.dma_start(
                        out=t[0:128, 0:512],
                        in_=ctx.dram["v_tok_d"][128 * j:128 * (j + 1), 0:512])),
                        reads=[ctx.bufs["v_tok_d"]], writes=[vtB[jb]], dma=True)
                    pr.op("sync", (lambda e, j=j, t=gt_i[jb]: e.dma_start(
                        out=t[0:128, 0:512],
                        in_=ctx.dram["gate_d"][128 * j:128 * (j + 1), 0:512])),
                        reads=[ctx.bufs["gate_d"]], writes=[gtB[jb]], dma=True)
                    pr.op("tensor", (lambda e, sl=sl, bA=bA: e.matmul(
                        pb[bA][0:128, 0:1], Grow[0:1, sl], c["c_ones_f"][0:1, 0:1],
                        start=True, stop=True)), reads=[GrowB, cB], writes=[pbB[bA]])
                    pr.op("tensor", (lambda e, sl=sl, bA=bA: e.matmul(
                        pb[bA][0:128, 1:2], brow[0:1, sl], c["c_ones_f"][0:1, 0:1],
                        start=True, stop=True)), reads=[browB, cB], writes=[pbB[bA]])
                    pr.op("vector", (lambda e, t=gcol2[jb], bA=bA: e.tensor_copy(
                        t[0:128, 0:2], pb[bA][0:128, 0:2])),
                        reads=[pbB[bA]], writes=[gcol2B[jb]])
                    pr.op("tensor", (lambda e, t=gcol2[jb], bA=bA: e.matmul(
                        pb[bA][0:128, 2:3], c["c_sel127"][0:128, 0:128], t[0:128, 0:1],
                        start=True, stop=True)), reads=[gcol2B[jb], cB], writes=[pbB[bA]])
                    pr.op("vector", (lambda e, t=gcb[jb], bA=bA: e.tensor_copy(
                        t[0:128, 0:1], pb[bA][0:128, 2:3])),
                        reads=[pbB[bA]], writes=[gcbB[jb]])
                    pr.op("vector", (lambda e, t=sc[jb], g2=gcol2[jb]: e.tensor_scalar(
                        t[0:128, 0:1], g2[0:128, 1:2], -1.0, None, ALU.mult)),
                        reads=[gcol2B[jb]], writes=[scB[jb]])
                    pr.op("scalar", (lambda e, t=sc[jb], g2=gcol2[jb]: e.activation(
                        t[0:128, 1:2], g2[0:128, 0:1], AF.Exp)),
                        reads=[gcol2B[jb]], writes=[scB[jb]])
                    pr.op("vector", (lambda e, t=sc[jb], gc=gcb[jb], g2=gcol2[jb]:
                                     e.tensor_scalar(
                        t[0:128, 4:5], gc[0:128, 0:1], g2[0:128, 0:1], None, ALU.subtract)),
                        reads=[gcbB[jb], gcol2B[jb]], writes=[scB[jb]])
                    pr.op("scalar", (lambda e, t=sc[jb]: e.activation(
                        t[0:128, 2:3], t[0:128, 4:5], AF.Exp)),
                        reads=[scB[jb]], writes=[scB[jb]])
                    pr.op("scalar", (lambda e, t=sc[jb], gc=gcb[jb]: e.activation(
                        t[0:128, 3:4], gc[0:128, 0:1], AF.Exp)),
                        reads=[gcbB[jb]], writes=[scB[jb]])
                    pr.op("tensor", (lambda e, sl=sl, bB_=bB_: e.matmul(
                        pb[bB_][0:128, 0:128], c["c_ones_f"][0:1, 0:128], Grow[0:1, sl],
                        start=True, stop=True)), reads=[GrowB, cB], writes=[pbB[bB_]])
                    for (dst, dstB, mname, esc) in ((eaf[jb], eafB[jb], "c_mA", -1.0),
                                                    (ewf[jb], ewfB[jb], "c_mW", 1.0),
                                                    (ews[jb], ewsB[jb], "c_mWs", 1.0)):
                        pr.op("vector", (lambda e, dst=dst, mname=mname, g2=gcol2[jb],
                                         bB_=bB_: e.scalar_tensor_tensor(
                            dst[0:128, 0:128], pb[bB_][0:128, 0:128], g2[0:128, 0:1],
                            c[mname][0:128, 0:128], ALU.subtract, ALU.add)),
                            reads=[pbB[bB_], gcol2B[jb], cB], writes=[dstB])
                        pr.op("scalar", (lambda e, dst=dst, esc=esc: e.activation(
                            dst[0:128, 0:128], dst[0:128, 0:128], AF.Exp, scale=esc)),
                            reads=[dstB], writes=[dstB])
                    for sb_ in range(2):
                        pr.op("tensor", (lambda e, sb_=sb_, t=knT_i[jb], bC=bC: e.matmul(
                            pb[bC][0:128, 0:128], t[0:128, 128 * sb_:128 * (sb_ + 1)],
                            t[0:128, 128 * sb_:128 * (sb_ + 1)],
                            start=(sb_ == 0), stop=(sb_ == 1))),
                            reads=[knTB[jb]], writes=[pbB[bC]])
                    pr.op("vector", (lambda e, t=tchf[jb], ea=eaf[jb], bC=bC: e.tensor_mul(
                        t[0:128, 0:128], ea[0:128, 0:128], pb[bC][0:128, 0:128])),
                        reads=[eafB[jb], pbB[bC]], writes=[tchfB[jb]])
                    pr.op("vector", (lambda e, t=nab[jb], tc_=tchf[jb], s_=sc[jb]:
                                     e.tensor_scalar(
                        t[0:128, 0:128], tc_[0:128, 0:128], s_[0:128, 0:1], None, ALU.mult)),
                        reads=[tchfB[jb], scB[jb]], writes=[nabB[jb]])
                    pr.op("tensor", (lambda e, sl=sl, bB_=bB_: e.matmul(
                        pb[bB_][0:128, 0:128], c["c_ones_f"][0:1, 0:128], nbrow[0:1, sl],
                        start=True, stop=True)), reads=[nbrowB, cB], writes=[pbB[bB_]])
                    pr.op("vector", (lambda e, t=tchf[jb], ew=ews[jb], bC=bC: e.tensor_mul(
                        t[0:128, 0:128], ew[0:128, 0:128], pb[bC][0:128, 0:128])),
                        reads=[ewsB[jb], pbB[bC], tchfB[jb]], writes=[tchfB[jb]])
                    pr.op("vector", (lambda e, t=natb[jb], tc_=tchf[jb], bB_=bB_: e.tensor_mul(
                        t[0:128, 0:128], tc_[0:128, 0:128], pb[bB_][0:128, 0:128])),
                        reads=[tchfB[jb], pbB[bB_]], writes=[natbB[jb]])
                    pr.op("vector", (lambda e, t=rtA[jb], n_=natb[jb]: e.tensor_add(
                        t[0:128, 0:128], n_[0:128, 0:128], c["c_id_bf"][0:128, 0:128])),
                        reads=[natbB[jb], cB], writes=[rtAB[jb]])
                    neu_state[jb] = [nab[jb], nabB[jb], natb[jb], natbB[jb],
                                     rtA[jb], rtAB[jb]]

                def emit_neumann_iter(j, it):
                    jb = j % 2
                    bA, bB_, bC = (0, 1, 2) if jb == 0 else (3, 4, 5)
                    sn_cur, snB_cur, st_cur, stB_cur, rt_cur, rtB_cur = neu_state[jb]
                    rts = [(rtBt[jb], rtBB[jb]), (rtA[jb], rtAB[jb])]
                    sns = [(snA[jb], snAB[jb]), (snBt[jb], snBB[jb])]
                    sts = [(stA[jb], stAB[jb]), (stBt[jb], stBB[jb])]
                    bq = bA if it % 2 == 0 else bC
                    bp = bC if it % 2 == 0 else bA
                    sn_nxt, snB_nxt = sns[it % 2]
                    st_nxt, stB_nxt = sts[it % 2]
                    pr.op("tensor", (lambda e, a=st_cur, b2=sn_cur, bq=bq: e.matmul(
                        pb[bq][0:128, 0:128], a[0:128, 0:128], b2[0:128, 0:128],
                        start=True, stop=True)),
                        reads=[stB_cur, snB_cur], writes=[pbB[bq]])
                    pr.op("scalar", (lambda e, o=sn_nxt, bq=bq: e.copy(
                        o[0:128, 0:128], pb[bq][0:128, 0:128])),
                        reads=[pbB[bq]], writes=[snB_nxt])
                    pr.op("tensor", (lambda e, a=sn_cur, b2=st_cur, bq=bq: e.matmul(
                        pb[bq][0:128, 0:128], a[0:128, 0:128], b2[0:128, 0:128],
                        start=True, stop=True)),
                        reads=[snB_cur, stB_cur], writes=[pbB[bq]])
                    pr.op("scalar", (lambda e, o=st_nxt, bq=bq: e.copy(
                        o[0:128, 0:128], pb[bq][0:128, 0:128])),
                        reads=[pbB[bq]], writes=[stB_nxt])
                    pr.op("tensor", (lambda e, b2=rt_cur, bp=bp: e.matmul(
                        pb[bp][0:128, 0:128], c["c_id_bf"][0:128, 0:128],
                        b2[0:128, 0:128], start=True, stop=False)),
                        reads=[cB, rtB_cur], writes=[pbB[bp]])
                    pr.op("tensor", (lambda e, a=sn_nxt, b2=rt_cur, bp=bp: e.matmul(
                        pb[bp][0:128, 0:128], a[0:128, 0:128], b2[0:128, 0:128],
                        start=False, stop=True)),
                        reads=[snB_nxt, rtB_cur], writes=[pbB[bp]])
                    rt_nxt, rtB_nxt = rts[it % 2]
                    pr.op("scalar", (lambda e, o=rt_nxt, bp=bp: e.copy(
                        o[0:128, 0:128], pb[bp][0:128, 0:128])),
                        reads=[pbB[bp]], writes=[rtB_nxt])
                    neu_state[jb] = [sn_nxt, snB_nxt, st_nxt, stB_nxt, rt_nxt, rtB_nxt]
                    if it == 5:
                        rt_fin[jb] = (rt_nxt, rtB_nxt)

                def emit_post(j):
                    jb = j % 2
                    bA, bB_, bC = (0, 1, 2) if jb == 0 else (3, 4, 5)
                    sl = slice(128 * j, 128 * (j + 1))
                    rt_cur, rtB_cur = rt_fin[jb]
                    for sb_ in range(2):
                        pr.op("tensor", (lambda e, sb_=sb_, t=knT_i[jb], bB_=bB_: e.matmul(
                            pb[bB_][0:128, 0:512], t[0:128, 128 * sb_:128 * (sb_ + 1)],
                            Sbf[0:128, 512 * sb_:512 * (sb_ + 1)],
                            start=(sb_ == 0), stop=(sb_ == 1))),
                            reads=[knTB[jb], SbfB], writes=[pbB[bB_]])
                    pr.op("vector", (lambda e, t=ut[jb], v=vt_i[jb], s_=sc[jb], bB_=bB_:
                                     e.scalar_tensor_tensor(
                        t[0:128, 0:512], pb[bB_][0:128, 0:512], s_[0:128, 1:2],
                        v[0:128, 0:512], ALU.mult, ALU.subtract)),
                        reads=[pbB[bB_], scB[jb], vtB[jb]], writes=[utB[jb]])
                    pr.op("vector", (lambda e, t=ubf[jb], u=ut[jb], s_=sc[jb]: e.tensor_scalar(
                        t[0:128, 0:512], u[0:128, 0:512], s_[0:128, 0:1], None, ALU.mult)),
                        reads=[utB[jb], scB[jb]], writes=[ubfB[jb]])
                    pr.op("tensor", (lambda e, a=rt_cur, u=ubf[jb], bC=bC: e.matmul(
                        pb[bC][0:128, 0:512], a[0:128, 0:128], u[0:128, 0:512],
                        start=True, stop=True)), reads=[rtB_cur, ubfB[jb]], writes=[pbB[bC]])
                    pr.op("vector", (lambda e, t=dbf[jb], bC=bC: e.tensor_copy(
                        t[0:128, 0:512], pb[bC][0:128, 0:512])),
                        reads=[pbB[bC]], writes=[dbfB[jb]])
                    for sb_ in range(2):
                        pr.op("tensor", (lambda e, sb_=sb_, t=qnT_i[jb], bB_=bB_: e.matmul(
                            pb[bB_][0:128, 0:512], t[0:128, 128 * sb_:128 * (sb_ + 1)],
                            Sbf[0:128, 512 * sb_:512 * (sb_ + 1)],
                            start=(sb_ == 0), stop=(sb_ == 1))),
                            reads=[qnTB[jb], SbfB], writes=[pbB[bB_]])
                    pr.op("vector", (lambda e, t=o1t[jb], s_=sc[jb], bB_=bB_: e.tensor_scalar(
                        t[0:128, 0:512], pb[bB_][0:128, 0:512], s_[0:128, 1:2], None,
                        ALU.mult)), reads=[pbB[bB_], scB[jb]], writes=[o1tB[jb]])
                    for sb_ in range(2):
                        pr.op("tensor", (lambda e, sb_=sb_, tk=knT_i[jb], tq=qnT_i[jb],
                                         bA=bA: e.matmul(
                            pb[bA][0:128, 0:128], tk[0:128, 128 * sb_:128 * (sb_ + 1)],
                            tq[0:128, 128 * sb_:128 * (sb_ + 1)],
                            start=(sb_ == 0), stop=(sb_ == 1))),
                            reads=[knTB[jb], qnTB[jb]], writes=[pbB[bA]])
                    pr.op("vector", (lambda e, t=wtb[jb], ew=ewf[jb], bA=bA: e.tensor_mul(
                        t[0:128, 0:128], ew[0:128, 0:128], pb[bA][0:128, 0:128])),
                        reads=[ewfB[jb], pbB[bA]], writes=[wtbB[jb]])
                    pr.op("tensor", (lambda e, w=wtb[jb], d=dbf[jb], bC=bC: e.matmul(
                        pb[bC][0:128, 0:512], w[0:128, 0:128], d[0:128, 0:512],
                        start=True, stop=True)), reads=[wtbB[jb], dbfB[jb]], writes=[pbB[bC]])
                    pr.op("vector", (lambda e, t=ot[jb], o1=o1t[jb], bC=bC: e.tensor_add(
                        t[0:128, 0:512], o1[0:128, 0:512], pb[bC][0:128, 0:512])),
                        reads=[o1tB[jb], pbB[bC]], writes=[otB[jb]])
                    pr.op("scalar", (lambda e, t=osq[jb], o=ot[jb]: e.activation(
                        t[0:128, 0:512], o[0:128, 0:512], AF.Square)),
                        reads=[otB[jb]], writes=[osqB[jb]])
                    pr.op("vector", (lambda e, t=sc[jb], q=osq[jb]: e.reduce_sum(
                        t[0:128, 5:6], q[0:128, 0:512], AX.X)),
                        reads=[osqB[jb], scB[jb]], writes=[scB[jb]])
                    pr.op("scalar", (lambda e, t=sc[jb]: e.activation(
                        t[0:128, 6:7], t[0:128, 5:6], AF.Ln, scale=1.0 / 512, bias=1e-6)),
                        reads=[scB[jb], cB], writes=[scB[jb]])
                    pr.op("scalar", (lambda e, t=sc[jb]: e.activation(
                        t[0:128, 7:8], t[0:128, 6:7], AF.Exp, scale=-0.5)),
                        reads=[scB[jb]], writes=[scB[jb]])
                    pr.op("vector", (lambda e, t=ot[jb], s_=sc[jb]: e.tensor_scalar(
                        t[0:128, 0:512], t[0:128, 0:512], s_[0:128, 7:8], None, ALU.mult)),
                        reads=[otB[jb], scB[jb]], writes=[otB[jb]])
                    pr.op("vector", (lambda e, t=ofin[jb], o=ot[jb], g=gt_i[jb]: e.tensor_mul(
                        t[0:128, 0:512], o[0:128, 0:512], g[0:128, 0:512])),
                        reads=[otB[jb], gtB[jb]], writes=[ofinB[jb]])
                    pr.op("vector", (lambda e, t=kdb[jb], k_=knk_i[jb], s_=sc[jb]:
                                     e.tensor_scalar(
                        t[0:128, 0:256], k_[0:128, 0:256], s_[0:128, 2:3], None, ALU.mult)),
                        reads=[knkB[jb], scB[jb]], writes=[kdbB[jb]])
                    for sb_ in range(2):
                        bx = bA if sb_ == 0 else bB_
                        pr.op("tensor", (lambda e, sb_=sb_, k_=kdb[jb], d=dbf[jb], bx=bx:
                                         e.matmul(
                            pb[bx][0:128, 0:512], k_[0:128, 128 * sb_:128 * (sb_ + 1)],
                            d[0:128, 0:512], start=True, stop=True)),
                            reads=[kdbB[jb], dbfB[jb]], writes=[pbB[bx]])
                        pr.op("vector", (lambda e, sb_=sb_, s_=sc[jb], bx=bx:
                                         e.scalar_tensor_tensor(
                            S32[0:128, 512 * sb_:512 * (sb_ + 1)],
                            S32[0:128, 512 * sb_:512 * (sb_ + 1)], s_[0:128, 3:4],
                            pb[bx][0:128, 0:512], ALU.mult, ALU.add)),
                            reads=[pbB[bx], scB[jb], S32B], writes=[S32B])
                        pr.op("vector", (lambda e, sb_=sb_: e.tensor_copy(
                            Sbf[0:128, 512 * sb_:512 * (sb_ + 1)],
                            S32[0:128, 512 * sb_:512 * (sb_ + 1)])),
                            reads=[S32B], writes=[SbfB])
                    for m in range(4):
                        ptp = pt[m % 2]
                        ptpB = ptB[m % 2]
                        pr.op("tensor", (lambda e, m=m, o=ofin[jb], ptp=ptp: e.transpose(
                            ptp[0:128, 0:128], o[0:128, 128 * m:128 * (m + 1)],
                            c["c_id_bf"][0:128, 0:128])), reads=[ofinB[jb], cB], writes=[ptpB])
                        pr.op("vector", (lambda e, m=m, t=otst[jb], ptp=ptp: e.tensor_copy(
                            t[0:128, 128 * m:128 * (m + 1)], ptp[0:128, 0:128])),
                            reads=[ptpB], writes=[otstB[jb]])
                    pr.op("sync", (lambda e, j=j, h=h, t=otst[jb]: e.dma_start(
                        out=ctx.dram["oTs"][512 * h:512 * (h + 1),
                                            128 * j:128 * (j + 1)].rearrange(
                            "(s p) t -> p s t", p=128),
                        in_=t[0:128, 0:512].rearrange("p (s t) -> p s t", s=4))),
                        reads=[otstB[jb]], writes=[ctx.bufs["oTs"]], dma=True)

                emit_pre(0)
                for it in range(6):
                    emit_neumann_iter(0, it)
                for j in range(NCH):
                    if j + 1 < NCH:
                        emit_pre(j + 1)
                        for it in range(6):
                            emit_neumann_iter(j + 1, it)
                    emit_post(j)
            pr.fence()
        pr.fence()

    st_gemm_ln(ctx, src="oTs", wname="L1_attn_Wo", K=2048, res=["r2_32"],
               out32=None, outbf="h2bf", zout="x1_32",
               affine=("L1_attn_ln2_g", "L1_attn_ln2_b"),
               wscale_input=("L1_attn_o_norm_w", 4))
    pr.fence()
    st_gemm_ln(ctx, src="h2bf", wname="L1_attn_W2", K=1024, res=["x1_32", "r2_32"],
               out32="r3_32", outbf="r3_bf", act=("gelu", "L1_attn_b2"))
    pr.fence()
def st_wtrans(ctx, wname, odim, idim, dst):
    nc, pr, io, c = ctx.nc, ctx.pr, ctx.io, ctx.c
    pt, ptB = ctx.pt, ctx.ptB
    cB = ctx.cB
    nki = idim // 128
    with contextlib.ExitStack() as stk:
        wld = [_sb(ctx, stk, f"wld{i}", [128, idim], BF16) for i in range(2)]
        wldB = [Buf(f"wld{i}") for i in range(2)]
        wst = [_sb(ctx, stk, f"wst{i}", [128, idim], BF16) for i in range(2)]
        wstB = [Buf(f"wst{i}") for i in range(2)]
        for mo in range(odim // 128):
            b = mo % 2
            pr.op("gpsimd", (lambda e, mo=mo, t=wld[b]: e.dma_start(
                out=t[0:128, 0:idim],
                in_=io[wname][128 * mo:128 * (mo + 1), :])),
                writes=[wldB[b]], dma=True)
            for ki in range(nki):
                p = pt[ki % 2]
                pB = ptB[ki % 2]
                pr.op("tensor", (lambda e, ki=ki, t=wld[b], p=p: e.transpose(
                    p[0:128, 0:128], t[0:128, 128 * ki:128 * (ki + 1)],
                    c["c_id_bf"][0:128, 0:128])),
                    reads=[wldB[b], cB], writes=[pB])
                pr.op("vector", (lambda e, ki=ki, p=p, t=wst[b]: e.tensor_copy(
                    t[0:128, 128 * ki:128 * (ki + 1)], p[0:128, 0:128])),
                    reads=[pB], writes=[wstB[b]])
            pr.op("sync", (lambda e, mo=mo, t=wst[b]: e.dma_start(
                out=ctx.dram[dst][0:idim, 128 * mo:128 * (mo + 1)].rearrange(
                    "(k p) m -> p k m", p=128),
                in_=t[0:128, 0:idim].rearrange("p (k m) -> p k m", k=nki))),
                reads=[wstB[b]], writes=[ctx.bufs[dst]], dma=True)
    pr.fence()


def st_big_gemm(ctx, src, wbf, scale_name, dst, kdim, odim, act):
    nc, pr, io, c = ctx.nc, ctx.pr, ctx.io, ctx.c
    pb, pbB = ctx.pb, ctx.pbB
    cB = ctx.cB
    nk = kdim // 128
    nm = odim // 128
    with contextlib.ExitStack() as stk:
        scol, scolB = _col_loads(ctx, pr, stk, "scol",
                                 lambda i: io[scale_name][128 * i:128 * (i + 1)][:, None],
                                 nm, "bscol")
        wg = [_sb(ctx, stk, f"bgw{i}", [128, nk * 4 * 128], BF16) for i in range(2)]
        wgB = [Buf(f"bgw{i}") for i in range(2)]
        rin = [_sb(ctx, stk, f"bgr{i}", [128, nk * 512], BF16) for i in range(2)]
        rinB = [Buf(f"bgr{i}") for i in range(2)]
        yst = [_sb(ctx, stk, f"bgy{i}", [128, 512], BF16) for i in range(2)]
        ystB = [Buf(f"bgy{i}") for i in range(2)]
        for mg in range(nm // 4):
            b = mg % 2
            for k in range(nk):
                pr.op("sync", (lambda e, mg=mg, k=k, t=wg[b]: e.dma_start(
                    out=t[0:128, 512 * k:512 * (k + 1)],
                    in_=ctx.dram[wbf][512 * mg:512 * (mg + 1), 128 * k:128 * (k + 1)],
                    transpose=True)),
                    reads=[ctx.bufs[wbf]], writes=[wgB[b]], dma=True)
            for tc in range(8):
                rb = tc % 2
                pr.op("sync", (lambda e, tc=tc, t=rin[rb]: e.dma_start(
                    out=t[0:128, 0:nk * 512].rearrange("p (k t) -> p k t", k=nk),
                    in_=ctx.dram[src][0:kdim, 512 * tc:512 * (tc + 1)].rearrange(
                        "(k p) t -> p k t", p=128))),
                    reads=[ctx.bufs[src]], writes=[rinB[rb]], dma=True)
                for mi in range(4):
                    pp = pb[mi]
                    ppB = pbB[mi]
                    for k in range(nk):
                        pr.op("tensor", (lambda e, k=k, mi=mi, w=wg[b], t=rin[rb], pp=pp:
                                         e.matmul(
                            pp[0:128, 0:512], w[0:128, (k * 4 + mi) * 128:(k * 4 + mi) * 128 + 128],
                            t[0:128, 512 * k:512 * (k + 1)],
                            start=(k == 0), stop=(k == nk - 1))),
                            reads=[wgB[b], rinB[rb]], writes=[ppB])
                    yb = (tc * 4 + mi) % 2
                    m = 4 * mg + mi
                    fn = AF.Gelu if act == "gelu" else AF.Silu
                    pr.op("scalar", (lambda e, m=m, y=yst[yb], pp=pp, fn=fn: e.activation(
                        y[0:128, 0:512], pp[0:128, 0:512], fn,
                        scale=scol[0:128, m:m + 1])),
                        reads=[ppB, scolB], writes=[ystB[yb]])
                    pr.op("sync", (lambda e, m=m, tc=tc, y=yst[yb]: e.dma_start(
                        out=ctx.dram[dst][128 * m:128 * (m + 1), 512 * tc:512 * (tc + 1)],
                        in_=y[0:128, 0:512])), reads=[ystB[yb]], writes=[ctx.bufs[dst]],
                        dma=True)
    pr.fence()


def st_l1ffn(ctx):
    nc, pr, io, c = ctx.nc, ctx.pr, ctx.io, ctx.c
    pb, pbB = ctx.pb, ctx.pbB
    pt, ptB = ctx.pt, ctx.ptB
    cB = ctx.cB
    with contextlib.ExitStack() as stk:
        s1c, s1B = _col_loads(ctx, pr, stk, "s1c",
                              lambda i: io["L1_ffn_w1_s"][128 * i:128 * (i + 1)][:, None],
                              32, "fs1")
        s2c, s2B = _col_loads(ctx, pr, stk, "s2c",
                              lambda i: io["L1_ffn_w2_s"][128 * i:128 * (i + 1)][:, None],
                              32, "fs2")
        w1g = [_sb(ctx, stk, f"f1w{i}", [128, 8 * 4 * 128], BF16) for i in range(2)]
        w1gB = [Buf(f"f1w{i}") for i in range(2)]
        w2g = [_sb(ctx, stk, f"f2w{i}", [128, 8 * 4 * 128], BF16) for i in range(2)]
        w2gB = [Buf(f"f2w{i}") for i in range(2)]
        rin = [_sb(ctx, stk, f"f1r{i}", [128, 8 * 512], BF16) for i in range(2)]
        rinB = [Buf(f"f1r{i}") for i in range(2)]
        t1b = [_sb(ctx, stk, f"f1t{i}", [128, 512], F32) for i in range(2)]
        t1B = [Buf(f"f1t{i}") for i in range(2)]
        t2b = [_sb(ctx, stk, f"f2t{i}", [128, 512], F32) for i in range(2)]
        t2B = [Buf(f"f2t{i}") for i in range(2)]
        yst = [_sb(ctx, stk, f"f1y{i}", [128, 512], BF16) for i in range(2)]
        ystB = [Buf(f"f1y{i}") for i in range(2)]
        for mg in range(8):
            b = mg % 2
            for (wn, t, tB) in (("wbf_w1", w1g[b], w1gB[b]), ("wbf_w2", w2g[b], w2gB[b])):
                for k in range(8):
                    pr.op("sync", (lambda e, mg=mg, k=k, wn=wn, t=t: e.dma_start(
                        out=t[0:128, 512 * k:512 * (k + 1)],
                        in_=ctx.dram[wn][512 * mg:512 * (mg + 1), 128 * k:128 * (k + 1)],
                        transpose=True)),
                        reads=[ctx.bufs[wn]], writes=[tB], dma=True)
            for tc in range(8):
                rb = tc % 2
                pr.op("sync", (lambda e, tc=tc, t=rin[rb]: e.dma_start(
                    out=t[0:128, 0:8 * 512].rearrange("p (k t) -> p k t", k=8),
                    in_=ctx.dram["r3_bf"][0:1024, 512 * tc:512 * (tc + 1)].rearrange(
                        "(k p) t -> p k t", p=128))),
                    reads=[ctx.bufs["r3_bf"]], writes=[rinB[rb]], dma=True)
                for mi in range(4):
                    pa = pb[0 + 2 * (mi % 2)]
                    paB = pbB[0 + 2 * (mi % 2)]
                    pc = pb[1 + 2 * (mi % 2)]
                    pcB = pbB[1 + 2 * (mi % 2)]
                    for k in range(8):
                        pr.op("tensor", (lambda e, k=k, mi=mi, w=w1g[b], t=rin[rb], pa=pa:
                                         e.matmul(
                            pa[0:128, 0:512], w[0:128, (k * 4 + mi) * 128:(k * 4 + mi) * 128 + 128],
                            t[0:128, 512 * k:512 * (k + 1)],
                            start=(k == 0), stop=(k == 7))),
                            reads=[w1gB[b], rinB[rb]], writes=[paB])
                    for k in range(8):
                        pr.op("tensor", (lambda e, k=k, mi=mi, w=w2g[b], t=rin[rb], pc=pc:
                                         e.matmul(
                            pc[0:128, 0:512], w[0:128, (k * 4 + mi) * 128:(k * 4 + mi) * 128 + 128],
                            t[0:128, 512 * k:512 * (k + 1)],
                            start=(k == 0), stop=(k == 7))),
                            reads=[w2gB[b], rinB[rb]], writes=[pcB])
                    m = 4 * mg + mi
                    eb = mi % 2
                    pr.op("scalar", (lambda e, m=m, t=t1b[eb], pa=pa: e.activation(
                        t[0:128, 0:512], pa[0:128, 0:512], AF.Silu,
                        scale=s1c[0:128, m:m + 1])), reads=[paB, s1B], writes=[t1B[eb]])
                    pr.op("vector", (lambda e, m=m, t=t2b[eb], pc=pc: e.tensor_scalar(
                        t[0:128, 0:512], pc[0:128, 0:512], s2c[0:128, m:m + 1], None,
                        ALU.mult)), reads=[pcB, s2B], writes=[t2B[eb]])
                    pr.op("vector", (lambda e, t1=t1b[eb], t2=t2b[eb], y=yst[eb]: e.tensor_mul(
                        y[0:128, 0:512], t1[0:128, 0:512], t2[0:128, 0:512])),
                        reads=[t1B[eb], t2B[eb]], writes=[ystB[eb]])
                    pr.op("sync", (lambda e, m=m, tc=tc, y=yst[eb]: e.dma_start(
                        out=ctx.dram["h1T"][128 * m:128 * (m + 1), 512 * tc:512 * (tc + 1)],
                        in_=y[0:128, 0:512])), reads=[ystB[eb]], writes=[ctx.bufs["h1T"]],
                        dma=True)
    pr.fence()
    st_big_gemm(ctx, "h1T", "wbf_h1", "L1_ffn_h1_s", "h2T", 4096, 4096, "gelu")
    st_big_gemm(ctx, "h2T", "wbf_h2", "L1_ffn_h2_s", "h3T", 4096, 4096, "gelu")
    st_gemm_ln(ctx, src="h3T", wname="wbf_w3", K=4096, res=["r3_32"],
               out32="r4_32", outbf="r4_bf", out_scale_name="L1_ffn_w3_s",
               w_from_dram=True)
    pr.fence()
    with contextlib.ExitStack() as stk:
        rin = [_sb(ctx, stk, f"o4r{i}", [128, 1024], F32) for i in range(2)]
        rinB = [Buf(f"o4r{i}") for i in range(2)]
        ost = [_sb(ctx, stk, f"o4s{i}", [128, 1024], F32) for i in range(2)]
        ostB = [Buf(f"o4s{i}") for i in range(2)]
        for i in range(32):
            b = i % 2
            pr.op("sync", (lambda e, i=i, t=rin[b]: e.dma_start(
                out=t[0:128, 0:8 * 128].rearrange("p (k t) -> p k t", k=8),
                in_=ctx.dram["r4_32"][0:1024, 128 * i:128 * (i + 1)].rearrange(
                    "(k p) t -> p k t", p=128))),
                reads=[ctx.bufs["r4_32"]], writes=[rinB[b]], dma=True)
            for m in range(8):
                pz = pb[m % 2]
                pzB = pbB[m % 2]
                pr.op("tensor", (lambda e, m=m, t=rin[b], pz=pz: e.transpose(
                    pz[0:128, 0:128], t[0:128, 128 * m:128 * (m + 1)],
                    c["c_id_f"][0:128, 0:128])),
                    reads=[rinB[b], cB], writes=[pzB])
                pr.op("vector", (lambda e, m=m, t=ost[b], pz=pz: e.tensor_copy(
                    t[0:128, 128 * m:128 * (m + 1)], pz[0:128, 0:128])),
                    reads=[pzB], writes=[ostB[b]])
            pr.op("sync", (lambda e, i=i, t=ost[b]: e.dma_start(
                out=ctx.out[128 * i:128 * (i + 1), 0:1024], in_=t[0:128, 0:1024])),
                reads=[ostB[b]], writes=[ctx.bufs["out"]], dma=True)
    pr.fence()



def kernel(**inputs):
    x = np.ascontiguousarray(inputs["L0_attn_x"])
    cns = host_constants()
    nc, ctx = build_program()
    in_maps = []
    for core in range(NC_):
        m = {"x": np.ascontiguousarray(x[core])}
        for k, v in cns.items():
            m[k] = v
        for k, v in inputs.items():
            if k != "L0_attn_x":
                m[k] = np.ascontiguousarray(v)
        in_maps.append(m)
    res = run_bass_kernel_spmd(nc, in_maps, core_ids=list(range(NC_)))
    return np.stack([res.results[i]["out"] for i in range(NC_)], axis=0)


# version 04
# speedup vs baseline: 1.0690x; 1.0690x over previous
import contextlib
import numpy as np
import ml_dtypes
import concourse.bass as bass
import concourse.mybir as mybir
from concourse.bass_utils import run_bass_kernel_spmd

F32 = mybir.dt.float32
BF16 = mybir.dt.bfloat16
AF = mybir.ActivationFunctionType
ALU = mybir.AluOpType
AX = mybir.AxisListType

CONST_VALS = [1e-5, 1e-6, 1.0, -2.772588722239781]

T = 4096
D = 1024
NC_ = 8


ENGS = ("tensor", "vector", "scalar", "gpsimd", "sync")


class Buf:
    __slots__ = ("writers", "readers", "name", "fam", "excl")

    def __init__(self, name="", excl=False):
        self.writers = []
        self.readers = []
        self.name = name
        self.fam = None
        self.excl = excl


class Op:
    __slots__ = ("eng", "build", "deps", "dma", "needs_inc", "inc_cum", "gidx",
                 "waits", "fam")

    def __init__(self):
        self.deps = []
        self.needs_inc = False
        self.waits = None
        self.fam = None


class Pr:

    def __init__(self, nc):
        self.nc = nc
        self.ops = {e: [] for e in ENGS}
        self.gidx = 0
        self.pending_fence = {e: None for e in ENGS}
        self.nfam = 0
        self.fence_gidxs = []

    def op(self, eng, build, reads=(), writes=(), dma=False):
        o = Op()
        o.eng = eng
        o.build = build
        o.dma = dma
        if dma:
            o.needs_inc = True
            assert len(writes) >= 1, "dma op needs a destination buffer"
            b0 = writes[0]
            if b0.fam is None:
                b0.fam = self.nfam
                self.nfam += 1
            o.fam = b0.fam
        def keep(d):
            return d.dma or d.eng != eng or eng != "tensor"
        if self.pending_fence[eng]:
            for d in self.pending_fence[eng]:
                if d is not o and keep(d):
                    o.deps.append(d)
            self.pending_fence[eng] = None
        for b in reads:
            for w in b.writers:
                if keep(w):
                    o.deps.append(w)
            if b.excl:
                for r in b.readers:
                    if keep(r):
                        o.deps.append(r)
            b.readers.append(o)
        for b in writes:
            for w in b.writers:
                if keep(w):
                    o.deps.append(w)
            for r in b.readers:
                if r is not o and keep(r):
                    o.deps.append(r)
            b.readers.clear()
            b.writers.append(o)
        for d in o.deps:
            d.needs_inc = True
        o.gidx = self.gidx
        self.gidx += 1
        self.ops[eng].append(o)
        return o

    def fence(self):
        lasts = [self.ops[e][-1] for e in ENGS if self.ops[e]]
        for e in ENGS:
            self.pending_fence[e] = lasts
        self.fence_gidxs.append(self.gidx)

    @staticmethod
    def _skey(o):
        return ("f", o.fam) if o.dma else ("e", o.eng)

    def finalize(self, st):
        ecnt = {e: 0 for e in ENGS}
        fcnt = {}
        for o in sorted((o for e in ENGS for o in self.ops[e]), key=lambda o: o.gidx):
            if o.dma:
                fcnt[o.fam] = fcnt.get(o.fam, 0) + 16
                o.inc_cum = fcnt[o.fam]
            elif o.needs_inc:
                ecnt[o.eng] += 1
                o.inc_cum = ecnt[o.eng]
            else:
                o.inc_cum = ecnt[o.eng]
        allops = sorted((o for e in ENGS for o in self.ops[e]), key=lambda o: o.gidx)
        cur = {e: {} for e in ENGS}
        opclock = {}
        for o in allops:
            c = cur[o.eng]
            wd = {}
            for d in o.deps:
                skey = self._skey(d)
                v = d.inc_cum
                if c.get(skey, 0) < v:
                    wd[skey] = max(wd.get(skey, 0), v)
                    dc = opclock.get(d.gidx)
                    if dc:
                        for s2, v2 in dc.items():
                            if v2 > c.get(s2, 0):
                                c[s2] = v2
                    c[skey] = max(c.get(skey, 0), v)
            o.waits = list(wd.items())
            if o.needs_inc:
                mykey = self._skey(o)
                snap = dict(c)
                snap[mykey] = max(snap.get(mykey, 0), o.inc_cum)
                opclock[o.gidx] = snap
        import bisect
        fam_first = {}
        fam_last = {}
        for o in allops:
            if o.dma:
                fam_first.setdefault(o.fam, o.gidx)
                fam_last[o.fam] = max(fam_last.get(o.fam, 0), o.gidx)
            for d in o.deps:
                if d.dma:
                    fam_last[d.fam] = max(fam_last.get(d.fam, 0), o.gidx)
        sems = {}
        for e in ENGS:
            sems[("e", e)] = st.enter_context(self.nc.semaphore(f"s_{e}"))
        phys = []
        fam_phys = {}
        fam_base = {}
        fams_sorted = sorted(fam_first, key=lambda f: fam_first[f])
        fam_total = {}
        for o in allops:
            if o.dma:
                fam_total[o.fam] = o.inc_cum
        for f in fams_sorted:
            fg = fam_first[f]
            i = bisect.bisect_right(self.fence_gidxs, fg) - 1
            fence_g = self.fence_gidxs[i] if i >= 0 else -1
            slot = None
            for ph in phys:
                if ph[1] < fence_g:
                    slot = ph
                    break
            if slot is None:
                slot = [st.enter_context(self.nc.semaphore(f"sf{len(phys)}")), -1, 0]
                phys.append(slot)
            fam_phys[f] = slot
            fam_base[f] = slot[2]
            slot[1] = fam_last[f]
            slot[2] += fam_total[f]
        self.fam_sem = {f: fam_phys[f][0] for f in fam_phys}
        self.fam_base = fam_base
        self.nphys = len(phys)
        self.sems = sems
        nwaits = sum(len(o.waits) for o in allops)
        nincs = sum(1 for o in allops if o.needs_inc)
        self.stats = (len(allops), nwaits, nincs, self.nfam, self.nphys)

    def run_block(self):
        sems = self.sems

        def sem_of(skey):
            return sems[skey] if skey[0] == "e" else self.fam_sem[skey[1]]

        def base_of(skey):
            return 0 if skey[0] == "e" else self.fam_base[skey[1]]

        with self.nc.Block() as block:
            def mk(e):
                def runner(eng):
                    for o in self.ops[e]:
                        for (skey, v) in o.waits:
                            eng.wait_ge(sem_of(skey), base_of(skey) + v)
                        ins = o.build(eng)
                        if o.needs_inc:
                            ins.then_inc(sem_of(self._skey(o)) if o.dma else
                                         sems[self._skey(o)], 16 if o.dma else 1)
                return runner
            block.tensor(mk("tensor"))
            block.vector(mk("vector"))
            block.scalar(mk("scalar"))
            block.gpsimd(mk("gpsimd"))
            block.sync(mk("sync"))



def host_constants():
    c = {}
    c["c_id_bf"] = np.eye(128, dtype=ml_dtypes.bfloat16)
    c["c_id_f"] = np.eye(128, dtype=np.float32)
    c["c_ones_f"] = np.ones((1, 128), dtype=np.float32)
    c["c_onescol_bf"] = np.ones((128, 1), dtype=ml_dtypes.bfloat16)
    p = np.arange(128)[:, None]
    f = np.arange(512)[None, :]
    mk = np.zeros((128, 4 * 512), dtype=np.float32)
    for m in range(4):
        mk[:, 512 * m:512 * (m + 1)] = np.where(p + 128 * m > f, -1e30, 0.0)
    c["c_maskc"] = mk
    pp = np.arange(128)[:, None]
    ff = np.arange(128)[None, :]
    c["c_mA"] = np.where(ff >= pp, 1e30, 0.0).astype(np.float32)
    c["c_mW"] = np.where(pp > ff, -1e30, 0.0).astype(np.float32)
    c["c_mWs"] = np.where(pp >= ff, -1e30, 0.0).astype(np.float32)
    c["c_sel127"] = np.zeros((128, 128), dtype=np.float32)
    c["c_sel127"][127, :] = 1.0
    hs = np.zeros((48, 16), dtype=np.float32)
    for h in range(16):
        hs[3 * h:3 * h + 3, h] = 1.0
    c["c_hsel"] = hs
    c["c_fvals"] = np.tile(np.asarray([CONST_VALS], np.float32), (128, 1))
    return c



class Ctx:
    pass


_SB_N = [0]


def _sb(ctx, stack, name, shape, dt):
    _SB_N[0] += 1
    t = stack.enter_context(ctx.nc.sbuf_tensor(f"{name}_{_SB_N[0]}", shape, dt))
    return t


def declare_io(nc, input_specs, dumps=()):
    io = {}
    for name, shape, dt in input_specs:
        io[name] = nc.dram_tensor(name, list(shape), dt, kind="ExternalInput").ap()
    return io


def build_program(dumps=(), stages=("l0", "moe", "l1a", "l1f"), dbg=(), feeds=()):
    nc = bass.Bass("TRN2", target_bir_lowering=False)
    ctx = Ctx()
    ctx.nc = nc
    ctx.pr = Pr(nc)
    ctx.stack = contextlib.ExitStack()
    ctx.dbg = set(dbg)

    cns = host_constants()
    specs = [("x", (T, D), F32)]
    for k, v in cns.items():
        specs.append((k, v.shape, BF16 if v.dtype == ml_dtypes.bfloat16 else F32))
    wspec = [
        ("L0_attn_wq", (D, 48)), ("L0_attn_wk", (D, 48)), ("L0_attn_wv", (D, D)),
        ("L0_attn_wo", (D, D)), ("L0_attn_head_dirs", (16, 3)), ("L0_attn_scale", (1,)),
        ("L0_ffn_Wg", (D, 8)), ("L0_ffn_bg", (8,)),
        ("L0_ffn_W0", (8, D, 4096)), ("L0_ffn_b0", (8, 4096)),
        ("L0_ffn_W1", (8, D, 4096)), ("L0_ffn_b1", (8, 4096)),
        ("L0_ffn_W2", (8, 4096, D)), ("L0_ffn_b2", (8, D)),
        ("L1_attn_ln1_g", (D,)), ("L1_attn_ln1_b", (D,)),
        ("L1_attn_ln2_g", (D,)), ("L1_attn_ln2_b", (D,)),
        ("L1_attn_rms_in_w", (D,)),
        ("L1_attn_Wq", (D, D)), ("L1_attn_Wk", (D, D)), ("L1_attn_Wv", (D, 2048)),
        ("L1_attn_conv_q_w", (D, 4)), ("L1_attn_conv_k_w", (D, 4)),
        ("L1_attn_conv_v_w", (2048, 4)),
        ("L1_attn_Wa", (D, 4)), ("L1_attn_dt_bias", (4,)), ("L1_attn_A_log", (4,)),
        ("L1_attn_Wb", (D, 4)), ("L1_attn_Wg", (D, 2048)),
        ("L1_attn_o_norm_w", (512,)), ("L1_attn_Wo", (2048, D)),
        ("L1_attn_W2", (D, D)), ("L1_attn_b2", (D,)),
        ("L1_ffn_w1_w", (4096, D)), ("L1_ffn_w1_s", (4096,)),
        ("L1_ffn_w2_w", (4096, D)), ("L1_ffn_w2_s", (4096,)),
        ("L1_ffn_h1_w", (4096, 4096)), ("L1_ffn_h1_s", (4096,)),
        ("L1_ffn_h2_w", (4096, 4096)), ("L1_ffn_h2_s", (4096,)),
        ("L1_ffn_w3_w", (D, 4096)), ("L1_ffn_w3_s", (D,)),
    ]
    for n, s in wspec:
        specs.append((n, s, F32))
    class LazyIO(dict):
        def __init__(self, specs):
            super().__init__()
            self.specs = {n: (sh, dt) for n, sh, dt in specs}

        def __missing__(self, key):
            sh, dt = self.specs[key]
            ap = nc.dram_tensor(key, list(sh), dt, kind="ExternalInput").ap()
            self[key] = ap
            return ap

    io = LazyIO(specs)
    ctx.io = io

    def scratch(name, shape, dt):
        kind = "ExternalOutput" if name in dumps else (
            "ExternalInput" if name in feeds else "Internal")
        ap = nc.dram_tensor(name, list(shape), dt, kind=kind).ap()
        ctx.bufs[name] = Buf(name)
        return ap

    ctx.bufs = {}
    ctx.dram = {}
    for name, shape, dt in [
        ("xT32", (D, T), F32),
        ("vTs", (D, T), BF16),
        ("aoTs", (D, T), BF16),
        ("r1_32", (D, T), F32), ("r1_bf", (D, T), BF16),
        ("r2_32", (D, T), F32), ("r2_bf", (D, T), BF16),
        ("hTbf", (D, T), BF16),
        ("oTs", (2048, T), BF16),
        ("x1_32", (D, T), F32),
        ("h2bf", (D, T), BF16),
        ("r3_32", (D, T), F32), ("r3_bf", (D, T), BF16),
        ("h1T", (4096, T), BF16), ("h2T", (4096, T), BF16), ("h3T", (4096, T), BF16),
        ("wbf_w1", (4096, D), BF16), ("wbf_w2", (4096, D), BF16),
        ("wbf_h1", (4096, 4096), BF16), ("wbf_h2", (4096, 4096), BF16),
        ("wbf_w3", (D, 4096), BF16),
        ("r4_32", (D, T), F32), ("r4_bf", (D, T), BF16),
        ("lgts", (T, 8), F32),
        ("qnT_d", (256, T), BF16), ("knT_d", (256, T), BF16),
        ("kn_tok_d", (T, 256), BF16), ("v_tok_d", (T, 512), BF16),
        ("gate_d", (T, 512), BF16),
        ("dbg_q4", (16, T), BF16),
        ("dbg_pr", (128, T), BF16),
        ("dbg_sm", (48, 17), F32),
        ("dbg_moe", (128, 8192), F32),
        ("dbg_ch", (128, 12288), F32),
        ("dbg_cmb", (128, 64), F32),
    ]:
        ctx.dram[name] = scratch(name, shape, dt)
    ctx.out = nc.dram_tensor("out", [T, D], F32, kind="ExternalOutput").ap()
    ctx.bufs["out"] = Buf("out")

    stk = ctx.stack
    ctx.pb = [stk.enter_context(nc.psum_tensor(f"pb{i}", [128, 512], F32)) for i in range(6)]
    ctx.pt = [stk.enter_context(nc.psum_tensor(f"pt{i}", [128, 128], BF16)) for i in range(2)]
    ctx.pbB = [Buf(f"pb{i}", excl=True) for i in range(6)]
    ctx.ptB = [Buf(f"pt{i}", excl=True) for i in range(2)]

    ctx.c = {}
    ctx.cB = Buf("consts")
    pr = ctx.pr
    for nm, shape, dt in [("c_id_bf", [128, 128], BF16), ("c_id_f", [128, 128], F32),
                          ("c_ones_f", [1, 128], F32), ("c_onescol_bf", [128, 1], BF16),
                          ("c_maskc", [128, 2048], F32), ("c_mA", [128, 128], F32),
                          ("c_mW", [128, 128], F32), ("c_mWs", [128, 128], F32),
                          ("c_sel127", [128, 128], F32),
                          ("c_hsel", [48, 16], F32),
                          ("c_fvals", [128, len(CONST_VALS)], F32)]:
        tile = _sb(ctx, stk, nm + "_sb", shape, dt)
        ctx.c[nm] = tile
        pr.op("sync", (lambda e, t=tile, a=io[nm], s=shape: e.dma_start(
            out=t[0:s[0], 0:s[1]], in_=a)), writes=[ctx.cB], dma=True)
    for i, v in enumerate(CONST_VALS):
        nc.const_aps.register(F32, v, ctx.c["c_fvals"][0:128, i:i + 1])

    if "l0" in stages:
        st_l0(ctx)
    if "moe" in stages:
        st_moe(ctx)
    if "l1a" in stages:
        st_l1attn(ctx)
    if "l1f" in stages:
        st_l1ffn(ctx)

    ctx.pr.finalize(ctx.stack)
    ctx.pr.run_block()
    ctx.stack.close()
    return nc, ctx



L0_PARTS = ("init", "proj", "heads", "lna")
INIT_N = 32
HEADS_N = 16


def st_l0(ctx):
    nc, pr, io, c = ctx.nc, ctx.pr, ctx.io, ctx.c
    pb, pbB = ctx.pb, ctx.pbB
    cB = ctx.cB
    with contextlib.ExitStack() as stk:
        xTs = _sb(ctx, stk, "xTs", [128, 8 * T], BF16)
        xTsB = Buf("xTs")
        xin = [_sb(ctx, stk, f"xin{i}", [128, 1024], F32) for i in range(2)]
        xinB = [Buf(f"xin{i}") for i in range(2)]
        xst = [_sb(ctx, stk, f"xst{i}", [128, 1024], F32) for i in range(2)]
        xstB = [Buf(f"xst{i}") for i in range(2)]
        import os
        SUB = os.environ.get("INIT_SUB", "ltvse").lower()
        for i in range(INIT_N):
            bi = i % 2
            if "l" in SUB:
                pr.op("sync", (lambda e, i=i, t=xin[bi]: e.dma_start(
                    out=t[0:128, 0:1024], in_=io["x"][128 * i:128 * (i + 1), :])),
                    reads=[], writes=[xinB[bi]], dma=True)
            for k in range(8):
                pt = pb[k % 2]
                ptB = pbB[k % 2]
                if "t" in SUB:
                    pr.op("tensor", (lambda e, t=xin[bi], k=k, pt=pt: e.transpose(
                        pt[0:128, 0:128], t[0:128, 128 * k:128 * (k + 1)], c["c_id_f"][0:128, 0:128])),
                        reads=[xinB[bi], cB], writes=[ptB])
                if "v" in SUB:
                    pr.op("vector", (lambda e, i=i, k=k, pt=pt: e.tensor_copy(
                        xTs[0:128, T * k + 128 * i: T * k + 128 * (i + 1)], pt[0:128, 0:128])),
                        reads=[ptB], writes=[xTsB])
                if "s" in SUB:
                    pr.op("scalar", (lambda e, i=i, k=k, pt=pt, s=xst[bi]: e.copy(
                        s[0:128, 128 * k:128 * (k + 1)], pt[0:128, 0:128])),
                        reads=[ptB], writes=[xstB[bi]])
            if "e" in SUB:
                pr.op("sync", (lambda e, i=i, s=xst[bi]: e.dma_start(
                    out=ctx.dram["xT32"][0:1024, 128 * i:128 * (i + 1)].rearrange(
                        "(a p) t -> p a t", p=128), in_=s[0:128, 0:1024].rearrange(
                        "p (a t) -> p a t", a=8))),
                    reads=[xstB[bi]], writes=[ctx.bufs["xT32"]], dma=True)

        if "proj" not in L0_PARTS:
            pr.fence()
            return
        wq_sb = _sb(ctx, stk, "wq_sb", [128, 8 * 48], BF16)
        wk_sb = _sb(ctx, stk, "wk_sb", [128, 8 * 48], BF16)
        wB = Buf("l0w")
        for k in range(8):
            pr.op("gpsimd", (lambda e, k=k: e.dma_start(
                out=wq_sb[0:128, 48 * k:48 * (k + 1)], in_=io["L0_attn_wq"][128 * k:128 * (k + 1), :])),
                writes=[wB], dma=True)
            pr.op("gpsimd", (lambda e, k=k: e.dma_start(
                out=wk_sb[0:128, 48 * k:48 * (k + 1)], in_=io["L0_attn_wk"][128 * k:128 * (k + 1), :])),
                writes=[wB], dma=True)
        DTf = _sb(ctx, stk, "DTf", [48, 16], F32)
        DTb = _sb(ctx, stk, "DTb", [48, 16], BF16)
        DTB = Buf("DT")
        scale_sb = _sb(ctx, stk, "scale_sb", [1, 1], F32)
        scol = _sb(ctx, stk, "scol", [48, 1], F32)
        scolB = Buf("scol")
        dirow = _sb(ctx, stk, "dirow", [1, 48], F32)
        dicol = _sb(ctx, stk, "dicol", [48, 1], F32)
        pr.op("sync", lambda e: e.dma_start(
            out=dirow[0:1, 0:48], in_=io["L0_attn_head_dirs"].rearrange("a b -> (a b)")[None, :]),
            writes=[DTB], dma=True)
        pr.op("tensor", lambda e: e.matmul(
            pb[5][0:48, 0:1], dirow[0:1, 0:48], c["c_ones_f"][0:1, 0:1],
            start=True, stop=True),
            reads=[DTB, cB], writes=[pbB[5]])
        pr.op("vector", lambda e: e.tensor_copy(dicol[0:48, 0:1], pb[5][0:48, 0:1]),
              reads=[pbB[5]], writes=[DTB])
        pr.op("vector", lambda e: e.tensor_scalar(
            DTf[0:48, 0:16], c["c_hsel"][0:48, 0:16], dicol[0:48, 0:1], None, ALU.mult),
            reads=[DTB, cB], writes=[DTB])
        pr.op("vector", lambda e: e.tensor_copy(DTb[0:48, 0:16], DTf[0:48, 0:16]),
              reads=[DTB], writes=[DTB])
        pr.op("sync", lambda e: e.dma_start(out=scale_sb[0:1, 0:1],
                                            in_=io["L0_attn_scale"][:, None]),
              writes=[scolB], dma=True)
        pr.op("tensor", lambda e: e.matmul(pb[5][0:48, 0:1], c["c_ones_f"][0:1, 0:48],
                                           scale_sb[0:1, 0:1], start=True, stop=True),
              reads=[scolB, cB], writes=[pbB[5]])
        pr.op("vector", lambda e: e.tensor_copy(scol[0:48, 0:1], pb[5][0:48, 0:1]),
              reads=[pbB[5]], writes=[scolB])

        q3u = _sb(ctx, stk, "q3u", [48, T], BF16)
        q3s = _sb(ctx, stk, "q3s", [48, T], BF16)
        k3u = _sb(ctx, stk, "k3u", [48, T], BF16)
        qdT = _sb(ctx, stk, "qdT", [16, T], BF16)
        kdT = _sb(ctx, stk, "kdT", [16, T], BF16)
        q3uB, q3sB, k3uB, qdTB, kdTB = (Buf(n) for n in ("q3u", "q3s", "k3u", "qdT", "kdT"))
        vstg = [_sb(ctx, stk, f"vstg{i}", [128, 512], BF16) for i in range(2)]
        vstgB = [Buf(f"vstg{i}") for i in range(2)]

        for tc in range(8):
            sl = slice(512 * tc, 512 * (tc + 1))
            for (wsb, uout, uB, sout) in ((wq_sb, q3u, q3uB, q3s), (wk_sb, k3u, k3uB, None)):
                pq = pb[2]
                for k in range(8):
                    pr.op("tensor", (lambda e, wsb=wsb, k=k, tc=tc, pq=pq: e.matmul(
                        pq[0:48, 0:512], wsb[0:128, 48 * k:48 * (k + 1)],
                        xTs[0:128, T * k + 512 * tc:T * k + 512 * (tc + 1)],
                        start=(k == 0), stop=(k == 7))),
                        reads=[wB, xTsB], writes=[pbB[2]])
                pr.op("vector", (lambda e, uout=uout, sl=sl, pq=pq: e.tensor_copy(
                    uout[0:48, sl], pq[0:48, 0:512])), reads=[pbB[2]], writes=[uB])
                if sout is not None:
                    pr.op("vector", (lambda e, sout=sout, sl=sl, pq=pq: e.tensor_scalar(
                        sout[0:48, sl], pq[0:48, 0:512], scol[0:48, 0:1], None, ALU.mult)),
                        reads=[pbB[2], scolB], writes=[q3sB])
            for (uin, uB, dout, dB) in ((q3u, q3uB, qdT, qdTB), (k3u, k3uB, kdT, kdTB)):
                pr.op("tensor", (lambda e, uin=uin, sl=sl: e.matmul(
                    pb[3][0:16, 0:512], DTb[0:48, 0:16], uin[0:48, sl], start=True, stop=True)),
                    reads=[DTB, uB], writes=[pbB[3]])
                pr.op("vector", (lambda e, dout=dout, sl=sl: e.tensor_copy(
                    dout[0:16, sl], pb[3][0:16, 0:512])), reads=[pbB[3]], writes=[dB])
        wv_t = _sb(ctx, stk, "wv_t", [128, 8 * 128], BF16)
        wvB = Buf("wv")
        for m in range(8):
            for k in range(8):
                pr.op("gpsimd", (lambda e, m=m, k=k: e.dma_start(
                    out=wv_t[0:128, 128 * k:128 * (k + 1)],
                    in_=io["L0_attn_wv"][128 * k:128 * (k + 1), 128 * m:128 * (m + 1)])),
                    writes=[wvB], dma=True)
            for tc in range(8):
                bi = tc % 2
                pv = pb[4]
                for k in range(8):
                    pr.op("tensor", (lambda e, k=k, tc=tc, pv=pv: e.matmul(
                        pv[0:128, 0:512], wv_t[0:128, 128 * k:128 * (k + 1)],
                        xTs[0:128, T * k + 512 * tc:T * k + 512 * (tc + 1)],
                        start=(k == 0), stop=(k == 7))),
                        reads=[wvB, xTsB], writes=[pbB[4]])
                pr.op("vector", (lambda e, bi=bi, pv=pv: e.tensor_copy(
                    vstg[bi][0:128, 0:512], pv[0:128, 0:512])),
                    reads=[pbB[4]], writes=[vstgB[bi]])
                pr.op("sync", (lambda e, m=m, tc=tc, bi=bi: e.dma_start(
                    out=ctx.dram["vTs"][128 * m:128 * (m + 1), 512 * tc:512 * (tc + 1)],
                    in_=vstg[bi][0:128, 0:512])),
                    reads=[vstgB[bi]], writes=[ctx.bufs["vTs"]], dma=True)
        if "proj_dump" in ctx.dbg:
            pr.op("sync", lambda e: e.dma_start(out=ctx.dram["dbg_pr"][0:48, :], in_=q3u[0:48, 0:T]),
                  reads=[q3uB], writes=[ctx.bufs["dbg_pr"]], dma=True)
            pr.op("sync", lambda e: e.dma_start(out=ctx.dram["dbg_pr"][48:96, :], in_=q3s[0:48, 0:T]),
                  reads=[q3sB], writes=[ctx.bufs["dbg_pr"]], dma=True)
            pr.op("sync", lambda e: e.dma_start(out=ctx.dram["dbg_pr"][96:112, :], in_=qdT[0:16, 0:T]),
                  reads=[qdTB], writes=[ctx.bufs["dbg_pr"]], dma=True)
            pr.op("sync", lambda e: e.dma_start(out=ctx.dram["dbg_pr"][112:128, :], in_=kdT[0:16, 0:T]),
                  reads=[kdTB], writes=[ctx.bufs["dbg_pr"]], dma=True)
            pr.op("sync", lambda e: e.dma_start(out=ctx.dram["dbg_sm"][0:48, 0:16], in_=DTf[0:48, 0:16]),
                  reads=[DTB], writes=[ctx.bufs["dbg_sm"]], dma=True)
            pr.op("sync", lambda e: e.dma_start(out=ctx.dram["dbg_sm"][0:48, 16:17], in_=scol[0:48, 0:1], allow_slow_non_contiguous=True),
                  reads=[scolB], writes=[ctx.bufs["dbg_sm"]], dma=True)
        pr.fence()

        if "heads" not in L0_PARTS:
            return
        q4 = [_sb(ctx, stk, f"q4_{i}", [4, T], BF16) for i in range(2)]
        k4 = [_sb(ctx, stk, f"k4_{i}", [4, T], BF16) for i in range(2)]
        q4B = [Buf(f"q4{i}") for i in range(2)]
        k4B = [Buf(f"k4{i}") for i in range(2)]
        vhT = [_sb(ctx, stk, f"vhT{i}", [64, T], BF16) for i in range(2)]
        vhTB = [Buf(f"vhT{i}") for i in range(2)]
        vaug = [_sb(ctx, stk, f"vaug{i}", [128, 32 * 65], BF16) for i in range(2)]
        vaugB = [Buf(f"vaug{i}") for i in range(2)]
        expT = [_sb(ctx, stk, f"expT{i}", [128, 512], BF16) for i in range(4)]
        expTB = [Buf(f"expT{i}") for i in range(4)]
        mskb = [_sb(ctx, stk, "mskb0", [128, 512], F32)] * 2
        mskbB = [Buf("mskb0")] * 2
        oun = [_sb(ctx, stk, "oun0", [65, 512], F32)] * 2
        ounB = [Buf("oun0")] * 2
        zr = [_sb(ctx, stk, f"zr{i}", [1, 512], F32) for i in range(2)]
        zrB = [Buf(f"zr{i}") for i in range(2)]
        aost = [_sb(ctx, stk, f"aost{i}", [64, 512], BF16) for i in range(2)]
        aostB = [Buf(f"aost{i}") for i in range(2)]
        SCB = {0: (0, 1), 1: (4, 5)}
        AVB = {0: 2, 1: 3}

        for hp in range(HEADS_N // 2):
            pair = (2 * hp, 2 * hp + 1)
            for h in pair:
                hb = h % 2
                pr.op("sync", (lambda e, h=h, t=q4[hb]: e.dma_start(
                    out=t[0:3, 0:T], in_=q3s[3 * h:3 * h + 3, 0:T])),
                    reads=[q3sB], writes=[q4B[hb]], dma=True)
                pr.op("sync", (lambda e, h=h, t=q4[hb]: e.dma_start(
                    out=t[3:4, 0:T], in_=qdT[h:h + 1, 0:T])),
                    reads=[qdTB], writes=[q4B[hb]], dma=True)
                pr.op("sync", (lambda e, h=h, t=k4[hb]: e.dma_start(
                    out=t[0:3, 0:T], in_=k3u[3 * h:3 * h + 3, 0:T])),
                    reads=[k3uB], writes=[k4B[hb]], dma=True)
                pr.op("sync", (lambda e, h=h, t=k4[hb]: e.dma_start(
                    out=t[3:4, 0:T], in_=kdT[h:h + 1, 0:T])),
                    reads=[kdTB], writes=[k4B[hb]], dma=True)
                pr.op("sync", (lambda e, h=h, t=vhT[hb]: e.dma_start(
                    out=t[0:64, 0:T], in_=ctx.dram["vTs"][64 * h:64 * (h + 1), 0:T])),
                    reads=[ctx.bufs["vTs"]], writes=[vhTB[hb]], dma=True)
                pr.op("vector", (lambda e, t=vaug[hb]: e.memset(t[0:128, 0:32 * 65], 1.0)),
                      writes=[vaugB[hb]])
                for j in range(32):
                    ptp = ctx.pt[j % 2]
                    ptpB = ctx.ptB[j % 2]
                    pr.op("tensor", (lambda e, j=j, t=vhT[hb], ptp=ptp: e.transpose(
                        ptp[0:128, 0:64], t[0:64, 128 * j:128 * (j + 1)],
                        c["c_id_bf"][0:64, 0:64])),
                        reads=[vhTB[hb], cB], writes=[ptpB])
                    pr.op("vector", (lambda e, j=j, t=vaug[hb], ptp=ptp: e.tensor_copy(
                        t[0:128, 65 * j:65 * j + 64], ptp[0:128, 0:64])),
                        reads=[ptpB], writes=[vaugB[hb]])
            for tc in range(8):
                tb = tc % 2
                jmax = 4 * tc + 3

                def emit_sc_exp(h, j, tc=tc):
                    hb = h % 2
                    psc = pb[SCB[hb][j % 2]]
                    pscB = pbB[SCB[hb][j % 2]]
                    eb = 2 * hb + j % 2
                    pr.op("tensor", (lambda e, j=j, tc=tc, psc=psc, tk=k4[hb], tq=q4[hb]:
                                     e.matmul(
                        psc[0:128, 0:512], tk[0:4, 128 * j:128 * (j + 1)],
                        tq[0:4, 512 * tc:512 * (tc + 1)], start=True, stop=True)),
                        reads=[k4B[hb], q4B[hb]], writes=[pscB])
                    if j >= 4 * tc:
                        m = j - 4 * tc
                        pr.op("vector", (lambda e, m=m, psc=psc, mk=mskb[hb]: e.tensor_add(
                            mk[0:128, 0:512], c["c_maskc"][0:128, 512 * m:512 * (m + 1)],
                            psc[0:128, 0:512])), reads=[pscB, cB], writes=[mskbB[hb]])
                        pr.op("scalar", (lambda e, eb=eb, mk=mskb[hb]: e.activation(
                            expT[eb][0:128, 0:512], mk[0:128, 0:512], AF.Exp)),
                            reads=[mskbB[hb]], writes=[expTB[eb]])
                    else:
                        pr.op("scalar", (lambda e, eb=eb, psc=psc: e.activation(
                            expT[eb][0:128, 0:512], psc[0:128, 0:512], AF.Exp)),
                            reads=[pscB], writes=[expTB[eb]])

                def emit_av(h, j, jmax=jmax):
                    hb = h % 2
                    eb = 2 * hb + j % 2
                    bav = AVB[hb]
                    pr.op("tensor", (lambda e, j=j, jmax=jmax, tv=vaug[hb], eb=eb,
                                     bav=bav: e.matmul(
                        pb[bav][0:65, 0:512], tv[0:128, 65 * j:65 * j + 65],
                        expT[eb][0:128, 0:512], start=(j == 0), stop=(j == jmax))),
                        reads=[vaugB[hb], expTB[eb]], writes=[pbB[bav]])

                for j in range(jmax + 1):
                    for h in pair:
                        emit_sc_exp(h, j)
                        if j > 0:
                            emit_av(h, j - 1)
                for h in pair:
                    emit_av(h, jmax)
                for h in pair:
                    hb = h % 2
                    bav = AVB[hb]
                    bz = SCB[hb][0]
                    pr.op("vector", (lambda e, tb=tb, bav=bav: e.tensor_copy(
                        oun[tb][0:65, 0:512], pb[bav][0:65, 0:512])),
                        reads=[pbB[bav]], writes=[ounB[tb]])
                    pr.op("vector", (lambda e, tb=tb, z=zr[hb]: e.reciprocal(
                        z[0:1, 0:512], oun[tb][64:65, 0:512])),
                        reads=[ounB[tb]], writes=[zrB[hb]])
                    pr.op("tensor", (lambda e, z=zr[hb], bz=bz: e.matmul(
                        pb[bz][0:64, 0:512], c["c_ones_f"][0:1, 0:64], z[0:1, 0:512],
                        start=True, stop=True)), reads=[zrB[hb], cB], writes=[pbB[bz]])
                    pr.op("vector", (lambda e, tb=tb, a=aost[hb], bz=bz: e.tensor_mul(
                        a[0:64, 0:512], oun[tb][0:64, 0:512], pb[bz][0:64, 0:512])),
                        reads=[ounB[tb], pbB[bz]], writes=[aostB[hb]])
                    pr.op("sync", (lambda e, h=h, tc=tc, a=aost[hb]: e.dma_start(
                        out=ctx.dram["aoTs"][64 * h:64 * (h + 1), 512 * tc:512 * (tc + 1)],
                        in_=a[0:64, 0:512])),
                        reads=[aostB[hb]], writes=[ctx.bufs["aoTs"]], dma=True)
        pr.fence()

    if "lna" not in L0_PARTS:
        return
    st_gemm_ln(ctx, src="aoTs", wname="L0_attn_wo", K=1024, res=["xT32"],
               out32="r1_32", outbf="r1_bf", gate=True)
    pr.fence()


def _col_loads(ctx, pr, stk, name, src_ap_fn, n, tag, buf=None, dt=F32):
    t = _sb(ctx, stk, tag, [128, n], dt)
    B = buf if buf is not None else Buf(tag)
    q = "sync" if dt == F32 else "gpsimd"
    for i in range(n):
        ap = src_ap_fn(i)
        pr.op(q, (lambda e, i=i, ap=ap: e.dma_start(
            out=t[0:128, i:i + 1], in_=ap,
            allow_slow_non_contiguous=True)), writes=[B], dma=True)
    return t, B


def _ln_rows(ctx, pr, rows, rowsB, sbank, qbank, eps, nfeat):
    pb, pbB = ctx.pb, ctx.pbB
    cB = ctx.cB
    pr.op("vector", (lambda e: e.tensor_scalar(
        rows[0:1, 0:512], pb[sbank][0:1, 0:512], 1.0 / nfeat, None, ALU.mult)),
        reads=[pbB[sbank]], writes=[rowsB])
    pr.op("scalar", (lambda e: e.activation(
        rows[0:1, 512:1024], rows[0:1, 0:512], AF.Square)),
        reads=[rowsB], writes=[rowsB])
    pr.op("vector", (lambda e: e.scalar_tensor_tensor(
        rows[0:1, 1024:1536], pb[qbank][0:1, 0:512], 1.0 / nfeat, rows[0:1, 512:1024],
        ALU.mult, ALU.subtract)), reads=[pbB[qbank], rowsB], writes=[rowsB])
    pr.op("scalar", (lambda e: e.activation(
        rows[0:1, 1536:2048], rows[0:1, 1024:1536], AF.Ln, bias=eps)),
        reads=[rowsB, cB], writes=[rowsB])
    pr.op("scalar", (lambda e: e.activation(
        rows[0:1, 2048:2560], rows[0:1, 1536:2048], AF.Exp, scale=-0.5)),
        reads=[rowsB], writes=[rowsB])


def st_gemm_ln(ctx, src, wname, K, res, out32, outbf, zout=None, affine=None,
               act=None, wscale_input=None, out_scale_name=None, w_from_dram=False,
               gate=False):
    nc, pr, io, c = ctx.nc, ctx.pr, ctx.io, ctx.c
    pb, pbB = ctx.pb, ctx.pbB
    cB = ctx.cB
    with contextlib.ExitStack() as stk:
        nk = K // 128
        w_sb = _sb(ctx, stk, "bw_sb", [128, nk * 1024], BF16)
        wB = Buf("bw")
        if wscale_input is not None:
            wsname, wsper = wscale_input
            wsc, wscB = _col_loads(ctx, pr, stk, "wsc",
                                   lambda i: io[wsname][128 * i:128 * (i + 1)][:, None],
                                   wsper, "wsc")
        if w_from_dram:
            for k in range(nk):
                pr.op("scalar", (lambda e, k=k: e.dma_start(
                    out=w_sb[0:128, 1024 * k:1024 * (k + 1)],
                    in_=ctx.dram[wname][0:1024, 128 * k:128 * (k + 1)],
                    transpose=True)),
                    reads=[ctx.bufs[wname]], writes=[wB], dma=True)
        else:
            pr.op("gpsimd", (lambda e: e.dma_start(
                out=w_sb[0:128, 0:nk * 1024].rearrange("p (k m) -> p k m", k=nk),
                in_=io[wname][0:K, :].rearrange("(k p) m -> p k m", p=128))),
                writes=[wB], dma=True)
        for k in range(nk):
            if wscale_input is not None:
                pr.op("vector", (lambda e, k=k: e.tensor_scalar(
                    w_sb[0:128, 1024 * k:1024 * (k + 1)],
                    w_sb[0:128, 1024 * k:1024 * (k + 1)],
                    wsc[0:128, (k % wsper):(k % wsper) + 1], None, ALU.mult)),
                    reads=[wB, wscB], writes=[wB])
        acols = None
        if act is not None:
            acols, acolsB = _col_loads(ctx, pr, stk, "acols",
                                       lambda i: io[act[1]][128 * i:128 * (i + 1)][:, None],
                                       8, "acols")
        oscols = None
        if out_scale_name is not None:
            oscols, oscolsB = _col_loads(ctx, pr, stk, "oscols",
                                         lambda i: io[out_scale_name][128 * i:128 * (i + 1)][:, None],
                                         8, "oscols")
        gcols = bcols = None
        if affine is not None:
            gcols, gcolsB = _col_loads(ctx, pr, stk, "gcols",
                                       lambda i: io[affine[0]][128 * i:128 * (i + 1)][:, None],
                                       8, "lngc")
            bcols, bcolsB = _col_loads(ctx, pr, stk, "bcols",
                                       lambda i: io[affine[1]][128 * i:128 * (i + 1)][:, None],
                                       8, "lnbc")
        src_in = [_sb(ctx, stk, f"bsrc{i}", [128, nk * 512], BF16) for i in range(2)]
        srcB = [Buf(f"bsrc{i}") for i in range(2)]
        res_in = [_sb(ctx, stk, "bres", [128, 8 * 512], F32)] * 2
        resB = [Buf("bres")] * 2
        if len(res) > 1:
            res2_in = [_sb(ctx, stk, "bres2", [128, 8 * 512], F32)] * 2
            res2B = [Buf("bres2")] * 2
        zbuf = _sb(ctx, stk, "bz", [128, 8 * 512], F32)
        zB = Buf("bz")
        zbf = [_sb(ctx, stk, f"bzb{i}", [128, 512], BF16) for i in range(2)]
        zbfB = [Buf(f"bzb{i}") for i in range(2)]
        zsq = [_sb(ctx, stk, f"bzs{i}", [128, 512], BF16) for i in range(2)]
        zsqB = [Buf(f"bzs{i}") for i in range(2)]
        rows = _sb(ctx, stk, "brows", [1, 5 * 512], F32)
        rowsB = Buf("brows")
        tg = [_sb(ctx, stk, f"btg{i}", [128, 512], F32) for i in range(2)]
        tgB = [Buf(f"btg{i}") for i in range(2)]
        yst = [_sb(ctx, stk, f"by{i}", [128, 512], F32) for i in range(4)]
        ystB = [Buf(f"by{i}") for i in range(4)]
        ybf = [_sb(ctx, stk, f"byb{i}", [128, 512], BF16) for i in range(4)]
        ybfB = [Buf(f"byb{i}") for i in range(4)]
        if gate:
            wgf = _sb(ctx, stk, "wgf", [128, 8 * 8], F32)
            wgfB = Buf("wgf")
            lgst = _sb(ctx, stk, "lgst", [128, 32], F32)
            lgstB = Buf("lgst")
            lgacc = _sb(ctx, stk, "lgacc", [128, 32], F32)
            lgaccB = Buf("lgacc")
            for k in range(8):
                pr.op("sync", (lambda e, k=k: e.dma_start(
                    out=wgf[0:128, 8 * k:8 * (k + 1)],
                    in_=io["L0_ffn_Wg"][128 * k:128 * (k + 1), :])), writes=[wgfB], dma=True)

        for tc in range(8):
            bi = tc % 2
            pr.op("sync", (lambda e, tc=tc, t=src_in[bi]: e.dma_start(
                out=t[0:128, 0:nk * 512].rearrange("p (k t) -> p k t", k=nk),
                in_=ctx.dram[src][0:K, 512 * tc:512 * (tc + 1)].rearrange(
                    "(k p) t -> p k t", p=128))),
                reads=[ctx.bufs[src]], writes=[srcB[bi]], dma=True)
            for m in range(8):
                mb = m % 2
                pm = pb[0]
                pmB = pbB[0]
                for k in range(nk):
                    pr.op("tensor", (lambda e, k=k, m=m, pm=pm, t=src_in[bi]: e.matmul(
                        pm[0:128, 0:512], w_sb[0:128, 1024 * k + 128 * m:1024 * k + 128 * (m + 1)],
                        t[0:128, 512 * k:512 * (k + 1)], start=(k == 0), stop=(k == nk - 1))),
                        reads=[wB, srcB[bi]], writes=[pmB])
                if m == 0:
                    pr.op("sync", (lambda e, tc=tc, t=res_in[bi]: e.dma_start(
                        out=t[0:128, 0:8 * 512].rearrange("p (k t) -> p k t", k=8),
                        in_=ctx.dram[res[0]][0:1024, 512 * tc:512 * (tc + 1)].rearrange(
                            "(k p) t -> p k t", p=128))),
                        reads=[ctx.bufs[res[0]]], writes=[resB[bi]], dma=True)
                    if len(res) > 1:
                        pr.op("sync", (lambda e, tc=tc, t=res2_in[bi]: e.dma_start(
                            out=t[0:128, 0:8 * 512].rearrange("p (k t) -> p k t", k=8),
                            in_=ctx.dram[res[1]][0:1024, 512 * tc:512 * (tc + 1)].rearrange(
                                "(k p) t -> p k t", p=128))),
                            reads=[ctx.bufs[res[1]]], writes=[res2B[bi]], dma=True)
                if act is not None:
                    pr.op("scalar", (lambda e, m=m, t=tg[mb], pm=pm: e.activation(
                        t[0:128, 0:512], pm[0:128, 0:512], AF.Gelu,
                        bias=acols[0:128, m:m + 1])),
                        reads=[pmB, acolsB], writes=[tgB[mb]])
                    pr.op("vector", (lambda e, m=m, t=tg[mb], r=res_in[bi]: e.tensor_add(
                        zbuf[0:128, 512 * m:512 * (m + 1)], t[0:128, 0:512],
                        r[0:128, 512 * m:512 * (m + 1)])),
                        reads=[tgB[mb], resB[bi]], writes=[zB])
                elif out_scale_name is not None:
                    pr.op("vector", (lambda e, m=m, r=res_in[bi], pm=pm: e.scalar_tensor_tensor(
                        zbuf[0:128, 512 * m:512 * (m + 1)], pm[0:128, 0:512],
                        oscols[0:128, m:m + 1], r[0:128, 512 * m:512 * (m + 1)],
                        ALU.mult, ALU.add)),
                        reads=[pmB, oscolsB, resB[bi]], writes=[zB])
                else:
                    pr.op("vector", (lambda e, m=m, r=res_in[bi], pm=pm: e.tensor_add(
                        zbuf[0:128, 512 * m:512 * (m + 1)], r[0:128, 512 * m:512 * (m + 1)],
                        pm[0:128, 0:512])), reads=[resB[bi], pmB], writes=[zB])
                if len(res) > 1:
                    pr.op("vector", (lambda e, m=m, r=res2_in[bi]: e.tensor_add(
                        zbuf[0:128, 512 * m:512 * (m + 1)],
                        zbuf[0:128, 512 * m:512 * (m + 1)],
                        r[0:128, 512 * m:512 * (m + 1)])),
                        reads=[res2B[bi], zB], writes=[zB])
                if zout is not None:
                    pr.op("sync", (lambda e, m=m, tc=tc: e.dma_start(
                        out=ctx.dram[zout][128 * m:128 * (m + 1), 512 * tc:512 * (tc + 1)],
                        in_=zbuf[0:128, 512 * m:512 * (m + 1)])),
                        reads=[zB], writes=[ctx.bufs[zout]], dma=True)
                pr.op("vector", (lambda e, m=m, zb=zbf[mb]: e.tensor_copy(
                    zb[0:128, 0:512], zbuf[0:128, 512 * m:512 * (m + 1)])),
                    reads=[zB], writes=[zbfB[mb]])
                pr.op("scalar", (lambda e, m=m, zb=zbf[mb], zs=zsq[mb]: e.activation(
                    zs[0:128, 0:512], zb[0:128, 0:512], AF.Square)),
                    reads=[zbfB[mb]], writes=[zsqB[mb]])
                pr.op("tensor", (lambda e, m=m, zb=zbf[mb]: e.matmul(
                    pb[2][0:1, 0:512], c["c_onescol_bf"][0:128, 0:1],
                    zb[0:128, 0:512], start=(m == 0), stop=(m == 7))),
                    reads=[zbfB[mb], cB], writes=[pbB[2]])
                pr.op("tensor", (lambda e, m=m, zs=zsq[mb]: e.matmul(
                    pb[3][0:1, 0:512], c["c_onescol_bf"][0:128, 0:1],
                    zs[0:128, 0:512], start=(m == 0), stop=(m == 7))),
                    reads=[zsqB[mb], cB], writes=[pbB[3]])
            _ln_rows(ctx, pr, rows, rowsB, 2, 3, 1e-5, 1024)
            pr.op("tensor", lambda e: e.matmul(
                pb[4][0:128, 0:512], c["c_ones_f"][0:1, 0:128], rows[0:1, 0:512],
                start=True, stop=True), reads=[rowsB, cB], writes=[pbB[4]])
            pr.op("tensor", lambda e: e.matmul(
                pb[5][0:128, 0:512], c["c_ones_f"][0:1, 0:128], rows[0:1, 2048:2560],
                start=True, stop=True), reads=[rowsB, cB], writes=[pbB[5]])
            for m in range(8):
                yb = m % 4
                pr.op("vector", (lambda e, m=m, y=yst[yb]: e.tensor_sub(
                    y[0:128, 0:512], zbuf[0:128, 512 * m:512 * (m + 1)], pb[4][0:128, 0:512])),
                    reads=[zB, pbB[4]], writes=[ystB[yb]])
                pr.op("vector", (lambda e, m=m, y=yst[yb]: e.tensor_mul(
                    y[0:128, 0:512], y[0:128, 0:512], pb[5][0:128, 0:512])),
                    reads=[ystB[yb], pbB[5]], writes=[ystB[yb]])
                if affine is not None:
                    pr.op("vector", (lambda e, m=m, y=yst[yb]: e.tensor_scalar(
                        y[0:128, 0:512], y[0:128, 0:512], gcols[0:128, m:m + 1],
                        bcols[0:128, m:m + 1], ALU.mult, ALU.add)),
                        reads=[ystB[yb], gcolsB, bcolsB], writes=[ystB[yb]])
                pr.op("vector", (lambda e, m=m, y=yst[yb], b=ybf[yb]: e.tensor_copy(
                    b[0:128, 0:512], y[0:128, 0:512])), reads=[ystB[yb]], writes=[ybfB[yb]])
                if gate:
                    if m == 0:
                        pr.op("vector", lambda e: e.memset(lgacc[0:128, 0:32], 0.0),
                              writes=[lgaccB])
                    for ts in range(4):
                        pr.op("tensor", (lambda e, m=m, ts=ts, y=yst[yb]: e.matmul(
                            pb[1][0:128, 8 * ts:8 * (ts + 1)],
                            y[0:128, 128 * ts:128 * (ts + 1)],
                            wgf[0:128, 8 * m:8 * (m + 1)],
                            start=True, stop=True)),
                            reads=[ystB[yb], wgfB], writes=[pbB[1]])
                    pr.op("vector", lambda e: e.tensor_add(
                        lgacc[0:128, 0:32], lgacc[0:128, 0:32], pb[1][0:128, 0:32]),
                        reads=[pbB[1], lgaccB], writes=[lgaccB])
                if out32 is not None:
                    pr.op("sync", (lambda e, m=m, tc=tc, y=yst[yb]: e.dma_start(
                        out=ctx.dram[out32][128 * m:128 * (m + 1), 512 * tc:512 * (tc + 1)],
                        in_=y[0:128, 0:512])), reads=[ystB[yb]], writes=[ctx.bufs[out32]],
                        dma=True)
                pr.op("sync", (lambda e, m=m, tc=tc, b=ybf[yb]: e.dma_start(
                    out=ctx.dram[outbf][128 * m:128 * (m + 1), 512 * tc:512 * (tc + 1)],
                    in_=b[0:128, 0:512])), reads=[ybfB[yb]], writes=[ctx.bufs[outbf]],
                    dma=True)
            if gate:
                pr.op("vector", lambda e: e.tensor_copy(lgst[0:128, 0:32], lgacc[0:128, 0:32]),
                      reads=[lgaccB], writes=[lgstB])
                pr.op("sync", (lambda e, tc=tc: e.dma_start(
                    out=ctx.dram["lgts"][512 * tc:512 * (tc + 1), 0:8].rearrange(
                        "(s p) e -> p s e", p=128), in_=lgst[0:128, 0:32].rearrange(
                        "p (s e) -> p s e", s=4))),
                    reads=[lgstB], writes=[ctx.bufs["lgts"]], dma=True)


def st_moe(ctx):
    nc, pr, io, c = ctx.nc, ctx.pr, ctx.io, ctx.c
    pb, pbB = ctx.pb, ctx.pbB
    cB = ctx.cB
    with contextlib.ExitStack() as stk:
        xqs = _sb(ctx, stk, "xqs", [128, 8 * 1024], BF16)
        xqsB = Buf("xqs")
        w2sh = _sb(ctx, stk, "w2s", [128, 8 * 1024], BF16)
        w2shB = Buf("w2s")
        wgt = [
            dict(w0=_sb(ctx, stk, f"w0s{i}", [128, 8 * 1024], BF16),
                 w1=_sb(ctx, stk, f"w1s{i}", [128, 8 * 1024], BF16),
                 w2=w2sh, w2B=w2shB,
                 B=Buf(f"moew{i}")) for i in range(2)
        ]
        ug = [_sb(ctx, stk, "ugb0", [128, 8 * 1024], BF16)] * 2
        ugB = [Buf("ugb0")] * 2
        acc = _sb(ctx, stk, "macc", [128, 8 * 1024], F32)
        accB = Buf("macc")
        usb = [_sb(ctx, stk, f"usb{i}", [128, 512], F32) for i in range(2)]
        usbB = [Buf(f"usb{i}") for i in range(2)]
        gsb = [_sb(ctx, stk, f"gsb{i}", [128, 512], F32) for i in range(2)]
        gsbB = [Buf(f"gsb{i}") for i in range(2)]
        combQ = _sb(ctx, stk, "combQ", [128, 64], F32)
        combQB = Buf("combQ")
        rt = _sb(ctx, stk, "rt", [128, 5 * 8], F32)
        rtB = Buf("rt")
        rcol = _sb(ctx, stk, "rcol", [128, 8], F32)
        rcolB = Buf("rcol")
        lgq = _sb(ctx, stk, "lgq", [128, 64], F32)
        lgqB = Buf("lgq")
        wg_sb = _sb(ctx, stk, "wg_sb", [128, 8 * 8], BF16)
        bgb = _sb(ctx, stk, "bgb", [128, 8], F32)
        bgrow = _sb(ctx, stk, "bgrow", [1, 8], F32)
        wgB = Buf("wg")
        b0c = _sb(ctx, stk, "b0c", [128, 64], F32)
        b1c = _sb(ctx, stk, "b1c", [128, 64], F32)
        bcB = Buf("bc")
        b2a = _sb(ctx, stk, "b2a", [8, 1024], BF16)
        b2aB = Buf("b2a")
        ctb = _sb(ctx, stk, "ctb", [8, 128], BF16)
        ctbB = Buf("ctb")
        rin = [_sb(ctx, stk, f"mrin{i}", [128, 512], F32) for i in range(2)]
        rinB = [Buf(f"mrin{i}") for i in range(2)]
        zbuf = [_sb(ctx, stk, "mz0", [128, 8 * 512], F32)] * 2
        zB = [Buf("mz0")] * 2
        zbf = [_sb(ctx, stk, f"mzb{i}", [128, 512], BF16) for i in range(2)]
        zbfB = [Buf(f"mzb{i}") for i in range(2)]
        zsq = [_sb(ctx, stk, f"mzs{i}", [128, 512], BF16) for i in range(2)]
        zsqB = [Buf(f"mzs{i}") for i in range(2)]
        rows = _sb(ctx, stk, "mrows", [1, 5 * 512], F32)
        rowsB = Buf("mrows")
        yst = [_sb(ctx, stk, f"my{i}", [128, 512], F32) for i in range(2)]
        ystB = [Buf(f"my{i}") for i in range(2)]
        ybf = [_sb(ctx, stk, f"myb{i}", [128, 512], BF16) for i in range(2)]
        ybfB = [Buf(f"myb{i}") for i in range(2)]

        pr.op("sync", lambda e: e.dma_start(out=bgrow[0:1, 0:8], in_=io["L0_ffn_bg"][None, :]),
              writes=[wgB], dma=True)
        pr.op("tensor", lambda e: e.matmul(pb[4][0:128, 0:8], c["c_ones_f"][0:1, 0:128],
                                           bgrow[0:1, 0:8], start=True, stop=True),
              reads=[wgB, cB], writes=[pbB[4]])
        pr.op("vector", lambda e: e.tensor_copy(bgb[0:128, 0:8], pb[4][0:128, 0:8]),
              reads=[pbB[4]], writes=[wgB])
        for ee in range(8):
            for hs in range(8):
                pr.op("sync", (lambda e, ee=ee, hs=hs: e.dma_start(
                    out=b0c[0:128, 8 * ee + hs:8 * ee + hs + 1],
                    in_=io["L0_ffn_b0"][ee, 128 * hs:128 * (hs + 1)][:, None],
                    allow_slow_non_contiguous=True)), writes=[bcB], dma=True)
                pr.op("sync", (lambda e, ee=ee, hs=hs: e.dma_start(
                    out=b1c[0:128, 8 * ee + hs:8 * ee + hs + 1],
                    in_=io["L0_ffn_b1"][ee, 128 * hs:128 * (hs + 1)][:, None],
                    allow_slow_non_contiguous=True)), writes=[bcB], dma=True)
        pr.op("gpsimd", lambda e: e.dma_start(out=b2a[0:8, 0:1024], in_=io["L0_ffn_b2"]),
              writes=[b2aB], dma=True)

        for tq in range(4):
            pr.op("sync", (lambda e, tq=tq: e.dma_start(
                out=xqs[0:128, 0:8192].rearrange("p (k t) -> p k t", k=8),
                in_=ctx.dram["r1_bf"][0:1024, 1024 * tq:1024 * (tq + 1)].rearrange(
                    "(k p) t -> p k t", p=128))),
                reads=[ctx.bufs["r1_bf"]], writes=[xqsB], dma=True)
            for tt in range(8):
                pr.op("sync", (lambda e, tt=tt, tq=tq: e.dma_start(
                    out=lgq[0:128, 8 * tt:8 * (tt + 1)],
                    in_=ctx.dram["lgts"][1024 * tq + 128 * tt:1024 * tq + 128 * (tt + 1), 0:8])),
                    reads=[ctx.bufs["lgts"]], writes=[lgqB], dma=True)
            for tt in range(8):
                pr.op("vector", (lambda e, tt=tt: e.tensor_add(
                    rt[0:128, 0:8], bgb[0:128, 0:8], lgq[0:128, 8 * tt:8 * (tt + 1)])),
                      reads=[wgB, lgqB], writes=[rtB])
                pr.op("vector", lambda e: e.reduce_max(rcol[0:128, 0:1], rt[0:128, 0:8], AX.X),
                      reads=[rtB], writes=[rcolB])
                pr.op("vector", lambda e: e.tensor_scalar(rt[0:128, 8:16], rt[0:128, 0:8],
                                                          rcol[0:128, 0:1], None, ALU.is_equal),
                      reads=[rtB, rcolB], writes=[rtB])
                pr.op("vector", lambda e: e.scalar_tensor_tensor(
                    rt[0:128, 16:24], rt[0:128, 8:16], -1e30, rt[0:128, 0:8],
                    ALU.mult, ALU.add), reads=[rtB], writes=[rtB])
                pr.op("vector", lambda e: e.reduce_max(rcol[0:128, 1:2], rt[0:128, 16:24], AX.X),
                      reads=[rtB, rcolB], writes=[rcolB])
                pr.op("vector", lambda e: e.tensor_scalar(rt[0:128, 24:32], rt[0:128, 16:24],
                                                          rcol[0:128, 1:2], None, ALU.is_equal),
                      reads=[rtB, rcolB], writes=[rtB])
                pr.op("vector", lambda e: e.tensor_scalar(rcol[0:128, 4:5], rcol[0:128, 0:1],
                                                          rcol[0:128, 1:2], None, ALU.subtract),
                      reads=[rcolB], writes=[rcolB])
                pr.op("scalar", lambda e: e.activation(rcol[0:128, 2:3], rcol[0:128, 4:5],
                                                       AF.Sigmoid),
                      reads=[rcolB], writes=[rcolB])
                pr.op("vector", lambda e: e.tensor_scalar(rcol[0:128, 3:4], rcol[0:128, 2:3],
                                                          -1.0, 1.0, ALU.mult, ALU.add),
                      reads=[rcolB], writes=[rcolB])
                pr.op("vector", lambda e: e.tensor_scalar(rt[0:128, 32:40], rt[0:128, 8:16],
                                                          rcol[0:128, 2:3], None, ALU.mult),
                      reads=[rtB, rcolB], writes=[rtB])
                pr.op("vector", (lambda e, tt=tt: e.scalar_tensor_tensor(
                    combQ[0:128, 8 * tt:8 * (tt + 1)], rt[0:128, 24:32], rcol[0:128, 3:4],
                    rt[0:128, 32:40], ALU.mult, ALU.add)),
                    reads=[rtB, rcolB], writes=[combQB])
            pr.op("vector", lambda e: e.memset(acc[0:128, 0:8192], 0.0), writes=[accB])

            sets = [(ee, g) for ee in range(8) for g in range(4)]
            for si, (ee, g) in enumerate(sets):
                wb = wgt[si % 2]
                pr.op("gpsimd", (lambda e, ee=ee, g=g, wb=wb: e.dma_start(
                    out=wb["w0"][0:128, 0:8192].rearrange("p (k h) -> p k h", k=8),
                    in_=io["L0_ffn_W0"][ee, :, 1024 * g:1024 * (g + 1)].rearrange(
                        "(k p) h -> p k h", p=128))), writes=[wb["B"]], dma=True)
                pr.op("gpsimd", (lambda e, ee=ee, g=g, wb=wb: e.dma_start(
                    out=wb["w1"][0:128, 0:8192].rearrange("p (k h) -> p k h", k=8),
                    in_=io["L0_ffn_W1"][ee, :, 1024 * g:1024 * (g + 1)].rearrange(
                        "(k p) h -> p k h", p=128))), writes=[wb["B"]], dma=True)
                pr.op("gpsimd", (lambda e, ee=ee, g=g, wb=wb: e.dma_start(
                    out=wb["w2"][0:128, 0:8192].rearrange("p (k h) -> p k h", k=8),
                    in_=io["L0_ffn_W2"][ee, 1024 * g:1024 * (g + 1), :].rearrange(
                        "(k p) h -> p k h", p=128))), writes=[wb["w2B"]], dma=True)
                for idx in range(16):
                    hs, th = idx // 2, idx % 2
                    pu = pb[(idx % 2) * 2]
                    pg = pb[(idx % 2) * 2 + 1]
                    puB = pbB[(idx % 2) * 2]
                    pgB = pbB[(idx % 2) * 2 + 1]
                    for k in range(8):
                        pr.op("tensor", (lambda e, k=k, hs=hs, th=th, wb=wb, pu=pu: e.matmul(
                            pu[0:128, 0:512],
                            wb["w0"][0:128, 1024 * k + 128 * hs:1024 * k + 128 * (hs + 1)],
                            xqs[0:128, 1024 * k + 512 * th:1024 * k + 512 * (th + 1)],
                            start=(k == 0), stop=(k == 7))),
                            reads=[wb["B"], xqsB], writes=[puB])
                    for k in range(8):
                        pr.op("tensor", (lambda e, k=k, hs=hs, th=th, wb=wb, pg=pg: e.matmul(
                            pg[0:128, 0:512],
                            wb["w1"][0:128, 1024 * k + 128 * hs:1024 * k + 128 * (hs + 1)],
                            xqs[0:128, 1024 * k + 512 * th:1024 * k + 512 * (th + 1)],
                            start=(k == 0), stop=(k == 7))),
                            reads=[wb["B"], xqsB], writes=[pgB])
                    ub = usb[idx % 2]; gb = gsb[idx % 2]
                    pr.op("vector", (lambda e, ee=ee, hs=hs, ub=ub, pu=pu: e.tensor_scalar(
                        ub[0:128, 0:512], pu[0:128, 0:512],
                        b0c[0:128, 8 * ee + hs:8 * ee + hs + 1], None, ALU.add)),
                        reads=[puB, bcB], writes=[usbB[idx % 2]])
                    pr.op("scalar", (lambda e, ee=ee, hs=hs, gb=gb, pg=pg: e.activation(
                        gb[0:128, 0:512], pg[0:128, 0:512], AF.Silu,
                        bias=b1c[0:128, 8 * ee + hs:8 * ee + hs + 1])),
                        reads=[pgB, bcB], writes=[gsbB[idx % 2]])
                    pr.op("vector", (lambda e, hs=hs, th=th, ub=ub, gb=gb, u=ug[si % 2]:
                                     e.tensor_mul(
                        u[0:128, 1024 * hs + 512 * th:1024 * hs + 512 * (th + 1)],
                        ub[0:128, 0:512], gb[0:128, 0:512])),
                        reads=[usbB[idx % 2], gsbB[idx % 2]], writes=[ugB[si % 2]])
                for tt in range(8):
                    for ch in range(2):
                        py = pb[4 + ((tt * 2 + ch) % 2)]
                        pyB = pbB[4 + ((tt * 2 + ch) % 2)]
                        for hs in range(8):
                            pr.op("tensor", (lambda e, hs=hs, tt=tt, ch=ch, wb=wb,
                                             u=ug[si % 2], py=py: e.matmul(
                                py[0:128, 0:512],
                                u[0:128, 1024 * hs + 128 * tt:1024 * hs + 128 * (tt + 1)],
                                wb["w2"][0:128, 1024 * hs + 512 * ch:1024 * hs + 512 * (ch + 1)],
                                start=(hs == 0), stop=(hs == 7))),
                                reads=[ugB[si % 2], wb["w2B"]], writes=[pyB])
                        pr.op("vector", (lambda e, tt=tt, ch=ch, ee=ee, py=py:
                                         e.scalar_tensor_tensor(
                            acc[0:128, 1024 * tt + 512 * ch:1024 * tt + 512 * (ch + 1)],
                            py[0:128, 0:512], combQ[0:128, 8 * tt + ee:8 * tt + ee + 1],
                            acc[0:128, 1024 * tt + 512 * ch:1024 * tt + 512 * (ch + 1)],
                            ALU.mult, ALU.add)),
                            reads=[pyB, combQB, accB], writes=[accB])
            if tq == 0 and "moe" in ctx.dbg:
                pr.op("sync", lambda e: e.dma_start(out=ctx.dram["dbg_moe"][0:128, :],
                                                    in_=acc[0:128, 0:8192]),
                      reads=[accB], writes=[ctx.bufs["dbg_moe"]], dma=True)
                pr.op("sync", lambda e: e.dma_start(out=ctx.dram["dbg_cmb"][0:128, :],
                                                    in_=combQ[0:128, 0:64]),
                      reads=[combQB], writes=[ctx.bufs["dbg_cmb"]], dma=True)
            for tt in range(8):
                pr.op("tensor", (lambda e, tt=tt: e.transpose(
                    pb[0][0:8, 0:128], combQ[0:128, 8 * tt:8 * (tt + 1)],
                    c["c_id_f"][0:128, 0:128])), reads=[combQB, cB], writes=[pbB[0]])
                pr.op("vector", lambda e: e.tensor_copy(ctb[0:8, 0:128], pb[0][0:8, 0:128]),
                      reads=[pbB[0]], writes=[ctbB])
                for ch in range(2):
                    pr.op("tensor", (lambda e, ch=ch: e.matmul(
                        pb[1][0:128, 0:512], ctb[0:8, 0:128],
                        b2a[0:8, 512 * ch:512 * (ch + 1)], start=True, stop=True)),
                        reads=[ctbB, b2aB], writes=[pbB[1]])
                    pr.op("vector", (lambda e, tt=tt, ch=ch: e.tensor_add(
                        acc[0:128, 1024 * tt + 512 * ch:1024 * tt + 512 * (ch + 1)],
                        acc[0:128, 1024 * tt + 512 * ch:1024 * tt + 512 * (ch + 1)],
                        pb[1][0:128, 0:512])), reads=[pbB[1], accB], writes=[accB])
            for tch in range(2):
                tc = 2 * tq + tch
                bi = tch % 2
                for m in range(8):
                    mb = m % 2
                    pz = pb[m % 2]
                    pzB = pbB[m % 2]
                    pr.op("sync", (lambda e, m=m, tc=tc, t=rin[mb]: e.dma_start(
                        out=t[0:128, 0:512],
                        in_=ctx.dram["r1_32"][128 * m:128 * (m + 1), 512 * tc:512 * (tc + 1)])),
                        reads=[ctx.bufs["r1_32"]], writes=[rinB[mb]], dma=True)
                    for ti in range(4):
                        pr.op("tensor", (lambda e, m=m, ti=ti, tch=tch, pz=pz: e.transpose(
                            pz[0:128, 128 * ti:128 * (ti + 1)],
                            acc[0:128, 1024 * (4 * tch + ti) + 128 * m:
                                1024 * (4 * tch + ti) + 128 * (m + 1)],
                            c["c_id_f"][0:128, 0:128])),
                            reads=[accB, cB], writes=[pzB])
                    pr.op("vector", (lambda e, m=m, t=rin[mb], z=zbuf[bi], pz=pz: e.tensor_add(
                        z[0:128, 512 * m:512 * (m + 1)], t[0:128, 0:512],
                        pz[0:128, 0:512])), reads=[rinB[mb], pzB], writes=[zB[bi]])
                    pr.op("vector", (lambda e, m=m, z=zbuf[bi], zb=zbf[mb]: e.tensor_copy(
                        zb[0:128, 0:512], z[0:128, 512 * m:512 * (m + 1)])),
                        reads=[zB[bi]], writes=[zbfB[mb]])
                    pr.op("scalar", (lambda e, m=m, zb=zbf[mb], zs=zsq[mb]: e.activation(
                        zs[0:128, 0:512], zb[0:128, 0:512],
                        AF.Square)), reads=[zbfB[mb]], writes=[zsqB[mb]])
                    pr.op("tensor", (lambda e, m=m, zb=zbf[mb]: e.matmul(
                        pb[2][0:1, 0:512], c["c_onescol_bf"][0:128, 0:1],
                        zb[0:128, 0:512], start=(m == 0), stop=(m == 7))),
                        reads=[zbfB[mb], cB], writes=[pbB[2]])
                    pr.op("tensor", (lambda e, m=m, zs=zsq[mb]: e.matmul(
                        pb[3][0:1, 0:512], c["c_onescol_bf"][0:128, 0:1],
                        zs[0:128, 0:512], start=(m == 0), stop=(m == 7))),
                        reads=[zsqB[mb], cB], writes=[pbB[3]])
                pr.op("vector", lambda e: e.tensor_scalar(
                    rows[0:1, 0:512], pb[2][0:1, 0:512], 1.0 / 1024, None, ALU.mult),
                    reads=[pbB[2]], writes=[rowsB])
                pr.op("scalar", lambda e: e.activation(
                    rows[0:1, 512:1024], rows[0:1, 0:512], AF.Square),
                    reads=[rowsB], writes=[rowsB])
                pr.op("vector", lambda e: e.scalar_tensor_tensor(
                    rows[0:1, 1024:1536], pb[3][0:1, 0:512], 1.0 / 1024, rows[0:1, 512:1024],
                    ALU.mult, ALU.subtract), reads=[pbB[3], rowsB], writes=[rowsB])
                pr.op("scalar", lambda e: e.activation(
                    rows[0:1, 1536:2048], rows[0:1, 1024:1536], AF.Ln, bias=1e-5),
                    reads=[rowsB, cB], writes=[rowsB])
                pr.op("scalar", lambda e: e.activation(
                    rows[0:1, 2048:2560], rows[0:1, 1536:2048], AF.Exp, scale=-0.5),
                    reads=[rowsB], writes=[rowsB])
                pr.op("tensor", lambda e: e.matmul(
                    pb[4][0:128, 0:512], c["c_ones_f"][0:1, 0:128], rows[0:1, 0:512],
                    start=True, stop=True), reads=[rowsB, cB], writes=[pbB[4]])
                pr.op("tensor", lambda e: e.matmul(
                    pb[5][0:128, 0:512], c["c_ones_f"][0:1, 0:128], rows[0:1, 2048:2560],
                    start=True, stop=True), reads=[rowsB, cB], writes=[pbB[5]])
                for m in range(8):
                    yb = m % 2
                    pr.op("vector", (lambda e, m=m, z=zbuf[bi], y=yst[yb]: e.tensor_sub(
                        y[0:128, 0:512], z[0:128, 512 * m:512 * (m + 1)], pb[4][0:128, 0:512])),
                        reads=[zB[bi], pbB[4]], writes=[ystB[yb]])
                    pr.op("vector", (lambda e, m=m, y=yst[yb]: e.tensor_mul(
                        y[0:128, 0:512], y[0:128, 0:512], pb[5][0:128, 0:512])),
                        reads=[ystB[yb], pbB[5]], writes=[ystB[yb]])
                    pr.op("vector", (lambda e, m=m, y=yst[yb], b=ybf[yb]: e.tensor_copy(
                        b[0:128, 0:512], y[0:128, 0:512])), reads=[ystB[yb]], writes=[ybfB[yb]])
                    pr.op("sync", (lambda e, m=m, tc=tc, y=yst[yb]: e.dma_start(
                        out=ctx.dram["r2_32"][128 * m:128 * (m + 1), 512 * tc:512 * (tc + 1)],
                        in_=y[0:128, 0:512])), reads=[ystB[yb]], writes=[ctx.bufs["r2_32"]],
                        dma=True)
                    pr.op("sync", (lambda e, m=m, tc=tc, b=ybf[yb]: e.dma_start(
                        out=ctx.dram["r2_bf"][128 * m:128 * (m + 1), 512 * tc:512 * (tc + 1)],
                        in_=b[0:128, 0:512])), reads=[ybfB[yb]], writes=[ctx.bufs["r2_bf"]],
                        dma=True)
        pr.fence()


def st_l1attn(ctx):
    nc, pr, io, c = ctx.nc, ctx.pr, ctx.io, ctx.c
    pb, pbB = ctx.pb, ctx.pbB
    pt, ptB = ctx.pt, ctx.ptB
    cB = ctx.cB
    for wn, dst in (("L1_ffn_w1_w", "wbf_w1"), ("L1_ffn_w2_w", "wbf_w2"),
                    ("L1_ffn_h1_w", "wbf_h1"), ("L1_ffn_h2_w", "wbf_h2"),
                    ("L1_ffn_w3_w", "wbf_w3")):
        pr.op("gpsimd", (lambda e, wn=wn, dst=dst: e.dma_start(
            out=ctx.dram[dst][:, :], in_=io[wn][:, :])),
            writes=[ctx.bufs[dst]], dma=True)

    with contextlib.ExitStack() as stk:
        g1c, g1B = _col_loads(ctx, pr, stk, "g1c",
                              lambda i: io["L1_attn_ln1_g"][128 * i:128 * (i + 1)][:, None],
                              8, "g1c")
        b1c, b1B = _col_loads(ctx, pr, stk, "b1c",
                              lambda i: io["L1_attn_ln1_b"][128 * i:128 * (i + 1)][:, None],
                              8, "b1c2")
        rwc, rwB = _col_loads(ctx, pr, stk, "rwc",
                              lambda i: io["L1_attn_rms_in_w"][128 * i:128 * (i + 1)][:, None],
                              8, "rwc")
        rbf_in = [_sb(ctx, stk, f"arin{i}", [128, 8 * 512], BF16) for i in range(2)]
        rbfB = [Buf(f"arin{i}") for i in range(2)]
        r32_in = [_sb(ctx, stk, f"ar32{i}", [128, 8 * 512], F32) for i in range(2)]
        r32B = [Buf(f"ar32{i}") for i in range(2)]
        zsq = [_sb(ctx, stk, f"azs{i}", [128, 512], BF16) for i in range(2)]
        zsqB = [Buf(f"azs{i}") for i in range(2)]
        tbuf = _sb(ctx, stk, "atb", [128, 8 * 512], F32)
        tB = Buf("atb")
        tbf = [_sb(ctx, stk, f"atbf{i}", [128, 512], BF16) for i in range(2)]
        tbfB = [Buf(f"atbf{i}") for i in range(2)]
        rows = _sb(ctx, stk, "arows", [1, 5 * 512], F32)
        rowsB = Buf("arows")
        rows2 = _sb(ctx, stk, "arows2", [1, 2 * 512], F32)
        rows2B = Buf("arows2")
        hst = [_sb(ctx, stk, f"ahs{i}", [128, 512], BF16) for i in range(2)]
        hstB = [Buf(f"ahs{i}") for i in range(2)]
        for tc in range(8):
            bi2 = tc % 2
            pr.op("sync", (lambda e, tc=tc, t=rbf_in[bi2]: e.dma_start(
                out=t[0:128, 0:8 * 512].rearrange("p (k t) -> p k t", k=8),
                in_=ctx.dram["r2_bf"][0:1024, 512 * tc:512 * (tc + 1)].rearrange(
                    "(k p) t -> p k t", p=128))),
                reads=[ctx.bufs["r2_bf"]], writes=[rbfB[bi2]], dma=True)
            pr.op("sync", (lambda e, tc=tc, t=r32_in[bi2]: e.dma_start(
                out=t[0:128, 0:8 * 512].rearrange("p (k t) -> p k t", k=8),
                in_=ctx.dram["r2_32"][0:1024, 512 * tc:512 * (tc + 1)].rearrange(
                    "(k p) t -> p k t", p=128))),
                reads=[ctx.bufs["r2_32"]], writes=[r32B[bi2]], dma=True)
            for m in range(8):
                mb = m % 2
                pr.op("scalar", (lambda e, m=m, t=rbf_in[bi2], zs=zsq[mb]: e.activation(
                    zs[0:128, 0:512], t[0:128, 512 * m:512 * (m + 1)], AF.Square)),
                    reads=[rbfB[bi2]], writes=[zsqB[mb]])
                pr.op("tensor", (lambda e, m=m, t=rbf_in[bi2]: e.matmul(
                    pb[2][0:1, 0:512], c["c_onescol_bf"][0:128, 0:1],
                    t[0:128, 512 * m:512 * (m + 1)], start=(m == 0), stop=(m == 7))),
                    reads=[rbfB[bi2], cB], writes=[pbB[2]])
                pr.op("tensor", (lambda e, m=m, zs=zsq[mb]: e.matmul(
                    pb[3][0:1, 0:512], c["c_onescol_bf"][0:128, 0:1],
                    zs[0:128, 0:512], start=(m == 0), stop=(m == 7))),
                    reads=[zsqB[mb], cB], writes=[pbB[3]])
            _ln_rows(ctx, pr, rows, rowsB, 2, 3, 1e-5, 1024)
            pr.op("tensor", lambda e: e.matmul(
                pb[4][0:128, 0:512], c["c_ones_f"][0:1, 0:128], rows[0:1, 0:512],
                start=True, stop=True), reads=[rowsB, cB], writes=[pbB[4]])
            pr.op("tensor", lambda e: e.matmul(
                pb[5][0:128, 0:512], c["c_ones_f"][0:1, 0:128], rows[0:1, 2048:2560],
                start=True, stop=True), reads=[rowsB, cB], writes=[pbB[5]])
            for m in range(8):
                mb = m % 2
                pr.op("vector", (lambda e, m=m, t=r32_in[bi2]: e.tensor_sub(
                    tbuf[0:128, 512 * m:512 * (m + 1)], t[0:128, 512 * m:512 * (m + 1)],
                    pb[4][0:128, 0:512])), reads=[r32B[bi2], pbB[4]], writes=[tB])
                pr.op("vector", (lambda e, m=m: e.tensor_mul(
                    tbuf[0:128, 512 * m:512 * (m + 1)],
                    tbuf[0:128, 512 * m:512 * (m + 1)], pb[5][0:128, 0:512])),
                    reads=[tB, pbB[5]], writes=[tB])
                pr.op("vector", (lambda e, m=m: e.tensor_scalar(
                    tbuf[0:128, 512 * m:512 * (m + 1)],
                    tbuf[0:128, 512 * m:512 * (m + 1)],
                    g1c[0:128, m:m + 1], b1c[0:128, m:m + 1], ALU.mult, ALU.add)),
                    reads=[tB, g1B, b1B], writes=[tB])
                pr.op("vector", (lambda e, m=m, zb=tbf[mb]: e.tensor_copy(
                    zb[0:128, 0:512], tbuf[0:128, 512 * m:512 * (m + 1)])),
                    reads=[tB], writes=[tbfB[mb]])
                pr.op("scalar", (lambda e, m=m, zb=tbf[mb], zs=zsq[mb]: e.activation(
                    zs[0:128, 0:512], zb[0:128, 0:512], AF.Square)),
                    reads=[tbfB[mb]], writes=[zsqB[mb]])
                pr.op("tensor", (lambda e, m=m, zs=zsq[mb]: e.matmul(
                    pb[3][0:1, 0:512], c["c_onescol_bf"][0:128, 0:1],
                    zs[0:128, 0:512], start=(m == 0), stop=(m == 7))),
                    reads=[zsqB[mb], cB], writes=[pbB[3]])
            pr.op("scalar", lambda e: e.activation(
                rows2[0:1, 0:512], pb[3][0:1, 0:512], AF.Ln, scale=1.0 / 1024, bias=1e-6),
                reads=[pbB[3], cB], writes=[rows2B])
            pr.op("scalar", lambda e: e.activation(
                rows2[0:1, 512:1024], rows2[0:1, 0:512], AF.Exp, scale=-0.5),
                reads=[rows2B], writes=[rows2B])
            pr.op("tensor", lambda e: e.matmul(
                pb[5][0:128, 0:512], c["c_ones_f"][0:1, 0:128], rows2[0:1, 512:1024],
                start=True, stop=True), reads=[rows2B, cB], writes=[pbB[5]])
            for m in range(8):
                mb = m % 2
                pr.op("vector", (lambda e, m=m, hs=hst[mb]: e.tensor_mul(
                    hs[0:128, 0:512], tbuf[0:128, 512 * m:512 * (m + 1)],
                    pb[5][0:128, 0:512])), reads=[tB, pbB[5]], writes=[hstB[mb]])
                pr.op("vector", (lambda e, m=m, hs=hst[mb]: e.tensor_scalar(
                    hs[0:128, 0:512], hs[0:128, 0:512], rwc[0:128, m:m + 1], None,
                    ALU.mult)), reads=[hstB[mb], rwB], writes=[hstB[mb]])
                pr.op("sync", (lambda e, m=m, tc=tc, hs=hst[mb]: e.dma_start(
                    out=ctx.dram["hTbf"][128 * m:128 * (m + 1), 512 * tc:512 * (tc + 1)],
                    in_=hs[0:128, 0:512])), reads=[hstB[mb]], writes=[ctx.bufs["hTbf"]],
                    dma=True)
        pr.fence()

    CHUNK = 128
    NCH = T // CHUNK
    with contextlib.ExitStack() as stko:
        dtA = _sb(ctx, stko, "dtA", [1, 8], F32)
        dtAB = Buf("dtA")
        pr.op("sync", lambda e: e.dma_start(out=dtA[0:1, 0:4], in_=io["L1_attn_dt_bias"][None, :]),
              writes=[dtAB], dma=True)
        pr.op("sync", lambda e: e.dma_start(out=dtA[0:1, 4:8], in_=io["L1_attn_A_log"][None, :]),
              writes=[dtAB], dma=True)
        pr.op("scalar", lambda e: e.activation(dtA[0:1, 4:8], dtA[0:1, 4:8], AF.Exp),
              reads=[dtAB], writes=[dtAB])
        pr.op("vector", lambda e: e.tensor_scalar(dtA[0:1, 4:8], dtA[0:1, 4:8], -1.0,
                                                  None, ALU.mult),
              reads=[dtAB], writes=[dtAB])
        arow = _sb(ctx, stko, "arow", [1, T], F32)
        arowB = Buf("arow")
        brow = _sb(ctx, stko, "brow", [1, T], F32)
        browB = Buf("brow")
        nbrow = _sb(ctx, stko, "nbrow", [1, T], F32)
        nbrowB = Buf("nbrow")
        Grow = _sb(ctx, stko, "Grow", [1, T], F32)
        GrowB = Buf("Grow")
        zrow = _sb(ctx, stko, "zrow", [1, 128], F32)
        zrowB = Buf("zr0")
        pr.op("vector", lambda e: e.memset(zrow[0:1, 0:128], 0.0), writes=[zrowB])
        S32 = _sb(ctx, stko, "S32", [128, 2 * 512], F32)
        S32B = Buf("S32")
        Sbf = _sb(ctx, stko, "Sbf", [128, 2 * 512], BF16)
        SbfB = Buf("Sbf")

        HB = {k: Buf(k) for k in (
            "wh", "wacol", "wbcol", "cwq", "cwk", "cwv", "qp", "kp", "vp",
            "htc0", "htc1", "gst0", "gst1", "cacc", "tka", "tsq", "scc", "tst",
            "tmpr", "qnti0", "qnti1", "knti0", "knti1", "knki0", "knki1",
            "vti0", "vti1", "gti0", "gti1", "gcol2", "gcb", "sc", "eaf", "ewf",
            "tchf", "nab", "natb", "rtn0", "rtn1", "snn0", "snn1", "stn0", "stn1",
            "ut", "ubf", "dbf", "o1t", "wtb", "ot", "osq", "ofin", "kdb", "otst")}
        for h in range(4):
            with contextlib.ExitStack() as stk:
                wq_h = _sb(ctx, stk, "wq_h", [128, 8 * 256], BF16)
                wk_h = _sb(ctx, stk, "wk_h", [128, 8 * 256], BF16)
                wv_h = _sb(ctx, stk, "wv_h", [128, 8 * 512], BF16)
                wg_h = _sb(ctx, stk, "wg_h", [128, 8 * 512], BF16)
                whB = HB["wh"]
                for k in range(8):
                    pr.op("gpsimd", (lambda e, k=k, h=h: e.dma_start(
                        out=wq_h[0:128, 256 * k:256 * (k + 1)],
                        in_=io["L1_attn_Wq"][128 * k:128 * (k + 1), 256 * h:256 * (h + 1)])),
                        writes=[whB], dma=True)
                    pr.op("gpsimd", (lambda e, k=k, h=h: e.dma_start(
                        out=wk_h[0:128, 256 * k:256 * (k + 1)],
                        in_=io["L1_attn_Wk"][128 * k:128 * (k + 1), 256 * h:256 * (h + 1)])),
                        writes=[whB], dma=True)
                    pr.op("gpsimd", (lambda e, k=k, h=h: e.dma_start(
                        out=wv_h[0:128, 512 * k:512 * (k + 1)],
                        in_=io["L1_attn_Wv"][128 * k:128 * (k + 1), 512 * h:512 * (h + 1)])),
                        writes=[whB], dma=True)
                    pr.op("gpsimd", (lambda e, k=k, h=h: e.dma_start(
                        out=wg_h[0:128, 512 * k:512 * (k + 1)],
                        in_=io["L1_attn_Wg"][128 * k:128 * (k + 1), 512 * h:512 * (h + 1)])),
                        writes=[whB], dma=True)
                wacol, waB = _col_loads(ctx, pr, stk, "wacol",
                                        lambda i: io["L1_attn_Wa"][128 * i:128 * (i + 1), h:h + 1],
                                        8, "wacol", buf=HB["wacol"], dt=BF16)
                wbcol, wbB = _col_loads(ctx, pr, stk, "wbcol",
                                        lambda i: io["L1_attn_Wb"][128 * i:128 * (i + 1), h:h + 1],
                                        8, "wbcol", buf=HB["wbcol"], dt=BF16)
                cwq, cwqB = _col_loads(ctx, pr, stk, "cwq",
                                       lambda i: io["L1_attn_conv_q_w"]
                                       [256 * h + 128 * (i // 4):256 * h + 128 * (i // 4 + 1),
                                        i % 4:i % 4 + 1], 8, "cwq", buf=HB["cwq"])
                cwk, cwkB = _col_loads(ctx, pr, stk, "cwk",
                                       lambda i: io["L1_attn_conv_k_w"]
                                       [256 * h + 128 * (i // 4):256 * h + 128 * (i // 4 + 1),
                                        i % 4:i % 4 + 1], 8, "cwk", buf=HB["cwk"])
                cwv, cwvB = _col_loads(ctx, pr, stk, "cwv",
                                       lambda i: io["L1_attn_conv_v_w"]
                                       [512 * h + 128 * (i // 4):512 * h + 128 * (i // 4 + 1),
                                        i % 4:i % 4 + 1], 16, "cwv", buf=HB["cwv"])
                PW = 4099
                qp = _sb(ctx, stk, "qp", [128, 2 * PW], BF16)
                qpB = HB["qp"]
                kp = _sb(ctx, stk, "kp", [128, 2 * PW], BF16)
                kpB = HB["kp"]
                vp = _sb(ctx, stk, "vp", [128, 4 * PW], BF16)
                vpB = HB["vp"]
                pr.op("vector", lambda e: e.memset(qp[0:128, 0:2 * PW], 0.0), writes=[qpB])
                pr.op("vector", lambda e: e.memset(kp[0:128, 0:2 * PW], 0.0), writes=[kpB])
                pr.op("vector", lambda e: e.memset(vp[0:128, 0:4 * PW], 0.0), writes=[vpB])
                htc = [_sb(ctx, stk, f"htc{i}", [128, 8 * 512], BF16) for i in range(2)]
                htcB = [HB["htc0"], HB["htc1"]]
                gst = [_sb(ctx, stk, f"gst{i}", [128, 512], BF16) for i in range(2)]
                gstB = [HB["gst0"], HB["gst1"]]
                for tc in range(8):
                    bi = tc % 2
                    pr.op("sync", (lambda e, tc=tc, t=htc[bi]: e.dma_start(
                        out=t[0:128, 0:8 * 512].rearrange("p (k t) -> p k t", k=8),
                        in_=ctx.dram["hTbf"][0:1024, 512 * tc:512 * (tc + 1)].rearrange(
                            "(k p) t -> p k t", p=128))),
                        reads=[ctx.bufs["hTbf"]], writes=[htcB[bi]], dma=True)
                    for (wt, nsub, dst, dstB) in ((wq_h, 2, qp, qpB), (wk_h, 2, kp, kpB),
                                                  (wv_h, 4, vp, vpB)):
                        for sub in range(nsub):
                            pp = pb[sub % 2]
                            ppB = pbB[sub % 2]
                            for k in range(8):
                                pr.op("tensor", (lambda e, k=k, sub=sub, wt=wt, nsub=nsub,
                                                 t=htc[bi], pp=pp: e.matmul(
                                    pp[0:128, 0:512],
                                    wt[0:128, 128 * nsub * k + 128 * sub:
                                       128 * nsub * k + 128 * (sub + 1)],
                                    t[0:128, 512 * k:512 * (k + 1)],
                                    start=(k == 0), stop=(k == 7))),
                                    reads=[whB, htcB[bi]], writes=[ppB])
                            pr.op("vector", (lambda e, sub=sub, tc=tc, dst=dst, pp=pp:
                                             e.tensor_copy(
                                dst[0:128, PW * sub + 3 + 512 * tc:PW * sub + 3 + 512 * (tc + 1)],
                                pp[0:128, 0:512])), reads=[ppB], writes=[dstB])
                    for ts in range(4):
                        pg = pb[2 + ts % 2]
                        pgB = pbB[2 + ts % 2]
                        for k in range(8):
                            pr.op("tensor", (lambda e, k=k, ts=ts, t=htc[bi], pg=pg: e.matmul(
                                pg[0:128, 0:512],
                                t[0:128, 512 * k + 128 * ts:512 * k + 128 * (ts + 1)],
                                wg_h[0:128, 512 * k:512 * (k + 1)],
                                start=(k == 0), stop=(k == 7))),
                                reads=[whB, htcB[bi]], writes=[pgB])
                        pr.op("scalar", (lambda e, ts=ts, g=gst[ts % 2], pg=pg: e.activation(
                            g[0:128, 0:512], pg[0:128, 0:512], AF.Silu)),
                            reads=[pgB], writes=[gstB[ts % 2]])
                        pr.op("sync", (lambda e, ts=ts, tc=tc, h=h, g=gst[ts % 2]: e.dma_start(
                            out=ctx.dram["gate_d"][512 * tc + 128 * ts:512 * tc + 128 * (ts + 1),
                                                   0:512],
                            in_=g[0:128, 0:512])), reads=[gstB[ts % 2]],
                            writes=[ctx.bufs["gate_d"]], dma=True)
                    for (wc, wcB, dst, dstB, bias) in ((wacol, waB, arow, arowB, True),
                                                       (wbcol, wbB, brow, browB, False)):
                        for k in range(8):
                            pr.op("tensor", (lambda e, k=k, wc=wc, t=htc[bi]: e.matmul(
                                pb[4][0:1, 0:512], wc[0:128, k:k + 1],
                                t[0:128, 512 * k:512 * (k + 1)],
                                start=(k == 0), stop=(k == 7))),
                                reads=[wcB, htcB[bi]], writes=[pbB[4]])
                        if bias:
                            pr.op("vector", (lambda e, tc=tc, dst=dst: e.tensor_scalar(
                                dst[0:1, 512 * tc:512 * (tc + 1)], pb[4][0:1, 0:512],
                                dtA[0:1, h:h + 1], None, ALU.add)),
                                reads=[pbB[4], dtAB], writes=[dstB])
                        else:
                            pr.op("vector", (lambda e, tc=tc, dst=dst: e.tensor_copy(
                                dst[0:1, 512 * tc:512 * (tc + 1)], pb[4][0:1, 0:512])),
                                reads=[pbB[4]], writes=[dstB])
                if h == 0 and "ch" in ctx.dbg:
                    pr.op("gpsimd", lambda e: e.dma_start(
                        out=ctx.dram["dbg_ch"][0:128, 8848:8856], in_=wacol[0:128, 0:8]),
                        reads=[waB], writes=[ctx.bufs["dbg_ch"]], dma=True)
                    pr.op("gpsimd", (lambda e, t=htc[1]: e.dma_start(
                        out=ctx.dram["dbg_ch"][0:128, 8856:9368], in_=t[0:128, 0:512])),
                        reads=[htcB[1]], writes=[ctx.bufs["dbg_ch"]], dma=True)
                    pr.op("sync", lambda e: e.dma_start(
                        out=ctx.dram["dbg_ch"][0:1, 9368:9880], in_=arow[0:1, 0:512]),
                        reads=[arowB], writes=[ctx.bufs["dbg_ch"]], dma=True)
                cacc = _sb(ctx, stk, "cacc", [128, T], F32)
                caccB = HB["cacc"]
                tka = _sb(ctx, stk, "tka", [128, 512], BF16)
                tkaB = HB["tka"]
                tsq = _sb(ctx, stk, "tsq", [128, 512], BF16)
                tsqB = HB["tsq"]
                scc = _sb(ctx, stk, "scc", [128, 4], F32)
                sccB = HB["scc"]
                tst = _sb(ctx, stk, "tst", [128, 512], BF16)
                tstB = HB["tst"]
                for (src, srcB, nsub, wcol, wcolB, kind) in (
                        (qp, qpB, 2, cwq, cwqB, "q"), (kp, kpB, 2, cwk, cwkB, "k"),
                        (vp, vpB, 4, cwv, cwvB, "v")):
                    for sub in range(nsub):
                        base = PW * sub
                        pr.op("vector", (lambda e, base=base, src=src, wcol=wcol, sub=sub:
                                         e.tensor_scalar(
                            cacc[0:128, 0:T], src[0:128, base:base + T],
                            wcol[0:128, 4 * sub:4 * sub + 1], None, ALU.mult)),
                            reads=[srcB, wcolB], writes=[caccB])
                        for i in range(1, 4):
                            pr.op("vector", (lambda e, base=base, i=i, src=src, wcol=wcol,
                                             sub=sub: e.scalar_tensor_tensor(
                                cacc[0:128, 0:T], src[0:128, base + i:base + i + T],
                                wcol[0:128, 4 * sub + i:4 * sub + i + 1],
                                cacc[0:128, 0:T], ALU.mult, ALU.add)),
                                reads=[srcB, wcolB, caccB], writes=[caccB])
                        pr.op("scalar", (lambda e, base=base, src=src: e.activation(
                            src[0:128, base + 3:base + 3 + T], cacc[0:128, 0:T], AF.Silu)),
                            reads=[caccB], writes=[srcB])
                    for j in range(NCH):
                        for sub in range(nsub):
                            ptp = pt[sub % 2]
                            ptpB = ptB[sub % 2]
                            pr.op("tensor", (lambda e, j=j, sub=sub, src=src, ptp=ptp:
                                             e.transpose(
                                ptp[0:128, 0:128],
                                src[0:128, PW * sub + 3 + 128 * j:PW * sub + 3 + 128 * (j + 1)],
                                c["c_id_bf"][0:128, 0:128])),
                                reads=[srcB, cB], writes=[ptpB])
                            pr.op("vector", (lambda e, sub=sub, ptp=ptp: e.tensor_copy(
                                tka[0:128, 128 * sub:128 * (sub + 1)], ptp[0:128, 0:128])),
                                reads=[ptpB], writes=[tkaB])
                        if kind == "v":
                            pr.op("sync", (lambda e, j=j: e.dma_start(
                                out=ctx.dram["v_tok_d"][128 * j:128 * (j + 1), 0:512],
                                in_=tka[0:128, 0:512])), reads=[tkaB],
                                writes=[ctx.bufs["v_tok_d"]], dma=True)
                            continue
                        pr.op("scalar", (lambda e: e.activation(
                            tsq[0:128, 0:256], tka[0:128, 0:256], AF.Square)),
                            reads=[tkaB], writes=[tsqB])
                        pr.op("vector", (lambda e: e.reduce_sum(
                            scc[0:128, 0:1], tsq[0:128, 0:256], AX.X)),
                            reads=[tsqB], writes=[sccB])
                        pr.op("scalar", (lambda e: e.activation(
                            scc[0:128, 1:2], scc[0:128, 0:1], AF.Ln, bias=1e-6)),
                            reads=[sccB, cB], writes=[sccB])
                        if kind == "q":
                            pr.op("scalar", (lambda e: e.activation(
                                scc[0:128, 2:3], scc[0:128, 1:2], AF.Exp, scale=-0.5,
                                bias=-2.772588722239781)),
                                reads=[sccB, cB], writes=[sccB])
                        else:
                            pr.op("scalar", (lambda e: e.activation(
                                scc[0:128, 2:3], scc[0:128, 1:2], AF.Exp, scale=-0.5)),
                                reads=[sccB], writes=[sccB])
                        pr.op("vector", (lambda e: e.tensor_scalar(
                            tka[0:128, 0:256], tka[0:128, 0:256], scc[0:128, 2:3], None,
                            ALU.mult)), reads=[tkaB, sccB], writes=[tkaB])
                        if kind == "k":
                            pr.op("sync", (lambda e, j=j: e.dma_start(
                                out=ctx.dram["kn_tok_d"][128 * j:128 * (j + 1), 0:256],
                                in_=tka[0:128, 0:256])), reads=[tkaB],
                                writes=[ctx.bufs["kn_tok_d"]], dma=True)
                        for sub in range(2):
                            ptp = pt[sub % 2]
                            ptpB = ptB[sub % 2]
                            pr.op("tensor", (lambda e, sub=sub, ptp=ptp: e.transpose(
                                ptp[0:128, 0:128], tka[0:128, 128 * sub:128 * (sub + 1)],
                                c["c_id_bf"][0:128, 0:128])),
                                reads=[tkaB, cB], writes=[ptpB])
                            pr.op("vector", (lambda e, sub=sub, ptp=ptp: e.tensor_copy(
                                tst[0:128, 128 * sub:128 * (sub + 1)], ptp[0:128, 0:128])),
                                reads=[ptpB], writes=[tstB])
                        dname = "qnT_d" if kind == "q" else "knT_d"
                        pr.op("sync", (lambda e, j=j, dname=dname: e.dma_start(
                            out=ctx.dram[dname][0:256, 128 * j:128 * (j + 1)].rearrange(
                                "(s p) t -> p s t", p=128),
                            in_=tst[0:128, 0:256].rearrange("p (s t) -> p s t", s=2))),
                            reads=[tstB], writes=[ctx.bufs[dname]], dma=True)
            pr.fence()
            with contextlib.ExitStack() as stk:
                if h == 0 and "ch" in ctx.dbg:
                    pr.op("sync", lambda e: e.dma_start(
                        out=ctx.dram["dbg_ch"][0:1, 7824:7824 + 512], in_=arow[0:1, 0:512]),
                        reads=[arowB], writes=[ctx.bufs["dbg_ch"]], dma=True)
                    pr.op("sync", lambda e: e.dma_start(
                        out=ctx.dram["dbg_ch"][0:1, 8336:8336 + 512], in_=brow[0:1, 0:512]),
                        reads=[browB], writes=[ctx.bufs["dbg_ch"]], dma=True)
                tmpr = _sb(ctx, stk, "tmpr", [1, T], F32)
                tmprB = HB["tmpr"]
                pr.op("scalar", lambda e: e.activation(tmpr[0:1, 0:T], arow[0:1, 0:T], AF.Abs),
                      reads=[arowB], writes=[tmprB])
                pr.op("scalar", lambda e: e.activation(tmpr[0:1, 0:T], tmpr[0:1, 0:T],
                                                       AF.Exp, scale=-1.0),
                      reads=[tmprB], writes=[tmprB])
                pr.op("scalar", lambda e: e.activation(tmpr[0:1, 0:T], tmpr[0:1, 0:T],
                                                       AF.Ln, bias=1.0),
                      reads=[tmprB, cB], writes=[tmprB])
                pr.op("vector", lambda e: e.tensor_relu(arow[0:1, 0:T], arow[0:1, 0:T]),
                      reads=[arowB], writes=[arowB])
                pr.op("vector", lambda e: e.tensor_add(arow[0:1, 0:T], arow[0:1, 0:T],
                                                       tmpr[0:1, 0:T]),
                      reads=[arowB, tmprB], writes=[arowB])
                pr.op("vector", (lambda e, h=h: e.tensor_scalar(
                    arow[0:1, 0:T], arow[0:1, 0:T], dtA[0:1, 4 + h:5 + h], None, ALU.mult)),
                    reads=[arowB, dtAB], writes=[arowB])
                pr.op("scalar", lambda e: e.activation(brow[0:1, 0:T], brow[0:1, 0:T],
                                                       AF.Sigmoid),
                      reads=[browB], writes=[browB])
                pr.op("vector", lambda e: e.tensor_scalar(nbrow[0:1, 0:T], brow[0:1, 0:T],
                                                          -1.0, None, ALU.mult),
                      reads=[browB], writes=[nbrowB])
                for j in range(NCH):
                    pr.op("vector", (lambda e, j=j: e.tensor_tensor_scan(
                        Grow[0:1, 128 * j:128 * (j + 1)], arow[0:1, 128 * j:128 * (j + 1)],
                        zrow[0:1, 0:128], 0.0, ALU.add, ALU.bypass)),
                        reads=[arowB, zrowB], writes=[GrowB])
            pr.fence()
            with contextlib.ExitStack() as stk:
                def par2(name, shape, dt):
                    ts = [_sb(ctx, stk, f"{name}{i}", shape, dt) for i in range(2)]
                    bs = [Buf(f"{name}{i}_{h}") for i in range(2)]
                    return ts, bs
                qnT_i, qnTB = par2("qnti", [128, 256], BF16)
                knT_i, knTB = par2("knti", [128, 256], BF16)
                knk_i, knkB = par2("knki", [128, 256], BF16)
                vt_i, vtB = par2("vti", [128, 512], BF16)
                gt_i, gtB = par2("gti", [128, 512], BF16)
                gcol2, gcol2B = par2("gcol2", [128, 2], F32)
                gcb, gcbB = par2("gcb", [128, 1], F32)
                sc, scB = par2("sc", [128, 8], F32)
                eaf, eafB = par2("eaf", [128, 128], F32)
                ewf, ewfB = par2("ewf", [128, 128], F32)
                ews, ewsB = par2("ews", [128, 128], F32)
                tchf, tchfB = par2("tchf", [128, 128], F32)
                nab, nabB = par2("nab", [128, 128], BF16)
                natb, natbB = par2("natb", [128, 128], BF16)
                rtA, rtAB = par2("rtA", [128, 128], BF16)
                rtBt, rtBB = par2("rtB", [128, 128], BF16)
                snA, snAB = par2("snA", [128, 128], BF16)
                snBt, snBB = par2("snB", [128, 128], BF16)
                stA, stAB = par2("stA", [128, 128], BF16)
                stBt, stBB = par2("stB", [128, 128], BF16)
                ut, utB = par2("ut", [128, 512], F32)
                ubf, ubfB = par2("ubf", [128, 512], BF16)
                dbf, dbfB = par2("dbf", [128, 512], BF16)
                o1t, o1tB = par2("o1t", [128, 512], F32)
                wtb, wtbB = par2("wtb", [128, 128], BF16)
                ot, otB = par2("ot", [128, 512], F32)
                osq, osqB = par2("osq", [128, 512], BF16)
                ofin, ofinB = par2("ofin", [128, 512], BF16)
                kdb, kdbB = par2("kdb", [128, 256], BF16)
                otst, otstB = par2("otst", [128, 512], BF16)
                pr.op("vector", lambda e: e.memset(S32[0:128, 0:1024], 0.0), writes=[S32B])
                pr.op("vector", lambda e: e.memset(Sbf[0:128, 0:1024], 0.0), writes=[SbfB])
                rt_fin = [None, None]
                neu_state = [None, None]

                def emit_pre(j):
                    jb = j % 2
                    bA, bB_, bC = (0, 1, 2) if jb == 0 else (3, 4, 5)
                    sl = slice(128 * j, 128 * (j + 1))
                    pr.op("sync", (lambda e, j=j, t=qnT_i[jb]: e.dma_start(
                        out=t[0:128, 0:256].rearrange("p (s t) -> p s t", s=2),
                        in_=ctx.dram["qnT_d"][0:256, 128 * j:128 * (j + 1)].rearrange(
                            "(s p) t -> p s t", p=128))),
                        reads=[ctx.bufs["qnT_d"]], writes=[qnTB[jb]], dma=True)
                    pr.op("sync", (lambda e, j=j, t=knT_i[jb]: e.dma_start(
                        out=t[0:128, 0:256].rearrange("p (s t) -> p s t", s=2),
                        in_=ctx.dram["knT_d"][0:256, 128 * j:128 * (j + 1)].rearrange(
                            "(s p) t -> p s t", p=128))),
                        reads=[ctx.bufs["knT_d"]], writes=[knTB[jb]], dma=True)
                    pr.op("scalar", (lambda e, j=j, t=knk_i[jb]: e.dma_start(
                        out=t[0:128, 0:256],
                        in_=ctx.dram["kn_tok_d"][128 * j:128 * (j + 1), 0:256])),
                        reads=[ctx.bufs["kn_tok_d"]], writes=[knkB[jb]], dma=True)
                    pr.op("scalar", (lambda e, j=j, t=vt_i[jb]: e.dma_start(
                        out=t[0:128, 0:512],
                        in_=ctx.dram["v_tok_d"][128 * j:128 * (j + 1), 0:512])),
                        reads=[ctx.bufs["v_tok_d"]], writes=[vtB[jb]], dma=True)
                    pr.op("scalar", (lambda e, j=j, t=gt_i[jb]: e.dma_start(
                        out=t[0:128, 0:512],
                        in_=ctx.dram["gate_d"][128 * j:128 * (j + 1), 0:512])),
                        reads=[ctx.bufs["gate_d"]], writes=[gtB[jb]], dma=True)
                    pr.op("tensor", (lambda e, sl=sl, bA=bA: e.matmul(
                        pb[bA][0:128, 0:1], Grow[0:1, sl], c["c_ones_f"][0:1, 0:1],
                        start=True, stop=True)), reads=[GrowB, cB], writes=[pbB[bA]])
                    pr.op("tensor", (lambda e, sl=sl, bA=bA: e.matmul(
                        pb[bA][0:128, 1:2], brow[0:1, sl], c["c_ones_f"][0:1, 0:1],
                        start=True, stop=True)), reads=[browB, cB], writes=[pbB[bA]])
                    pr.op("vector", (lambda e, t=gcol2[jb], bA=bA: e.tensor_copy(
                        t[0:128, 0:2], pb[bA][0:128, 0:2])),
                        reads=[pbB[bA]], writes=[gcol2B[jb]])
                    pr.op("tensor", (lambda e, t=gcol2[jb], bA=bA: e.matmul(
                        pb[bA][0:128, 2:3], c["c_sel127"][0:128, 0:128], t[0:128, 0:1],
                        start=True, stop=True)), reads=[gcol2B[jb], cB], writes=[pbB[bA]])
                    pr.op("vector", (lambda e, t=gcb[jb], bA=bA: e.tensor_copy(
                        t[0:128, 0:1], pb[bA][0:128, 2:3])),
                        reads=[pbB[bA]], writes=[gcbB[jb]])
                    pr.op("vector", (lambda e, t=sc[jb], g2=gcol2[jb]: e.tensor_scalar(
                        t[0:128, 0:1], g2[0:128, 1:2], -1.0, None, ALU.mult)),
                        reads=[gcol2B[jb]], writes=[scB[jb]])
                    pr.op("scalar", (lambda e, t=sc[jb], g2=gcol2[jb]: e.activation(
                        t[0:128, 1:2], g2[0:128, 0:1], AF.Exp)),
                        reads=[gcol2B[jb]], writes=[scB[jb]])
                    pr.op("vector", (lambda e, t=sc[jb], gc=gcb[jb], g2=gcol2[jb]:
                                     e.tensor_scalar(
                        t[0:128, 4:5], gc[0:128, 0:1], g2[0:128, 0:1], None, ALU.subtract)),
                        reads=[gcbB[jb], gcol2B[jb]], writes=[scB[jb]])
                    pr.op("scalar", (lambda e, t=sc[jb]: e.activation(
                        t[0:128, 2:3], t[0:128, 4:5], AF.Exp)),
                        reads=[scB[jb]], writes=[scB[jb]])
                    pr.op("scalar", (lambda e, t=sc[jb], gc=gcb[jb]: e.activation(
                        t[0:128, 3:4], gc[0:128, 0:1], AF.Exp)),
                        reads=[gcbB[jb]], writes=[scB[jb]])
                    pr.op("tensor", (lambda e, sl=sl, bB_=bB_: e.matmul(
                        pb[bB_][0:128, 0:128], c["c_ones_f"][0:1, 0:128], Grow[0:1, sl],
                        start=True, stop=True)), reads=[GrowB, cB], writes=[pbB[bB_]])
                    for (dst, dstB, mname, esc) in ((eaf[jb], eafB[jb], "c_mA", -1.0),
                                                    (ewf[jb], ewfB[jb], "c_mW", 1.0),
                                                    (ews[jb], ewsB[jb], "c_mWs", 1.0)):
                        pr.op("vector", (lambda e, dst=dst, mname=mname, g2=gcol2[jb],
                                         bB_=bB_: e.scalar_tensor_tensor(
                            dst[0:128, 0:128], pb[bB_][0:128, 0:128], g2[0:128, 0:1],
                            c[mname][0:128, 0:128], ALU.subtract, ALU.add)),
                            reads=[pbB[bB_], gcol2B[jb], cB], writes=[dstB])
                        pr.op("scalar", (lambda e, dst=dst, esc=esc: e.activation(
                            dst[0:128, 0:128], dst[0:128, 0:128], AF.Exp, scale=esc)),
                            reads=[dstB], writes=[dstB])
                    for sb_ in range(2):
                        pr.op("tensor", (lambda e, sb_=sb_, t=knT_i[jb], bC=bC: e.matmul(
                            pb[bC][0:128, 0:128], t[0:128, 128 * sb_:128 * (sb_ + 1)],
                            t[0:128, 128 * sb_:128 * (sb_ + 1)],
                            start=(sb_ == 0), stop=(sb_ == 1))),
                            reads=[knTB[jb]], writes=[pbB[bC]])
                    pr.op("vector", (lambda e, t=tchf[jb], ea=eaf[jb], bC=bC: e.tensor_mul(
                        t[0:128, 0:128], ea[0:128, 0:128], pb[bC][0:128, 0:128])),
                        reads=[eafB[jb], pbB[bC]], writes=[tchfB[jb]])
                    pr.op("vector", (lambda e, t=nab[jb], tc_=tchf[jb], s_=sc[jb]:
                                     e.tensor_scalar(
                        t[0:128, 0:128], tc_[0:128, 0:128], s_[0:128, 0:1], None, ALU.mult)),
                        reads=[tchfB[jb], scB[jb]], writes=[nabB[jb]])
                    pr.op("tensor", (lambda e, sl=sl, bB_=bB_: e.matmul(
                        pb[bB_][0:128, 0:128], c["c_ones_f"][0:1, 0:128], nbrow[0:1, sl],
                        start=True, stop=True)), reads=[nbrowB, cB], writes=[pbB[bB_]])
                    pr.op("vector", (lambda e, t=tchf[jb], ew=ews[jb], bC=bC: e.tensor_mul(
                        t[0:128, 0:128], ew[0:128, 0:128], pb[bC][0:128, 0:128])),
                        reads=[ewsB[jb], pbB[bC], tchfB[jb]], writes=[tchfB[jb]])
                    pr.op("vector", (lambda e, t=natb[jb], tc_=tchf[jb], bB_=bB_: e.tensor_mul(
                        t[0:128, 0:128], tc_[0:128, 0:128], pb[bB_][0:128, 0:128])),
                        reads=[tchfB[jb], pbB[bB_]], writes=[natbB[jb]])
                    pr.op("vector", (lambda e, t=rtA[jb], n_=natb[jb]: e.tensor_add(
                        t[0:128, 0:128], n_[0:128, 0:128], c["c_id_bf"][0:128, 0:128])),
                        reads=[natbB[jb], cB], writes=[rtAB[jb]])
                    neu_state[jb] = [nab[jb], nabB[jb], natb[jb], natbB[jb],
                                     rtA[jb], rtAB[jb]]

                def emit_neumann_iter(j, it):
                    jb = j % 2
                    bA, bB_, bC = (0, 1, 2) if jb == 0 else (3, 4, 5)
                    sn_cur, snB_cur, st_cur, stB_cur, rt_cur, rtB_cur = neu_state[jb]
                    rts = [(rtBt[jb], rtBB[jb]), (rtA[jb], rtAB[jb])]
                    sns = [(snA[jb], snAB[jb]), (snBt[jb], snBB[jb])]
                    sts = [(stA[jb], stAB[jb]), (stBt[jb], stBB[jb])]
                    bq = bA if it % 2 == 0 else bC
                    bp = bC if it % 2 == 0 else bA
                    sn_nxt, snB_nxt = sns[it % 2]
                    st_nxt, stB_nxt = sts[it % 2]
                    pr.op("tensor", (lambda e, a=st_cur, b2=sn_cur, bq=bq: e.matmul(
                        pb[bq][0:128, 0:128], a[0:128, 0:128], b2[0:128, 0:128],
                        start=True, stop=True)),
                        reads=[stB_cur, snB_cur], writes=[pbB[bq]])
                    pr.op("scalar", (lambda e, o=sn_nxt, bq=bq: e.copy(
                        o[0:128, 0:128], pb[bq][0:128, 0:128])),
                        reads=[pbB[bq]], writes=[snB_nxt])
                    pr.op("tensor", (lambda e, a=sn_cur, b2=st_cur, bq=bq: e.matmul(
                        pb[bq][0:128, 0:128], a[0:128, 0:128], b2[0:128, 0:128],
                        start=True, stop=True)),
                        reads=[snB_cur, stB_cur], writes=[pbB[bq]])
                    pr.op("scalar", (lambda e, o=st_nxt, bq=bq: e.copy(
                        o[0:128, 0:128], pb[bq][0:128, 0:128])),
                        reads=[pbB[bq]], writes=[stB_nxt])
                    pr.op("tensor", (lambda e, b2=rt_cur, bp=bp: e.matmul(
                        pb[bp][0:128, 0:128], c["c_id_bf"][0:128, 0:128],
                        b2[0:128, 0:128], start=True, stop=False)),
                        reads=[cB, rtB_cur], writes=[pbB[bp]])
                    pr.op("tensor", (lambda e, a=sn_nxt, b2=rt_cur, bp=bp: e.matmul(
                        pb[bp][0:128, 0:128], a[0:128, 0:128], b2[0:128, 0:128],
                        start=False, stop=True)),
                        reads=[snB_nxt, rtB_cur], writes=[pbB[bp]])
                    rt_nxt, rtB_nxt = rts[it % 2]
                    pr.op("scalar", (lambda e, o=rt_nxt, bp=bp: e.copy(
                        o[0:128, 0:128], pb[bp][0:128, 0:128])),
                        reads=[pbB[bp]], writes=[rtB_nxt])
                    neu_state[jb] = [sn_nxt, snB_nxt, st_nxt, stB_nxt, rt_nxt, rtB_nxt]
                    if it == 5:
                        rt_fin[jb] = (rt_nxt, rtB_nxt)

                def emit_post(j, piece):
                    jb = j % 2
                    bA, bB_, bC = (0, 1, 2) if jb == 0 else (3, 4, 5)
                    sl = slice(128 * j, 128 * (j + 1))
                    rt_cur, rtB_cur = rt_fin[jb]
                    if piece == 0:
                        emit_post0(j, jb, bA, bB_, bC)
                    elif piece == 1:
                        emit_post1(j, jb, bA, bB_, bC, rt_cur, rtB_cur)
                    elif piece == 2:
                        emit_post2(j, jb, bA, bB_, bC)
                    elif piece == 3:
                        emit_post3(j, jb, bA, bB_, bC)
                    elif piece == 4:
                        emit_post4(j, jb, bA, bB_, bC)
                    else:
                        emit_post5(j, jb, bA, bB_, bC)

                def emit_post0(j, jb, bA, bB_, bC):
                    for sb_ in range(2):
                        pr.op("tensor", (lambda e, sb_=sb_, t=knT_i[jb], bB_=bB_: e.matmul(
                            pb[bB_][0:128, 0:512], t[0:128, 128 * sb_:128 * (sb_ + 1)],
                            Sbf[0:128, 512 * sb_:512 * (sb_ + 1)],
                            start=(sb_ == 0), stop=(sb_ == 1))),
                            reads=[knTB[jb], SbfB], writes=[pbB[bB_]])
                    pr.op("vector", (lambda e, t=ut[jb], v=vt_i[jb], s_=sc[jb], bB_=bB_:
                                     e.scalar_tensor_tensor(
                        t[0:128, 0:512], pb[bB_][0:128, 0:512], s_[0:128, 1:2],
                        v[0:128, 0:512], ALU.mult, ALU.subtract)),
                        reads=[pbB[bB_], scB[jb], vtB[jb]], writes=[utB[jb]])
                    pr.op("vector", (lambda e, t=ubf[jb], u=ut[jb], s_=sc[jb]: e.tensor_scalar(
                        t[0:128, 0:512], u[0:128, 0:512], s_[0:128, 0:1], None, ALU.mult)),
                        reads=[utB[jb], scB[jb]], writes=[ubfB[jb]])
                def emit_post1(j, jb, bA, bB_, bC, rt_cur, rtB_cur):
                    pr.op("tensor", (lambda e, a=rt_cur, u=ubf[jb], bC=bC: e.matmul(
                        pb[bC][0:128, 0:512], a[0:128, 0:128], u[0:128, 0:512],
                        start=True, stop=True)), reads=[rtB_cur, ubfB[jb]], writes=[pbB[bC]])
                    pr.op("vector", (lambda e, t=dbf[jb], bC=bC: e.tensor_copy(
                        t[0:128, 0:512], pb[bC][0:128, 0:512])),
                        reads=[pbB[bC]], writes=[dbfB[jb]])
                def emit_post2(j, jb, bA, bB_, bC):
                    for sb_ in range(2):
                        pr.op("tensor", (lambda e, sb_=sb_, t=qnT_i[jb], bB_=bB_: e.matmul(
                            pb[bB_][0:128, 0:512], t[0:128, 128 * sb_:128 * (sb_ + 1)],
                            Sbf[0:128, 512 * sb_:512 * (sb_ + 1)],
                            start=(sb_ == 0), stop=(sb_ == 1))),
                            reads=[qnTB[jb], SbfB], writes=[pbB[bB_]])
                    pr.op("vector", (lambda e, t=o1t[jb], s_=sc[jb], bB_=bB_: e.tensor_scalar(
                        t[0:128, 0:512], pb[bB_][0:128, 0:512], s_[0:128, 1:2], None,
                        ALU.mult)), reads=[pbB[bB_], scB[jb]], writes=[o1tB[jb]])
                    for sb_ in range(2):
                        pr.op("tensor", (lambda e, sb_=sb_, tk=knT_i[jb], tq=qnT_i[jb],
                                         bA=bA: e.matmul(
                            pb[bA][0:128, 0:128], tk[0:128, 128 * sb_:128 * (sb_ + 1)],
                            tq[0:128, 128 * sb_:128 * (sb_ + 1)],
                            start=(sb_ == 0), stop=(sb_ == 1))),
                            reads=[knTB[jb], qnTB[jb]], writes=[pbB[bA]])
                    pr.op("vector", (lambda e, t=wtb[jb], ew=ewf[jb], bA=bA: e.tensor_mul(
                        t[0:128, 0:128], ew[0:128, 0:128], pb[bA][0:128, 0:128])),
                        reads=[ewfB[jb], pbB[bA]], writes=[wtbB[jb]])
                def emit_post3(j, jb, bA, bB_, bC):
                    pr.op("tensor", (lambda e, w=wtb[jb], d=dbf[jb], bC=bC: e.matmul(
                        pb[bC][0:128, 0:512], w[0:128, 0:128], d[0:128, 0:512],
                        start=True, stop=True)), reads=[wtbB[jb], dbfB[jb]], writes=[pbB[bC]])
                    pr.op("vector", (lambda e, t=ot[jb], o1=o1t[jb], bC=bC: e.tensor_add(
                        t[0:128, 0:512], o1[0:128, 0:512], pb[bC][0:128, 0:512])),
                        reads=[o1tB[jb], pbB[bC]], writes=[otB[jb]])
                    pr.op("scalar", (lambda e, t=osq[jb], o=ot[jb]: e.activation(
                        t[0:128, 0:512], o[0:128, 0:512], AF.Square)),
                        reads=[otB[jb]], writes=[osqB[jb]])
                    pr.op("vector", (lambda e, t=sc[jb], q=osq[jb]: e.reduce_sum(
                        t[0:128, 5:6], q[0:128, 0:512], AX.X)),
                        reads=[osqB[jb], scB[jb]], writes=[scB[jb]])
                    pr.op("scalar", (lambda e, t=sc[jb]: e.activation(
                        t[0:128, 6:7], t[0:128, 5:6], AF.Ln, scale=1.0 / 512, bias=1e-6)),
                        reads=[scB[jb], cB], writes=[scB[jb]])
                    pr.op("scalar", (lambda e, t=sc[jb]: e.activation(
                        t[0:128, 7:8], t[0:128, 6:7], AF.Exp, scale=-0.5)),
                        reads=[scB[jb]], writes=[scB[jb]])
                    pr.op("vector", (lambda e, t=ot[jb], s_=sc[jb]: e.tensor_scalar(
                        t[0:128, 0:512], t[0:128, 0:512], s_[0:128, 7:8], None, ALU.mult)),
                        reads=[otB[jb], scB[jb]], writes=[otB[jb]])
                    pr.op("vector", (lambda e, t=ofin[jb], o=ot[jb], g=gt_i[jb]: e.tensor_mul(
                        t[0:128, 0:512], o[0:128, 0:512], g[0:128, 0:512])),
                        reads=[otB[jb], gtB[jb]], writes=[ofinB[jb]])
                def emit_post4(j, jb, bA, bB_, bC):
                    pr.op("vector", (lambda e, t=kdb[jb], k_=knk_i[jb], s_=sc[jb]:
                                     e.tensor_scalar(
                        t[0:128, 0:256], k_[0:128, 0:256], s_[0:128, 2:3], None, ALU.mult)),
                        reads=[knkB[jb], scB[jb]], writes=[kdbB[jb]])
                    for sb_ in range(2):
                        bx = bA if sb_ == 0 else bB_
                        pr.op("tensor", (lambda e, sb_=sb_, k_=kdb[jb], d=dbf[jb], bx=bx:
                                         e.matmul(
                            pb[bx][0:128, 0:512], k_[0:128, 128 * sb_:128 * (sb_ + 1)],
                            d[0:128, 0:512], start=True, stop=True)),
                            reads=[kdbB[jb], dbfB[jb]], writes=[pbB[bx]])
                        pr.op("vector", (lambda e, sb_=sb_, s_=sc[jb], bx=bx:
                                         e.scalar_tensor_tensor(
                            S32[0:128, 512 * sb_:512 * (sb_ + 1)],
                            S32[0:128, 512 * sb_:512 * (sb_ + 1)], s_[0:128, 3:4],
                            pb[bx][0:128, 0:512], ALU.mult, ALU.add)),
                            reads=[pbB[bx], scB[jb], S32B], writes=[S32B])
                        pr.op("vector", (lambda e, sb_=sb_: e.tensor_copy(
                            Sbf[0:128, 512 * sb_:512 * (sb_ + 1)],
                            S32[0:128, 512 * sb_:512 * (sb_ + 1)])),
                            reads=[S32B], writes=[SbfB])
                def emit_post5(j, jb, bA, bB_, bC):
                    for m in range(4):
                        ptp = pt[m % 2]
                        ptpB = ptB[m % 2]
                        pr.op("tensor", (lambda e, m=m, o=ofin[jb], ptp=ptp: e.transpose(
                            ptp[0:128, 0:128], o[0:128, 128 * m:128 * (m + 1)],
                            c["c_id_bf"][0:128, 0:128])), reads=[ofinB[jb], cB], writes=[ptpB])
                        pr.op("vector", (lambda e, m=m, t=otst[jb], ptp=ptp: e.tensor_copy(
                            t[0:128, 128 * m:128 * (m + 1)], ptp[0:128, 0:128])),
                            reads=[ptpB], writes=[otstB[jb]])
                    pr.op("sync", (lambda e, j=j, h=h, t=otst[jb]: e.dma_start(
                        out=ctx.dram["oTs"][512 * h:512 * (h + 1),
                                            128 * j:128 * (j + 1)].rearrange(
                            "(s p) t -> p s t", p=128),
                        in_=t[0:128, 0:512].rearrange("p (s t) -> p s t", s=4))),
                        reads=[otstB[jb]], writes=[ctx.bufs["oTs"]], dma=True)

                emit_pre(0)
                for it in range(6):
                    emit_neumann_iter(0, it)
                for j in range(NCH):
                    if j + 1 < NCH:
                        emit_pre(j + 1)
                        for it in range(6):
                            emit_neumann_iter(j + 1, it)
                            emit_post(j, it)
                    else:
                        for it in range(6):
                            emit_post(j, it)
            pr.fence()
        pr.fence()

    st_gemm_ln(ctx, src="oTs", wname="L1_attn_Wo", K=2048, res=["r2_32"],
               out32=None, outbf="h2bf", zout="x1_32",
               affine=("L1_attn_ln2_g", "L1_attn_ln2_b"),
               wscale_input=("L1_attn_o_norm_w", 4))
    pr.fence()
    st_gemm_ln(ctx, src="h2bf", wname="L1_attn_W2", K=1024, res=["x1_32", "r2_32"],
               out32="r3_32", outbf="r3_bf", act=("gelu", "L1_attn_b2"))
    pr.fence()
def st_wtrans(ctx, wname, odim, idim, dst):
    nc, pr, io, c = ctx.nc, ctx.pr, ctx.io, ctx.c
    pt, ptB = ctx.pt, ctx.ptB
    cB = ctx.cB
    nki = idim // 128
    with contextlib.ExitStack() as stk:
        wld = [_sb(ctx, stk, f"wld{i}", [128, idim], BF16) for i in range(2)]
        wldB = [Buf(f"wld{i}") for i in range(2)]
        wst = [_sb(ctx, stk, f"wst{i}", [128, idim], BF16) for i in range(2)]
        wstB = [Buf(f"wst{i}") for i in range(2)]
        for mo in range(odim // 128):
            b = mo % 2
            pr.op("gpsimd", (lambda e, mo=mo, t=wld[b]: e.dma_start(
                out=t[0:128, 0:idim],
                in_=io[wname][128 * mo:128 * (mo + 1), :])),
                writes=[wldB[b]], dma=True)
            for ki in range(nki):
                p = pt[ki % 2]
                pB = ptB[ki % 2]
                pr.op("tensor", (lambda e, ki=ki, t=wld[b], p=p: e.transpose(
                    p[0:128, 0:128], t[0:128, 128 * ki:128 * (ki + 1)],
                    c["c_id_bf"][0:128, 0:128])),
                    reads=[wldB[b], cB], writes=[pB])
                pr.op("vector", (lambda e, ki=ki, p=p, t=wst[b]: e.tensor_copy(
                    t[0:128, 128 * ki:128 * (ki + 1)], p[0:128, 0:128])),
                    reads=[pB], writes=[wstB[b]])
            pr.op("sync", (lambda e, mo=mo, t=wst[b]: e.dma_start(
                out=ctx.dram[dst][0:idim, 128 * mo:128 * (mo + 1)].rearrange(
                    "(k p) m -> p k m", p=128),
                in_=t[0:128, 0:idim].rearrange("p (k m) -> p k m", k=nki))),
                reads=[wstB[b]], writes=[ctx.bufs[dst]], dma=True)
    pr.fence()


def st_big_gemm(ctx, src, wbf, scale_name, dst, kdim, odim, act):
    nc, pr, io, c = ctx.nc, ctx.pr, ctx.io, ctx.c
    pb, pbB = ctx.pb, ctx.pbB
    cB = ctx.cB
    nk = kdim // 128
    nm = odim // 128
    with contextlib.ExitStack() as stk:
        scol, scolB = _col_loads(ctx, pr, stk, "scol",
                                 lambda i: io[scale_name][128 * i:128 * (i + 1)][:, None],
                                 nm, "bscol")
        wg = [_sb(ctx, stk, f"bgw{i}", [128, nk * 4 * 128], BF16) for i in range(2)]
        wgB = [Buf(f"bgw{i}") for i in range(2)]
        rin = [_sb(ctx, stk, f"bgr{i}", [128, nk * 512], BF16) for i in range(2)]
        rinB = [Buf(f"bgr{i}") for i in range(2)]
        yst = [_sb(ctx, stk, f"bgy{i}", [128, 512], BF16) for i in range(2)]
        ystB = [Buf(f"bgy{i}") for i in range(2)]
        def wgload(mg, ks):
            b = mg % 2
            for k in ks:
                pr.op("scalar", (lambda e, mg=mg, k=k, t=wg[b]: e.dma_start(
                    out=t[0:128, 512 * k:512 * (k + 1)],
                    in_=ctx.dram[wbf][512 * mg:512 * (mg + 1), 128 * k:128 * (k + 1)],
                    transpose=True)),
                    reads=[ctx.bufs[wbf]], writes=[wgB[b]], dma=True)

        wgload(0, range(nk))
        for mg in range(nm // 4):
            b = mg % 2
            for tc in range(8):
                rb = tc % 2
                pr.op("sync", (lambda e, tc=tc, t=rin[rb]: e.dma_start(
                    out=t[0:128, 0:nk * 512].rearrange("p (k t) -> p k t", k=nk),
                    in_=ctx.dram[src][0:kdim, 512 * tc:512 * (tc + 1)].rearrange(
                        "(k p) t -> p k t", p=128))),
                    reads=[ctx.bufs[src]], writes=[rinB[rb]], dma=True)
                if mg + 1 < nm // 4:
                    wgload(mg + 1, range(tc * nk // 8, (tc + 1) * nk // 8))
                for mi in range(4):
                    pp = pb[mi]
                    ppB = pbB[mi]
                    for k in range(nk):
                        pr.op("tensor", (lambda e, k=k, mi=mi, w=wg[b], t=rin[rb], pp=pp:
                                         e.matmul(
                            pp[0:128, 0:512], w[0:128, (k * 4 + mi) * 128:(k * 4 + mi) * 128 + 128],
                            t[0:128, 512 * k:512 * (k + 1)],
                            start=(k == 0), stop=(k == nk - 1))),
                            reads=[wgB[b], rinB[rb]], writes=[ppB])
                    yb = (tc * 4 + mi) % 2
                    m = 4 * mg + mi
                    fn = AF.Gelu if act == "gelu" else AF.Silu
                    pr.op("scalar", (lambda e, m=m, y=yst[yb], pp=pp, fn=fn: e.activation(
                        y[0:128, 0:512], pp[0:128, 0:512], fn,
                        scale=scol[0:128, m:m + 1])),
                        reads=[ppB, scolB], writes=[ystB[yb]])
                    pr.op("sync", (lambda e, m=m, tc=tc, y=yst[yb]: e.dma_start(
                        out=ctx.dram[dst][128 * m:128 * (m + 1), 512 * tc:512 * (tc + 1)],
                        in_=y[0:128, 0:512])), reads=[ystB[yb]], writes=[ctx.bufs[dst]],
                        dma=True)
    pr.fence()


def st_l1ffn(ctx):
    nc, pr, io, c = ctx.nc, ctx.pr, ctx.io, ctx.c
    pb, pbB = ctx.pb, ctx.pbB
    pt, ptB = ctx.pt, ctx.ptB
    cB = ctx.cB
    with contextlib.ExitStack() as stk:
        s1c, s1B = _col_loads(ctx, pr, stk, "s1c",
                              lambda i: io["L1_ffn_w1_s"][128 * i:128 * (i + 1)][:, None],
                              32, "fs1")
        s2c, s2B = _col_loads(ctx, pr, stk, "s2c",
                              lambda i: io["L1_ffn_w2_s"][128 * i:128 * (i + 1)][:, None],
                              32, "fs2")
        w1g = [_sb(ctx, stk, f"f1w{i}", [128, 8 * 4 * 128], BF16) for i in range(2)]
        w1gB = [Buf(f"f1w{i}") for i in range(2)]
        w2g = [_sb(ctx, stk, f"f2w{i}", [128, 8 * 4 * 128], BF16) for i in range(2)]
        w2gB = [Buf(f"f2w{i}") for i in range(2)]
        rin = [_sb(ctx, stk, f"f1r{i}", [128, 8 * 512], BF16) for i in range(2)]
        rinB = [Buf(f"f1r{i}") for i in range(2)]
        t1b = [_sb(ctx, stk, f"f1t{i}", [128, 512], F32) for i in range(2)]
        t1B = [Buf(f"f1t{i}") for i in range(2)]
        t2b = [_sb(ctx, stk, f"f2t{i}", [128, 512], F32) for i in range(2)]
        t2B = [Buf(f"f2t{i}") for i in range(2)]
        yst = [_sb(ctx, stk, f"f1y{i}", [128, 512], BF16) for i in range(2)]
        ystB = [Buf(f"f1y{i}") for i in range(2)]
        def f1wload(mg, ks):
            b = mg % 2
            for (wn, t, tB) in (("wbf_w1", w1g[b], w1gB[b]), ("wbf_w2", w2g[b], w2gB[b])):
                for k in ks:
                    pr.op("scalar", (lambda e, mg=mg, k=k, wn=wn, t=t: e.dma_start(
                        out=t[0:128, 512 * k:512 * (k + 1)],
                        in_=ctx.dram[wn][512 * mg:512 * (mg + 1), 128 * k:128 * (k + 1)],
                        transpose=True)),
                        reads=[ctx.bufs[wn]], writes=[tB], dma=True)

        f1wload(0, range(8))
        for mg in range(8):
            b = mg % 2
            for tc in range(8):
                rb = tc % 2
                pr.op("sync", (lambda e, tc=tc, t=rin[rb]: e.dma_start(
                    out=t[0:128, 0:8 * 512].rearrange("p (k t) -> p k t", k=8),
                    in_=ctx.dram["r3_bf"][0:1024, 512 * tc:512 * (tc + 1)].rearrange(
                        "(k p) t -> p k t", p=128))),
                    reads=[ctx.bufs["r3_bf"]], writes=[rinB[rb]], dma=True)
                if mg + 1 < 8 and tc % 2 == 0:
                    f1wload(mg + 1, range(tc // 2 * 2, tc // 2 * 2 + 2))
                for mi in range(4):
                    pa = pb[0 + 2 * (mi % 2)]
                    paB = pbB[0 + 2 * (mi % 2)]
                    pc = pb[1 + 2 * (mi % 2)]
                    pcB = pbB[1 + 2 * (mi % 2)]
                    for k in range(8):
                        pr.op("tensor", (lambda e, k=k, mi=mi, w=w1g[b], t=rin[rb], pa=pa:
                                         e.matmul(
                            pa[0:128, 0:512], w[0:128, (k * 4 + mi) * 128:(k * 4 + mi) * 128 + 128],
                            t[0:128, 512 * k:512 * (k + 1)],
                            start=(k == 0), stop=(k == 7))),
                            reads=[w1gB[b], rinB[rb]], writes=[paB])
                    for k in range(8):
                        pr.op("tensor", (lambda e, k=k, mi=mi, w=w2g[b], t=rin[rb], pc=pc:
                                         e.matmul(
                            pc[0:128, 0:512], w[0:128, (k * 4 + mi) * 128:(k * 4 + mi) * 128 + 128],
                            t[0:128, 512 * k:512 * (k + 1)],
                            start=(k == 0), stop=(k == 7))),
                            reads=[w2gB[b], rinB[rb]], writes=[pcB])
                    m = 4 * mg + mi
                    eb = mi % 2
                    pr.op("scalar", (lambda e, m=m, t=t1b[eb], pa=pa: e.activation(
                        t[0:128, 0:512], pa[0:128, 0:512], AF.Silu,
                        scale=s1c[0:128, m:m + 1])), reads=[paB, s1B], writes=[t1B[eb]])
                    pr.op("vector", (lambda e, m=m, t=t2b[eb], pc=pc: e.tensor_scalar(
                        t[0:128, 0:512], pc[0:128, 0:512], s2c[0:128, m:m + 1], None,
                        ALU.mult)), reads=[pcB, s2B], writes=[t2B[eb]])
                    pr.op("vector", (lambda e, t1=t1b[eb], t2=t2b[eb], y=yst[eb]: e.tensor_mul(
                        y[0:128, 0:512], t1[0:128, 0:512], t2[0:128, 0:512])),
                        reads=[t1B[eb], t2B[eb]], writes=[ystB[eb]])
                    pr.op("sync", (lambda e, m=m, tc=tc, y=yst[eb]: e.dma_start(
                        out=ctx.dram["h1T"][128 * m:128 * (m + 1), 512 * tc:512 * (tc + 1)],
                        in_=y[0:128, 0:512])), reads=[ystB[eb]], writes=[ctx.bufs["h1T"]],
                        dma=True)
    pr.fence()
    st_big_gemm(ctx, "h1T", "wbf_h1", "L1_ffn_h1_s", "h2T", 4096, 4096, "gelu")
    st_big_gemm(ctx, "h2T", "wbf_h2", "L1_ffn_h2_s", "h3T", 4096, 4096, "gelu")
    st_gemm_ln(ctx, src="h3T", wname="wbf_w3", K=4096, res=["r3_32"],
               out32="r4_32", outbf="r4_bf", out_scale_name="L1_ffn_w3_s",
               w_from_dram=True)
    pr.fence()
    with contextlib.ExitStack() as stk:
        rin = [_sb(ctx, stk, f"o4r{i}", [128, 1024], F32) for i in range(2)]
        rinB = [Buf(f"o4r{i}") for i in range(2)]
        ost = [_sb(ctx, stk, f"o4s{i}", [128, 1024], F32) for i in range(2)]
        ostB = [Buf(f"o4s{i}") for i in range(2)]
        for i in range(32):
            b = i % 2
            pr.op("sync", (lambda e, i=i, t=rin[b]: e.dma_start(
                out=t[0:128, 0:8 * 128].rearrange("p (k t) -> p k t", k=8),
                in_=ctx.dram["r4_32"][0:1024, 128 * i:128 * (i + 1)].rearrange(
                    "(k p) t -> p k t", p=128))),
                reads=[ctx.bufs["r4_32"]], writes=[rinB[b]], dma=True)
            for m in range(8):
                pz = pb[m % 2]
                pzB = pbB[m % 2]
                pr.op("tensor", (lambda e, m=m, t=rin[b], pz=pz: e.transpose(
                    pz[0:128, 0:128], t[0:128, 128 * m:128 * (m + 1)],
                    c["c_id_f"][0:128, 0:128])),
                    reads=[rinB[b], cB], writes=[pzB])
                pr.op("vector", (lambda e, m=m, t=ost[b], pz=pz: e.tensor_copy(
                    t[0:128, 128 * m:128 * (m + 1)], pz[0:128, 0:128])),
                    reads=[pzB], writes=[ostB[b]])
            pr.op("sync", (lambda e, i=i, t=ost[b]: e.dma_start(
                out=ctx.out[128 * i:128 * (i + 1), 0:1024], in_=t[0:128, 0:1024])),
                reads=[ostB[b]], writes=[ctx.bufs["out"]], dma=True)
    pr.fence()



def kernel(**inputs):
    x = np.ascontiguousarray(inputs["L0_attn_x"])
    cns = host_constants()
    nc, ctx = build_program()
    in_maps = []
    for core in range(NC_):
        m = {"x": np.ascontiguousarray(x[core])}
        for k, v in cns.items():
            m[k] = v
        for k, v in inputs.items():
            if k != "L0_attn_x":
                m[k] = np.ascontiguousarray(v)
        in_maps.append(m)
    res = run_bass_kernel_spmd(nc, in_maps, core_ids=list(range(NC_)))
    return np.stack([res.results[i]["out"] for i in range(NC_)], axis=0)
